# version 15
# baseline (speedup 1.0000x reference)
import sys
sys.path.insert(0, "/opt/trn_rl_repo")
import numpy as np
from dataclasses import dataclass
from contextlib import ExitStack

import concourse.bass as bass
import concourse.mybir as mybir
from concourse.bass_utils import run_bass_kernel_spmd


"""Tiny explicit-semaphore scheduling framework for raw Bass.

Engines: 'pe', 'act', 'dve', 'pool', 'sp' ('sp' = sync HWDGE queue,
'pool' = gpsimd: dma issue + memset etc).

DMA completions within one queue/semaphore are UNORDERED, so each logical
DMA stream gets its own channel (= own semaphore); consumers wait for the
channel's cumulative count snapshotted at consumer-record time. Emitters
must keep each channel either (a) internally chain-serialized via deps, or
(b) grouped with full-group waits and group boundaries ordered through
consumer dependencies.
"""

ENGINES = ("pe", "act", "dve", "pool", "sp")


@dataclass(frozen=True)
class Tok:
    eng: str
    idx: int


class FW:
    def __init__(self, nc):
        self.nc = nc
        self.ops = {e: [] for e in ENGINES}
        self.nops = {e: 0 for e in ENGINES}
        self.ndma = {}
        self.chq = {}
        self.waited = set()
        self._finish = ()

    def _snap_deps(self, deps):
        out = []
        for d in deps:
            if d.eng.startswith("dmach_"):
                ch = d.eng[6:]
                out.append(Tok(d.eng, self.ndma[ch]))
            else:
                out.append(d)
        return tuple(out)

    def op(self, eng, fn, deps=()):
        deps = self._snap_deps(deps)
        self.nops[eng] += 1
        self.ops[eng].append((fn, deps, None, self.nops[eng]))
        t = Tok(eng, self.nops[eng])
        for d in deps:
            self.waited.add(d)
        return t

    def dma(self, q, ch, fn, deps=()):
        deps = self._snap_deps(deps)
        self.ndma[ch] = self.ndma.get(ch, 0) + 1
        assert self.chq.setdefault(ch, q) == q, f"channel {ch} used on two queues"
        self.ops[q].append((fn, deps, ch, None))
        for d in deps:
            self.waited.add(d)
        return Tok("dmach_" + ch, self.ndma[ch])

    def last(self, eng):
        n = self.nops[eng]
        return Tok(eng, n) if n else None

    def set_finish(self, toks):
        self._finish = tuple(self._snap_deps(toks))
        for t in self._finish:
            self.waited.add(t)

    def finalize(self, block, ctx):
        nc = self.nc
        sems = {}
        for e in ENGINES:
            sems[e] = ctx.enter_context(nc.semaphore("s_" + e))
        for ch in self.ndma:
            sems["dmach_" + ch] = ctx.enter_context(nc.semaphore("sd_" + ch))

        inc_idx = {e: set() for e in ENGINES}
        for t in self.waited:
            if t.eng in ENGINES:
                inc_idx[t.eng].add(t.idx)
        inc_val = {}
        for e in ENGINES:
            c = 0
            vals = {}
            for i in range(1, self.nops[e] + 1):
                if i in inc_idx[e]:
                    c += 1
                vals[i] = c
            inc_val[e] = vals

        def tok_sem_val(t):
            if t.eng.startswith("dmach_"):
                return sems[t.eng], 16 * t.idx
            return sems[t.eng], inc_val[t.eng][t.idx]

        def emit_stream(e, engobj, tail=()):
            watermark = {}
            for (fn, deps, ch, cidx) in self.ops[e]:
                for d in deps:
                    sem, val = tok_sem_val(d)
                    if watermark.get(sem.name, -1) < val:
                        engobj.wait_ge(sem, val)
                        watermark[sem.name] = val
                ins = fn(engobj)
                if ch is not None:
                    ins.then_inc(sems["dmach_" + ch], 16)
                elif cidx in inc_idx[e]:
                    ins.then_inc(sems[e], 1)
            for t in tail:
                sem, val = tok_sem_val(t)
                if watermark.get(sem.name, -1) < val:
                    engobj.wait_ge(sem, val)
                    watermark[sem.name] = val

        @block.tensor
        def _(engobj):
            emit_stream("pe", engobj)

        @block.scalar
        def _(engobj):
            emit_stream("act", engobj)

        @block.vector
        def _(engobj):
            emit_stream("dve", engobj)

        @block.gpsimd
        def _(engobj):
            emit_stream("pool", engobj, tail=self._finish)

        @block.sync
        def _(engobj):
            emit_stream("sp", engobj)


"""Megakernel builder for mk_prb_d2 (2-layer parallel-residual transformer).

Layout conventions (per core):
 - residual stream x: feature-major fp32, SBUF tile xr [128, DC*S]
   (DC = D/128 chunks side by side; chunk c at cols [c*S, (c+1)*S)).
 - activations for matmuls: bf16, feature-major.
 - all matmul contractions padded to K=128 partitions (zero-pad constants).
 - per (head, qtile) attention in [d, q] output form.

Sequences: B_local sequences processed sequentially per core.
"""


F32 = mybir.dt.float32
BF16 = mybir.dt.bfloat16
I32 = mybir.dt.int32
AF = mybir.ActivationFunctionType
ALU = mybir.AluOpType
AX = mybir.AxisListType


class Cfg:
    def __init__(self, S=2048, D=1024, NH=16, DH=64, B_local=2,
                 FFN_H=3072, KV_RANK=256, Q_RANK=512, ROPE_DIM=32,
                 MAXREL=4, NT=512):
        self.S, self.D, self.NH, self.DH = S, D, NH, DH
        self.B_local = B_local
        self.FFN_H = FFN_H
        self.KV_RANK, self.Q_RANK, self.ROPE_DIM = KV_RANK, Q_RANK, ROPE_DIM
        self.MAXREL = MAXREL
        self.NT = NT
        self.DC = D // 128
        self.NTT = S // NT
        self.NQT = S // 128
        self.HC = FFN_H // 128
        self.NREL = 2 * MAXREL + 1
        self.W = 17 * 8
        self.RC = ROPE_DIM
        self.QG = NT // 128



def host_constants(cfg):
    c = {}
    W = cfg.W
    ms = np.zeros((128, 8 * W), np.float32)
    for j in range(1, 9):
        for p in range(128):
            for cpp in range(W):
                if cpp - p >= j:
                    ms[p, (j - 1) * W + cpp] = 1.0
    c["Mstep"] = ms
    WP = 131
    mp = np.zeros((128, WP), np.float32)
    for p in range(128):
        mp[p, : min(p + 4, WP)] = 1.0
    c["Mpre"] = mp
    dd = np.zeros((128, 7 * WP), np.float32)
    for oi, o in enumerate(range(-3, 4)):
        for p in range(128):
            j = p + o
            if 0 <= j < WP:
                dd[p, oi * WP + j] = 1.0
    c["Ddiag"] = dd
    Pc = np.zeros((8, 9), np.float32)
    for r in range(7):
        Pc[r, r + 1] = 1.0
        Pc[r, 0] = -1.0
    Pc[7, 8] = 1.0
    Pc[7, 0] = -1.0
    c["PcT"] = np.ascontiguousarray(Pc.T)
    ie = np.zeros((128, 64), np.float32)
    ie[:64] = np.eye(64)
    io = np.zeros((128, 64), np.float32)
    io[64:] = np.eye(64)
    c["Ipad_e"], c["Ipad_o"] = ie, io
    c["ones_row"] = np.ones((1, 128), np.float32)
    oe = np.zeros((128, 1), np.float32); oe[:64] = 1.0
    oo = np.zeros((128, 1), np.float32); oo[64:] = 1.0
    c["ones_e"], c["ones_o"] = oe, oo
    c["ones_col"] = np.ones((128, 1), np.float32)
    cm = np.zeros((128, 128), np.float32)
    for p in range(128):
        cm[p, p + 1:] = -3e38 / 4
    c["causal"] = cm
    rd = cfg.ROPE_DIM
    inv = 1.0 / 10000.0 ** (np.arange(0, rd, 2, dtype=np.float32) / rd)
    t = np.arange(cfg.S, dtype=np.float32)
    fr = np.outer(t, inv)
    emb = np.concatenate([fr, fr], -1)
    cos, sin = np.cos(emb), np.sin(emb)
    nrep = 128 // rd
    c["cos_rep"] = np.ascontiguousarray(np.tile(cos.T, (nrep, 1)))
    c["sin_rep"] = np.ascontiguousarray(np.tile(sin.T, (nrep, 1)))
    R = np.zeros((rd, rd), np.float32)
    h = rd // 2
    for i in range(h):
        R[i, i + h] = -1.0
        R[i + h, i] = 1.0
    c["R32T"] = np.ascontiguousarray(R.T)
    rp = np.zeros((128, rd), np.float32)
    rp[64:64 + rd] = R.T
    c["R32Tp"] = rp
    Rb = np.zeros((128, 128), np.float32)
    for b in range(nrep):
        Rb[b * rd:(b + 1) * rd, b * rd:(b + 1) * rd] = R
    c["R128T"] = np.ascontiguousarray(Rb.T)
    s3 = np.zeros((128, 32), np.float32)
    s3[96:128] = np.eye(32)
    c["SEL3"] = s3
    se = np.zeros((8, 64), np.float32)
    sr = np.zeros((8, 64), np.float32)
    for er in range(64):
        se[er // 8, er] = 1.0
        sr[er % 8, er] = 1.0
    c["SELe"], c["SELr"] = se, sr
    c["I128"] = np.eye(128, dtype=np.float32)
    c["ones_rowh"] = np.ones((1, 128), np.float32)
    c["ones_colh"] = np.ones((128, 1), np.float32)
    oq = np.zeros((128, 4), np.float32)
    for i in range(4):
        oq[32 * i:32 * (i + 1), i] = 1.0
    c["ones_q"] = oq
    sq_ = np.zeros((32, 4 * 128), np.float32)
    for qq in range(4):
        for i in range(32):
            sq_[i, qq * 128 + 32 * qq + i] = 1.0
    c["SELq"] = sq_
    return c


def tile_w(W, mcols, kchunks=None):
    K, M = W.shape
    kchunks = kchunks or K // 128
    mt = M // mcols
    Wr = W.reshape(kchunks, 128, mt, mcols)
    out = np.ascontiguousarray(Wr.transpose(2, 1, 0, 3).reshape(mt * 128,
                                                                kchunks * mcols))
    return out


def host_weights_l0(cfg, w):
    d = {}
    d["Wq"] = tile_w(w["L0_attn_Wq"], cfg.DH)
    d["Wk"] = tile_w(w["L0_attn_Wk"], 128)
    d["Wv"] = w["L0_attn_Wv"]
    d["Wo"] = tile_w(w["L0_attn_Wo"], 128)
    for n in ("q", "k", "v", "o"):
        b = w["L0_attn_b" + n]
        d["b" + n] = np.ascontiguousarray(b.reshape(cfg.DC, 128).T)
    tkT = np.ascontiguousarray(w["L0_attn_table_k"].T)
    nrel = tkT.shape[1]
    tkp = np.zeros((128, 2 * nrel), np.float32)
    dh = tkT.shape[0]
    tkp[:dh, :nrel] = tkT
    tkp[dh:2 * dh, nrel:] = tkT
    d["tkTpad"] = tkp
    d["tv"] = w["L0_attn_table_v"]
    d["tvT"] = np.ascontiguousarray(w["L0_attn_table_v"].T)
    d["wgT"] = tile_w(np.ascontiguousarray(w["L0_ffn_w_gate"].T), 128)
    d["wuT"] = tile_w(np.ascontiguousarray(w["L0_ffn_w_up"].T), 128)
    d["wdT"] = np.ascontiguousarray(w["L0_ffn_w_down"].T)
    return d


def host_weights_l1(cfg, w):
    d = {}
    d["Wkv"] = tile_w(w["L1_attn_W_kv"], 128)
    d["gkv"] = np.ascontiguousarray(
        w["L1_attn_g_kv"].reshape(cfg.KV_RANK // 128, 128).T)
    d["Wkdec"] = tile_w(w["L1_attn_W_kdec"], 128)
    d["Wvdec"] = w["L1_attn_W_vdec"]
    d["Wq"] = tile_w(w["L1_attn_W_q"], 128)
    d["Wkrope"] = tile_w(w["L1_attn_W_krope"], 128)
    qd = w["L1_attn_W_qdec"].reshape(cfg.Q_RANK, cfg.NH, cfg.DH)
    qr = w["L1_attn_W_qrope"].reshape(cfg.Q_RANK, cfg.NH, cfg.ROPE_DIM)
    d["Wqaug"] = tile_w(np.ascontiguousarray(
        np.concatenate([qd, qr], axis=2).reshape(
            cfg.Q_RANK, cfg.NH * (cfg.DH + cfg.ROPE_DIM))),
        cfg.DH + cfg.ROPE_DIM)
    d["Wo"] = tile_w(w["L1_attn_W_o"], 128)
    d["bwT"] = tile_w(np.ascontiguousarray(w["L1_ffn_base_w"].T), 128)
    d["bb"] = np.ascontiguousarray(w["L1_ffn_base_b"].reshape(cfg.DC, 128).T)
    d["lA"] = w["L1_ffn_lora_A"]
    lb = w["L1_ffn_lora_B"].reshape(64, cfg.D)
    d["lBpad"] = np.ascontiguousarray(
        np.concatenate([lb, np.zeros((64, cfg.D), np.float32)], axis=0))
    d["rwT"] = np.ascontiguousarray(w["L1_ffn_router_w"].T)
    d["rb"] = w["L1_ffn_router_b"].reshape(8, 1)
    return d


"""Device program builder for the mk_prb_d2 megakernel.

Conventions:
 - residual xr [128, DC*S] fp32 f-major (chunk c at cols [c*S,(c+1)*S)).
 - Kf f-major bf16 (2 heads/chunk); KR f-major bf16 (4 rope-heads/chunk).
 - Vf TOKEN-major bf16: [128, (S/128)*D]; token-chunk tc at cols [tc*D,(tc+1)*D).
 - xn is never stored full: recomputed from mrow/rrow where needed.
 - all accumulating matmuls K=128 (zero-padded operands).
 - attention: [d,q]-form attnV, bound-bias softmax, diagR-normalized pT.
"""


F32 = mybir.dt.float32
BF16 = mybir.dt.bfloat16
I32 = mybir.dt.int32
AF = mybir.ActivationFunctionType
ALU = mybir.AluOpType
AXX = mybir.AxisListType


class B:
    def __init__(self, cfg: Cfg, debug_dumps=()):
        self.cfg = cfg
        self.debug = set(debug_dumps)
        self.nc = bass.Bass("TRN2", target_bir_lowering=False)
        self.fw = FW(self.nc)
        self.dram = {}
        self.sb = {}
        self.ps = []
        self.pslast = [None] * 8
        self._dbg_toks = []

    def dram_in(self, name, shape, dtype=F32):
        self.dram[name] = self.nc.dram_tensor(name, list(shape), dtype,
                                              kind="ExternalInput").ap()

    def dram_out(self, name, shape, dtype=F32):
        self.dram[name] = self.nc.dram_tensor(name, list(shape), dtype,
                                              kind="ExternalOutput").ap()

    _name_ctr = 0

    def sbuf(self, ctx, name, shape, dtype):
        B._name_ctr += 1
        t = ctx.enter_context(self.nc.sbuf_tensor(f"{name}__{B._name_ctr}",
                                                  list(shape), dtype))
        self.sb[name] = t
        return t

    def op(self, eng, fn, deps=()):
        return self.fw.op(eng, fn, [d for d in deps if d is not None])

    def dma(self, q, ch, fn, deps=()):
        return self.fw.dma(q, ch, fn, [d for d in deps if d is not None])

    def mm(self, bank, out_slice, lhsT, rhs, start, stop, deps=()):
        d = [x for x in deps if x is not None]
        if start and self.pslast[bank] is not None:
            d.append(self.pslast[bank])
        return self.fw.op("pe", lambda e: e.matmul(out=out_slice, lhsT=lhsT,
                                                   rhs=rhs, start=start, stop=stop), d)

    def free_bank(self, bank, tok):
        self.pslast[bank] = tok

    def act(self, out, in_, func=AF.Copy, bias=None, scale=None, accum=None, deps=()):
        kw = {}
        if bias is not None:
            kw["bias"] = bias
        if scale is not None:
            kw["scale"] = scale
        if accum is not None:
            kw["accum_out"] = accum
        return self.op("act", lambda e: e.activation(out=out, in_=in_, func=func, **kw),
                       deps)

    def dbg(self, name, src_ap, deps, shape, dtype=F32):
        if name not in self.debug:
            return
        key = "dbg_" + name
        if key not in self.dram:
            self.dram_out(key, shape, dtype)
        t = self.dma("pool", "dbg", lambda e: e.dma_start(out=self.dram[key], in_=src_ap), deps)
        self._dbg_toks.append(t)

    def bar(self):
        d = []
        for e in ("pe", "act", "dve", "pool", "sp"):
            t = self.fw.last(e)
            if t:
                d.append(t)
        for ch, n in self.fw.ndma.items():
            if n:
                d.append(Tok("dmach_" + ch, n))
        return d

    def declare(self):
        cfg = self.cfg
        S, D, DC = cfg.S, cfg.D, cfg.DC
        self.dram_in("x", [cfg.B_local, D, S])
        self.dram_out("y", [cfg.B_local, D, S])
        for n, sh in [("Mstep", (128, 8 * cfg.W)), ("Mpre", (128, 131)),
                      ("Ddiag", (128, 7 * 131)),
                      ("ones_e", (128, 1)), ("ones_o", (128, 1)),
                      ("ones_q", (128, 4)), ("ones_colh", (128, 1)),
                      ("cos_rep", (128, S)), ("sin_rep", (128, S)),
                      ("R32T", (cfg.RC, cfg.RC)), ("R32Tp", (128, cfg.RC)), ("R128T", (128, 128)),
                      ("SELq", (32, 4 * 128)), ("SELe", (8, 64)), ("SELr", (8, 64)),
                      ("I128", (128, 128)), ("ones_rowh", (1, 128))]:
            self.dram_in(n, sh, BF16)
        for n, sh in [("PcT", (9, 8)), ("ones_row", (1, 128)),
                      ("ones_col", (128, 1)), ("causal", (128, 128))]:
            self.dram_in(n, sh)
        self.dram_in("tkTpad", (128, 2 * cfg.NREL))
        self.dram["dscr"] = self.nc.dram_tensor(
            "dscr", [128, DC * S], BF16, kind="Internal").ap()
        self.dram_in("Wq", [cfg.NH * 128, DC * cfg.DH])
        self.dram_in("Wk", [DC * 128, DC * 128])
        self.dram_in("Wv", [D, D]); self.dram_in("Wo", [DC * 128, DC * 128])
        for n in ("bq", "bk", "bv", "bo"):
            self.dram_in(n, [128, DC])
        self.dram_in("bv_row", [1, D])
        self.dram_in("tv", [cfg.NREL, cfg.DH]); self.dram_in("tvT", [cfg.DH, cfg.NREL])
        self.dram_in("wgT", [cfg.HC * 128, DC * 128], I32)
        self.dram_in("wuT", [cfg.HC * 128, DC * 128], I32)
        self.dram_in("wdT", [cfg.FFN_H, D], I32)
        KV, QR = cfg.KV_RANK, cfg.Q_RANK
        self.dram_in("Wkv", [(KV // 128) * 128, DC * 128])
        self.dram_in("gkv", [128, max(1, KV // 128)])
        self.dram_in("Wkdec", [DC * 128, (KV // 128) * 128])
        self.dram_in("Wvdec", [KV, D])
        self.dram_in("Wq1", [(QR // 128) * 128, DC * 128])
        self.dram_in("Wkrope", [(cfg.NH * cfg.RC // 128) * 128, DC * 128])
        self.dram_in("Wqaug", [cfg.NH * 128, (QR // 128) * (cfg.DH + cfg.RC)])
        self.dram_in("Wo1", [DC * 128, DC * 128])
        self.dram_in("bwT", [DC * 128, DC * 128]); self.dram_in("bb", [128, DC])
        self.dram_in("lA", [D, 8]); self.dram_in("lBpad", [128, D])
        self.dram_in("rwT", [D, 8]); self.dram_in("rb", [8, 1])

    def alloc_persistent(self, ctx):
        cfg = self.cfg
        D, DC = cfg.D, cfg.DC
        sb = lambda n, s, d: self.sbuf(ctx, n, s, d)
        sb("xr", [128, DC * cfg.S], F32)
        sb("cones_e", [128, 1], BF16); sb("cones_o", [128, 1], BF16)
        sb("cones_q", [128, 4], BF16); sb("cones_colh", [128, 1], BF16)
        sb("cR32T", [cfg.RC, cfg.RC], BF16); sb("cR32Tp", [128, cfg.RC], BF16)
        sb("cR128T", [128, 128], BF16)
        sb("cSELe", [8, 64], BF16); sb("cSELr", [8, 64], BF16)
        sb("cI128", [128, 128], BF16); sb("cones_rowh", [1, 128], BF16)
        sb("cones_row", [1, 128], F32); sb("cones_col", [128, 1], F32)
        sb("cPcT", [cfg.NREL, 8], F32)
        sb("ctv", [cfg.NREL, cfg.DH], F32); sb("ctvT", [cfg.DH, cfg.NREL], F32)
        sb("ctkT", [128, 2 * cfg.NREL], BF16)
        sb("biasT", [128, 8 * DC], F32)
        sb("cgkv", [128, max(1, cfg.KV_RANK // 128)], F32)
        sb("crb", [8, 1], F32)
        sb("lAh", [128, DC * 8], BF16); sb("rwh", [128, DC * 8], BF16)
        sb("QRt", [128, 16], F32)
        sb("Ucols", [128, 8], F32)
        sb("accD", [128, 24], F32)
        sb("D8_0", [128, 8], BF16)
        sb("D8_1", [128, 8], BF16)
        sb("D8Tp0", [128, 128], BF16)
        sb("D8Tp1", [128, 128], BF16)
        sb("Scol", [128, 1], F32); sb("recipS", [128, 1], F32)
        sb("diagR0", [128, 128], BF16)
        sb("diagR1", [128, 128], BF16)
        sb("bias1_0", [128, 1], F32); sb("bias2_0", [128, 1], F32)
        sb("bias1_1", [128, 1], F32); sb("bias2_1", [128, 1], F32)
        sb("MXcol", [128, 1], F32)
        sb("TVmodp", [128, cfg.DH], BF16)
        sb("hrow", [1, 16], F32)
        sb("epsrow", [1, 2], F32)
        sb("vrow", [1, cfg.NT], F32)
        sb("srow", [1, cfg.NT], F32)
        sb("evac", [128, 2 * cfg.NT], F32)
        sb("sq", [128, 2 * cfg.NT], F32)
        sb("hg", [128, 2 * cfg.NT], BF16)

    def alloc_psum(self, ctx):
        for i in range(8):
            self.ps.append(ctx.enter_context(
                self.nc.psum_tensor(f"pb{i}", [128, 512], F32)))

    def emit_init(self):
        cfg, sb, dr = self.cfg, self.sb, self.dram
        ltoks = []

        def ld(dst_ap, src):
            t = self.dma("sp", "ic", lambda e, d=dst_ap, s=dr[src]: e.dma_start(out=d, in_=s))
            ltoks.append(t)
            return t

        for dst, src in [("cones_e", "ones_e"), ("cones_o", "ones_o"),
                         ("cones_q", "ones_q"), ("cones_colh", "ones_colh"),
                         ("cR32T", "R32T"), ("cR32Tp", "R32Tp"), ("cR128T", "R128T"),
                         ("cSELe", "SELe"), ("cSELr", "SELr"), ("cI128", "I128"),
                         ("cones_rowh", "ones_rowh"), ("cones_row", "ones_row"),
                         ("cones_col", "ones_col"),
                         ("cPcT", "PcT"), ("ctv", "tv"), ("ctvT", "tvT"),
                         ("cgkv", "gkv"), ("crb", "rb")]:
            ld(sb[dst][:, :], src)
        DC = cfg.DC
        for i, n in enumerate(("bq", "bk", "bv", "bo", "bb")):
            ld(sb["biasT"][:, i * DC:(i + 1) * DC], n)
        t1 = self.dma("sp", "ict", lambda e: e.dma_start(
            out=sb["evac"][:, 0:2 * cfg.NREL], in_=dr["tkTpad"]))
        ltoks.append(t1)
        c1 = self.act(sb["ctkT"][:, :], sb["evac"][:, 0:2 * cfg.NREL], deps=[t1])
        lr = []
        srcA = dr["lA"].rearrange("(kc r) c -> r kc c", r=128)
        dstA = sb["sq"][:, 0:DC * 8].rearrange("p (kc c) -> p kc c", c=8)
        lr.append(self.dma("sp", "icl", lambda e: e.dma_start(out=dstA, in_=srcA)))
        srcR = dr["rwT"].rearrange("(kc r) c -> r kc c", r=128)
        dstR = sb["sq"][:, 512:512 + DC * 8].rearrange("p (kc c) -> p kc c", c=8)
        lr.append(self.dma("sp", "icl", lambda e: e.dma_start(out=dstR, in_=srcR)))
        c3 = self.act(sb["lAh"][:, :], sb["sq"][:, 0:DC * 8], deps=lr)
        c4 = self.act(sb["rwh"][:, :], sb["sq"][:, 512:512 + DC * 8], deps=lr)
        c2 = c5 = c4
        m1a = self.op("dve", lambda e: e.memset(sb["D8Tp0"][:, :], 0.0))
        m1 = self.op("dve", lambda e: e.memset(sb["D8Tp1"][:, :], 0.0), [m1a])
        m4 = self.op("dve", lambda e: e.memset(sb["TVmodp"][:, :], 0.0))
        m5 = self.op("dve", lambda e: e.memset(sb["epsrow"][0:1, 0:1], 1e-5))
        m5b = self.op("dve", lambda e: e.memset(sb["epsrow"][0:1, 1:2], 1e-6), [m5])
        m6 = m1
        m7 = m1

        tm = self.mm(7, self.ps[7][0:8, 0:cfg.DH], sb["cPcT"][:, :], sb["ctv"][:, :],
                     True, True, deps=ltoks)
        evt = self.act(sb["TVmodp"][0:8, :], self.ps[7][0:8, 0:cfg.DH], deps=[tm, m4])
        self.free_bank(7, evt)
        sq1 = self.act(sb["sq"][:, 0:2 * cfg.NREL], sb["ctkT"][:, :], func=AF.Square,
                       deps=[c1, c3, c4, c5])
        mm1 = self.mm(7, self.ps[7][0:1, 0:cfg.NREL], sb["cones_col"][:, :],
                      sb["sq"][:, 0:cfg.NREL], True, True, deps=[sq1])
        r1 = self.op("dve", lambda e: e.tensor_reduce(
            out=sb["hrow"][0:1, 2:3], in_=self.ps[7][0:1, 0:cfg.NREL],
            axis=AXX.X, op=ALU.max), [mm1])
        self.free_bank(7, r1)
        r2 = self.act(sb["hrow"][0:1, 2:3], sb["hrow"][0:1, 2:3], func=AF.Sqrt,
                      deps=[r1])
        self.init_toks = ltoks + [c1, c3, c4, m1, m4, m5b, m6, m7, evt, r2]

    def load_x(self, seq, deps):
        cfg, sb = self.cfg, self.sb
        src = self.dram["x"][seq].rearrange("(c r) t -> r c t", r=128)
        dst = sb["xr"][:, :].rearrange("p (c t) -> p c t", t=cfg.S)
        return [self.dma("sp", "xl", lambda e: e.dma_start(out=dst, in_=src), deps)]

    def store_y(self, seq, deps):
        cfg, sb = self.cfg, self.sb
        dst = self.dram["y"][seq].rearrange("(c r) t -> r c t", r=128)
        src = sb["xr"][:, :].rearrange("p (c t) -> p c t", t=cfg.S)
        return [self.dma("sp", "xl", lambda e: e.dma_start(out=dst, in_=src), deps)]

    def ln_stats(self, mrow, rrow, eps_idx, deps):
        out = []
        for t in range(self.cfg.NTT):
            out += self.ln_stats_one(mrow, rrow, eps_idx, t, t * self.cfg.NT, deps)
        return out

    def ln_stats_one(self, mrow, rrow, eps_idx, t, dcol, deps):
        cfg, sb, ps = self.cfg, self.sb, self.ps
        S, DC, NT, D = cfg.S, cfg.DC, cfg.NT, cfg.D
        out = []
        if True:
            col = t * NT
            mm_m = mm_v = None
            sq_free = {}
            for c in range(DC):
                xs = sb["xr"][:, c * S + col: c * S + col + NT]
                mm_m = self.mm(0, ps[0][0:1, 0:NT], sb["cones_col"][:, :], xs,
                               c == 0, c == DC - 1, deps=deps)
                par = c % 2
                sqs = sb["sq"][:, par * NT:(par + 1) * NT]
                sqt = self.act(sqs, xs, func=AF.Square,
                               deps=deps + [sq_free.get(par)])
                mm_v = self.mm(1, ps[1][0:1, 0:NT], sb["cones_col"][:, :], sqs,
                               c == 0, c == DC - 1, deps=[sqt])
                sq_free[par] = mm_v
            mt = self.act(mrow[0:1, dcol:dcol + NT], ps[0][0:1, 0:NT],
                          scale=1.0 / D, deps=[mm_m])
            m2t = self.act(sb["srow"][0:1, 0:NT], ps[0][0:1, 0:NT], func=AF.Square,
                           scale=1.0 / D, deps=[mm_m])
            self.free_bank(0, m2t)
            vt = self.op("dve", lambda e: e.scalar_tensor_tensor(
                out=sb["vrow"][0:1, 0:NT], in0=ps[1][0:1, 0:NT], scalar=1.0 / D,
                in1=sb["srow"][0:1, 0:NT], op0=ALU.mult, op1=ALU.subtract),
                [mm_v, m2t])
            self.free_bank(1, vt)
            st = self.act(sb["srow"][0:1, 0:NT], sb["vrow"][0:1, 0:NT], func=AF.Sqrt,
                          bias=sb["epsrow"][0:1, eps_idx:eps_idx + 1], deps=[vt])
            rt0 = self.op("dve", lambda e: e.reciprocal(
                out=sb["vrow"][0:1, 0:NT], in_=sb["srow"][0:1, 0:NT]), [st])
            rt = self.op("dve", lambda e, dcol=dcol: e.tensor_copy(
                out=rrow[0:1, dcol:dcol + NT], in_=sb["vrow"][0:1, 0:NT]), [rt0])
            out += [mt, rt]
        return out

    def xn_tile(self, xn_of_c, mrow, rrow, t, deps, bm=2, br=3, stats_col=None):
        cfg, sb, ps = self.cfg, self.sb, self.ps
        S, DC, NT = cfg.S, cfg.DC, cfg.NT
        col = t * NT
        scol = col if stats_col is None else stats_col
        ones = sb["cones_row"] if str(mrow.dtype).endswith("float32") else sb["cones_rowh"]
        bmt = self.mm(bm, ps[bm][:, 0:NT], ones[:, :],
                      mrow[0:1, scol:scol + NT], True, True, deps=deps)
        brt = self.mm(br, ps[br][:, 0:NT], ones[:, :],
                      rrow[0:1, scol:scol + NT], True, True, deps=deps)
        toks = []
        for c in range(DC):
            par = c % 2
            ev = sb["evac"][:, par * NT:(par + 1) * NT]
            xs = sb["xr"][:, c * S + col: c * S + col + NT]
            o1 = self.op("dve", lambda e, ev=ev, xs=xs: e.tensor_sub(
                out=ev, in0=xs, in1=ps[bm][:, 0:NT]),
                [bmt, self.tok_evac_free[par]] + deps)
            o2 = self.op("dve", lambda e, ev=ev, c=c: e.tensor_mul(
                out=xn_of_c(c), in0=ev, in1=ps[br][:, 0:NT]), [o1, brt])
            self.tok_evac_free[par] = o2
            toks.append(o2)
        self.free_bank(bm, toks[-1])
        self.free_bank(br, toks[-1])
        return toks

    def stage_cols(self, wname, mcol0, mcols, kchunks, free_dep, sq_off=0, ch=None):
        sb = self.sb
        ch = ch or ("w_" + wname)
        fd = list(free_dep) if isinstance(free_dep, list) else [free_dep]
        fd += [self.tok_hg_free, self.tok_hg_free2, self.tok_hgc_free]
        m = mcol0 // mcols
        src = self.dram[wname][m * 128:(m + 1) * 128, 0:kchunks * mcols]
        dst = sb["sq"][:, sq_off:sq_off + kchunks * mcols]
        dt = self.dma("sp", ch, lambda e: e.dma_start(out=dst, in_=src), fd)
        cst = self.act(sb["hg"][:, sq_off:sq_off + kchunks * mcols],
                       sb["sq"][:, sq_off:sq_off + kchunks * mcols],
                       deps=[dt] + fd)
        return cst

    def v_tm_tile(self, wrows_name, xng, t, kchunks, n_banks_cols, bias_row,
                  wstage_f, wstage_h, wfree, src_toks, dst_col0):
        cfg, sb = self.cfg, self.sb
        D, NT = cfg.D, cfg.NT
        ntc = NT // 128
        nn = n_banks_cols
        mms = {}
        cstp = wfree[0]
        last_mm = wfree[1]
        for kc in range(kchunks):
            dt = self.dma("sp", "w_" + wrows_name, lambda e, kc=kc: e.dma_start(
                out=wstage_f[:, 0:D],
                in_=self.dram[wrows_name][kc * 128:(kc + 1) * 128, :]), [cstp])
            cst = self.act(wstage_h[:, 0:D], wstage_f[:, 0:D],
                           deps=[dt, last_mm])
            cstp = cst
            for tc in range(ntc):
                for n2 in range(nn):
                    bank = tc * nn + n2
                    mms[(tc, n2)] = self.mm(
                        bank, self.ps[bank][:, 0:512],
                        xng[:, kc * NT + tc * 128: kc * NT + (tc + 1) * 128],
                        wstage_h[:, n2 * 512:(n2 + 1) * 512],
                        kc == 0, False, deps=[cst] + src_toks)
                    last_mm = mms[(tc, n2)]
        wfree[0] = cstp
        evs = []
        for tc in range(ntc):
            for n2 in range(nn):
                bank = tc * nn + n2
                bmm = self.mm(bank, self.ps[bank][:, 0:512],
                              sb["cones_rowh"][0:1, 0:128],
                              bias_row[0:1, n2 * 512:(n2 + 1) * 512],
                              False, True, deps=[mms[(tc, n2)]]) if bias_row is not None \
                    else self.mm(bank, self.ps[bank][:, 0:512],
                                 sb["cones_rowh"][0:1, 0:1].broadcast_to((1, 1)),
                                 None, False, True)
                ev = self.act(sb["Vf"][:, dst_col0 + (t * ntc + tc) * D + n2 * 512:
                                       dst_col0 + (t * ntc + tc) * D + (n2 + 1) * 512],
                              self.ps[bank][:, 0:512], deps=[bmm])
                self.free_bank(bank, ev)
                evs.append(ev)
        wfree[1] = evs[-1]
        return evs

    def l0_phase_a(self, mrow, rrow, xng, deps):
        cfg, sb = self.cfg, self.sb
        S, D, DC, NT = cfg.S, cfg.D, cfg.DC, cfg.NT
        ktoks, vtoks = [], []
        kfree = [None, None]
        prev_stats = list(deps)
        for t in range(cfg.NTT):
            stat_toks = self.ln_stats_one(mrow, rrow, 0, t, 0, deps + prev_stats)
            if kfree[0] is None:
                kfree = [stat_toks[-1], stat_toks[-1]]
            xnt = self.xn_tile(lambda c: xng[:, c * NT:(c + 1) * NT],
                               mrow, rrow, t, stat_toks, stats_col=0)
            prev_stats = stat_toks + xnt
            vtoks += self.v_tm_tile("Wv", xng, t, DC, D // 512, sb["bvh"],
                                    sb["sq"], sb["hg"], kfree, xnt, 0)
            for m in range(DC):
                cst = self.stage_cols("Wk", m * 128, 128, DC, [kfree[0], kfree[1]])
                kfree[0] = cst
                bk = m % 2
                mmk = None
                for kc in range(DC):
                    mmk = self.mm(bk, self.ps[bk][:, 0:NT],
                                  sb["hg"][:, kc * 128:(kc + 1) * 128],
                                  xng[:, kc * NT:(kc + 1) * NT],
                                  kc == 0, kc == DC - 1, deps=[cst] + xnt)
                ek = self.act(sb["Kf"][:, m * S + t * NT: m * S + (t + 1) * NT],
                              self.ps[bk][:, 0:NT], func=AF.Identity,
                              bias=sb["biasT"][:, DC + m:DC + m + 1], deps=[mmk])
                self.free_bank(bk, ek)
                kfree[0] = cst
                kfree[1] = ek
                ktoks.append(ek)
        return prev_stats, ktoks + vtoks

    def l0_attn_head(self, h, g, xng, xn_toks, kv_toks, wq_free):
        cfg, sb, ps = self.cfg, self.sb, self.ps
        Kf = sb["Kf"]
        wqs = sb["wqs"]
        wqh = sb["wqh"]
        qrel_s = sb["qrel_s"]
        Qg = sb["Qg"]
        S, D, DC, NT, DH = cfg.S, cfg.D, cfg.DC, cfg.NT, cfg.DH
        NREL, W = cfg.NREL, cfg.W
        par, ch = h % 2, h // 2
        fw = self.fw
        dts = [self.dma("sp", "w_Wq", lambda e: e.dma_start(
            out=wqs[:, 0:DC * DH],
            in_=self.dram["Wq"][h * 128:(h + 1) * 128, :]),
            [wq_free[0]])]
        cst = self.act(wqh[:, 0:DC * DH], wqs[:, 0:DC * DH],
                       deps=dts + [wq_free[1]])
        wq_free[0] = cst
        ms = self.op("dve", lambda e: e.memset(
            Qg[(1 - par) * 64:(1 - par) * 64 + 64, :], 0.0),
            [self.tok_qg_free])
        mmq = None
        for kc in range(DC):
            mmq = self.mm(4, ps[4][par * 64:par * 64 + 64, 0:NT],
                          wqh[:, kc * DH:(kc + 1) * DH],
                          xng[:, kc * NT:(kc + 1) * NT],
                          kc == 0, kc == DC - 1, deps=[cst] + xn_toks)
        eq = self.act(Qg[par * 64:par * 64 + 64, :],
                      ps[4][par * 64:par * 64 + 64, 0:NT], func=AF.Identity,
                      bias=sb["biasT"][par * 64:par * 64 + 64, ch:ch + 1],
                      deps=[mmq, ms, self.tok_qg_free])
        self.free_bank(4, eq)
        wq_free[1] = mmq
        mmr = self.mm(4, ps[4][0:NREL, 0:NT],
                      sb["ctkT"][:, par * NREL:(par + 1) * NREL],
                      Qg[:, 0:NT], True, True, deps=[eq])
        erl = self.act(qrel_s[0:NREL, 0:NT], ps[4][0:NREL, 0:NT], deps=[mmr])
        self.free_bank(4, erl)
        sqq = self.act(sb["hg"][:, 0:NT], Qg[:, :], func=AF.Square,
                       deps=[eq, self.tok_hg_free])
        mmn = self.mm(4, ps[4][0:1, 0:NT], sb["cones_colh"][:, 0:1],
                      sb["hg"][:, 0:NT], True, True, deps=[sqq])
        self.tok_hg_free = mmn
        sqt = self.act(sb["srow"][0:1, 0:NT], ps[4][0:1, 0:NT], func=AF.Sqrt,
                       deps=[mmn])
        self.free_bank(4, sqt)
        qm = self.op("dve", lambda e: e.tensor_reduce(
            out=sb["hrow"][0:1, 0:1], in_=sb["srow"][0:1, 0:NT],
            axis=AXX.X, op=ALU.max), [sqt, self.tok_mx_free])
        if g == 0:
            kms = []
            for tq in range(cfg.NTT):
                sqk = self.act(sb["hg"][:, 0:NT],
                               Kf[:, ch * S + tq * NT: ch * S + (tq + 1) * NT],
                               func=AF.Square, deps=kv_toks + [self.tok_hg_free])
                mmk = self.mm(4, ps[4][0:1, 0:NT],
                              sb["cones_e"][:, 0:1] if par == 0 else sb["cones_o"][:, 0:1],
                              sb["hg"][:, 0:NT], True, True, deps=[sqk])
                self.tok_hg_free = mmk
                km = self.op("dve", lambda e, tq=tq: e.tensor_reduce(
                    out=sb["hrow"][0:1, 8 + tq:9 + tq], in_=ps[4][0:1, 0:NT],
                    axis=AXX.X, op=ALU.max), [mmk])
                self.free_bank(4, km)
                kms.append(km)
            c1 = self.op("dve", lambda e: e.tensor_reduce(
                out=sb["hrow"][0:1, 1:2], in_=sb["hrow"][0:1, 8:8 + cfg.NTT],
                axis=AXX.X, op=ALU.max), kms)
            c2 = self.act(sb["hrow"][0:1, 1:2], sb["hrow"][0:1, 1:2], func=AF.Sqrt,
                          deps=[c1])
            self.tok_kmax = c2
        mx1 = self.op("dve", lambda e: e.tensor_add(
            out=sb["hrow"][0:1, 3:4], in0=sb["hrow"][0:1, 1:2],
            in1=sb["hrow"][0:1, 2:3]), [self.tok_kmax, qm])
        mx2 = self.op("dve", lambda e: e.tensor_mul(
            out=sb["hrow"][0:1, 3:4], in0=sb["hrow"][0:1, 3:4],
            in1=sb["hrow"][0:1, 0:1]), [mx1])
        mxb = self.mm(4, ps[4][:, 0:1], sb["cones_row"][:, :],
                      sb["hrow"][0:1, 3:4], True, True, deps=[mx2])
        mxe = self.act(sb["MXcol"][:, 0:1], ps[4][:, 0:1], deps=[mxb])
        self.free_bank(4, mxe)
        self.tok_mx_free = mxe
        return eq, erl, mxe

    def l0_attn_front_a(self, h, g, qt, eq, erl, mxe, kv_toks, pp):
        cfg, sb, ps = self.cfg, self.sb, self.ps
        probs = sb["probs" + pp]
        cMstep = sb["cMstep"]
        cMpre = sb["cMpre"]
        cDdiag = sb["cDdiag"]
        Kf = sb["Kf"]
        qrel_s = sb["qrel_s"]
        bandp = sb["bandp"]
        Qg = sb["Qg"]
        S, D, DC, NT, DH = cfg.S, cfg.D, cfg.DC, cfg.NT, cfg.DH
        NREL, W = cfg.NREL, cfg.W
        par, ch = h % 2, h // 2
        qti = g * cfg.QG + qt
        q0 = qti * 128
        qs = Qg[:, qt * 128:(qt + 1) * 128]
        nk = S // 512
        smm = []
        for b in range(nk):
            t = self.mm(b, ps[b][:, 0:512], qs,
                        Kf[:, ch * S + b * 512: ch * S + (b + 1) * 512],
                        True, True, deps=[eq] + kv_toks)
            smm.append(t)
        tq = self.mm(4, ps[4][:, 0:NREL], qrel_s[0:NREL, qt * 128:(qt + 1) * 128],
                     sb["cI128"][0:NREL, 0:NREL], True, True, deps=[erl])
        eqr = self.act(sb["QRt"][:, 0:NREL], ps[4][:, 0:NREL], deps=[tq])
        self.free_bank(4, eqr)
        uc = self.op("dve", lambda e: e.tensor_scalar(
            out=sb["Ucols"][:, 0:8], in0=sb["QRt"][:, 1:NREL],
            scalar1=sb["QRt"][:, 0:1], scalar2=None, op0=ALU.subtract),
            [eqr, self.tok_uc_free])
        b1 = self.op("dve", lambda e: e.tensor_scalar(
            out=sb["bias1_" + pp][:, 0:1], in0=sb["QRt"][:, 0:1],
            scalar1=sb["MXcol"][:, 0:1], scalar2=1.0 / 8.0,
            op0=ALU.subtract, op1=ALU.mult),
            [eqr, mxe, self.tok_b1_free[int(pp)]])
        b2 = self.op("dve", lambda e: e.tensor_scalar(
            out=sb["bias2_" + pp][:, 0:1], in0=sb["QRt"][:, NREL - 1:NREL],
            scalar1=sb["MXcol"][:, 0:1], scalar2=1.0 / 8.0,
            op0=ALU.subtract, op1=ALU.mult),
            [eqr, mxe, self.tok_b2_free[int(pp)]])
        btoks = []
        for half in range(2):
            prev = self.tok_bandp_free[half]
            for jj in range(4):
                j = half * 4 + jj + 1
                dst = bandp[:, half * W:(half + 1) * W]
                if jj == 0:
                    tt = self.op("dve", lambda e, j=j, dst=dst: e.tensor_scalar(
                        out=dst, in0=cMstep[:, (j - 1) * W:j * W],
                        scalar1=sb["Ucols"][:, j - 1:j], scalar2=None,
                        op0=ALU.mult), [uc, prev])
                else:
                    tt = self.op("dve", lambda e, j=j, dst=dst:
                                 e.scalar_tensor_tensor(
                                     out=dst, in0=cMstep[:, (j - 1) * W:j * W],
                                     scalar=sb["Ucols"][:, j - 1:j], in1=dst,
                                     op0=ALU.mult, op1=ALU.add), [uc, tt])
                prev = tt
            btoks.append(tt)
        wk0 = max(q0 - 4, 0)
        wk1 = min(q0 + 132, S)
        badds = []
        kk = wk0
        while kk < wk1:
            b = kk // 512
            ke = min(wk1, (b + 1) * 512)
            off = kk - (q0 - 4)
            t = self.op("dve", lambda e, b=b, kk=kk, ke=ke, off=off:
                        e.tensor_tensor(
                            out=ps[b][:, kk - b * 512:ke - b * 512],
                            in0=ps[b][:, kk - b * 512:ke - b * 512],
                            in1=bandp[:, off:off + ke - kk],
                            op=ALU.add), btoks + [smm[b]])
            t2 = self.op("dve", lambda e, b=b, kk=kk, ke=ke, off=off:
                         e.tensor_tensor(
                             out=ps[b][:, kk - b * 512:ke - b * 512],
                             in0=ps[b][:, kk - b * 512:ke - b * 512],
                             in1=bandp[:, W + off:W + off + ke - kk],
                             op=ALU.add), [t])
            badds.append((b, t2))
            kk = ke
        self.tok_bandp_free = [badds[-1][1], badds[-1][1]]
        badd_of_bank = dict(badds)
        self.tok_uc_free = btoks[-1]
        return dict(h=h, g=g, qt=qt, pp=pp, smm=smm, badd=badd_of_bank,
                    b1=b1, b2=b2, q0=q0, wk1=wk1, nk=nk)

    def l0_attn_fb_act(self, st):
        cfg, sb, ps = self.cfg, self.sb, self.ps
        pp = st["pp"]
        probs = sb["probs" + pp]
        S = cfg.S
        q0, wk1, nk = st["q0"], st["wk1"], st["nk"]
        smm, badd_of_bank, b1, b2 = st["smm"], st["badd"], st["b1"], st["b2"]
        ranges = []
        if q0 > 0:
            ranges.append((0, q0, "bias1_" + pp, 8))
        if wk1 < S:
            ranges.append((wk1, S, "bias2_" + pp, 10))
        ranges.append((q0, wk1, "bias1_" + pp, 9))
        prb_toks = []
        bank_exp = {}
        acc_cols = []
        acc_base = 9 + 7 * int(pp)
        acc_n = acc_base
        for (ka, kb, bn, acc_i) in ranges:
            kk = ka
            while kk < kb:
                b = kk // 512
                ke = min(kb, (b + 1) * 512)
                dep_sc = [badd_of_bank.get(b, smm[b]), b1, b2] + \
                    list(self.tok_probs_free)
                at = self.act(probs[:, kk:ke],
                              ps[b][:, kk - b * 512:ke - b * 512],
                              func=AF.Exp, bias=sb[bn][:, 0:1], scale=1.0 / 8.0,
                              accum=sb["accD"][:, acc_n:acc_n + 1],
                              deps=dep_sc)
                bank_exp[b] = at
                acc_cols.append((acc_i, acc_n))
                acc_n += 1
                prb_toks.append(at)
                kk = ke
        for b in range(nk):
            self.free_bank(b, bank_exp[b])
        self.tok_b1_free[int(pp)] = prb_toks[-1]
        self.tok_b2_free[int(pp)] = prb_toks[-1]
        st.update(prb_toks=prb_toks, acc_cols=acc_cols, acc_n=acc_n,
                  acc_base=acc_base)
        return st

    def l0_attn_fb_dve(self, st):
        cfg, sb, ps = self.cfg, self.sb, self.ps
        pp = st["pp"]
        probs = sb["probs" + pp]
        cMpre = sb["cMpre"]
        cDdiag = sb["cDdiag"]
        S, NT, NREL = cfg.S, cfg.NT, cfg.NREL
        q0, wk1, nk = st["q0"], st["wk1"], st["nk"]
        prb_toks = st["prb_toks"]
        acc_cols, acc_n = st["acc_cols"], st["acc_n"]
        acc_base = st["acc_base"]
        etoks = []
        pre_cols = [c for (ai, c) in acc_cols if ai == 8]
        if q0 > 0:
            npre = len(pre_cols)
            assert pre_cols == list(range(acc_base, acc_base + npre))
            if npre == 1:
                t = self.op("dve", lambda e, ab=acc_base: e.tensor_copy(
                    out=sb["accD"][:, 8:9], in_=sb["accD"][:, ab:ab + 1]),
                    prb_toks)
            else:
                t = self.op("dve", lambda e, npre=npre, ab=acc_base:
                            e.tensor_reduce(
                                out=sb["accD"][:, 8:9],
                                in_=sb["accD"][:, ab:ab + npre],
                                axis=AXX.X, op=ALU.add), prb_toks)
            etoks.append(t)
        sS = self.op("dve", lambda e, an=acc_n, ab=acc_base: e.tensor_reduce(
            out=sb["Scol"][:, 0:1], in_=sb["accD"][:, ab:an],
            axis=AXX.X, op=ALU.add), prb_toks)
        etoks.append(sS)
        s1 = sS
        rs = self.op("dve", lambda e: e.reciprocal(
            out=sb["recipS"][:, 0:1], in_=sb["Scol"][:, 0:1]), [s1])
        wp1 = min(q0 + 131, S)
        wlen = wp1 - q0
        dts = []
        for oi in range(7):
            dt = self.op("dve", lambda e, oi=oi: e.scalar_tensor_tensor(
                out=sb["hg"][:, oi * 131:oi * 131 + wlen], in0=probs[:, q0:wp1],
                scalar=1.0, in1=cDdiag[:, oi * 131:oi * 131 + wlen],
                op0=ALU.mult, op1=ALU.mult,
                accum_out=sb["accD"][:, oi:oi + 1]),
                prb_toks + [self.tok_hg_free2])
            dts.append(dt)
        mp = self.op("dve", lambda e: e.scalar_tensor_tensor(
            out=sb["hg"][:, 0:wlen], in0=probs[:, q0:wp1],
            scalar=1.0, in1=cMpre[:, 0:wlen], op0=ALU.mult, op1=ALU.mult,
            accum_out=sb["accD"][:, 23:24]), prb_toks + [dts[0]])
        self.tok_hg_free = mp
        self.tok_hg_free2 = dts[-1]
        if q0 > 0:
            t8a = self.op("dve", lambda e: e.tensor_sub(
                out=sb["accD"][:, 7:8], in0=sb["Scol"][:, 0:1],
                in1=sb["accD"][:, 8:9]), [s1, mp])
            t8 = self.op("dve", lambda e: e.tensor_sub(
                out=sb["accD"][:, 7:8], in0=sb["accD"][:, 7:8],
                in1=sb["accD"][:, 23:24]), [t8a])
        else:
            t8 = self.op("dve", lambda e: e.tensor_sub(
                out=sb["accD"][:, 7:8], in0=sb["Scol"][:, 0:1],
                in1=sb["accD"][:, 23:24]), [s1, mp])
        d8 = self.act(sb["D8_" + pp][:, 0:8], sb["accD"][:, 0:8],
                      func=AF.Identity, scale=sb["recipS"][:, 0:1],
                      deps=[t8, rs, self.tok_d8_free[int(pp)]] + dts)
        dR = self.op("dve", lambda e: e.tensor_scalar(
            out=sb["diagR" + pp][:, :], in0=sb["cI128"][:, :],
            scalar1=sb["recipS"][:, 0:1], scalar2=None, op0=ALU.mult),
            [rs, self.tok_diagR_free[int(pp)]])
        st.update(prb_toks=prb_toks, dR=dR, d8=d8, mp=mp, dts=dts)
        return st

    def l0_attn_back_pe(self, st, kv_toks):
        cfg, sb, ps = self.cfg, self.sb, self.ps
        h, g, qt, pp = st["h"], st["g"], st["qt"], st["pp"]
        probs = sb["probs" + pp]
        pTs = sb["pTs"]
        AO = sb["AO"]
        Vf = sb["Vf"]
        S, D, DC, NT, DH = cfg.S, cfg.D, cfg.DC, cfg.NT, cfg.DH
        par, ch = h % 2, h // 2
        prb_toks, dR = st["prb_toks"], st["dR"]
        td = self.mm(4, ps[4][0:8, 0:128], sb["D8_" + pp][:, 0:8],
                     sb["cI128"][:, :], True, True, deps=[st["d8"]])
        ed = self.act(sb["D8Tp" + pp][0:8, :], ps[4][0:8, 0:128],
                      deps=[td, self.tok_d8t_free[int(pp)]])
        self.free_bank(4, ed)
        self.tok_d8_free[int(pp)] = td
        nkc = S // 128
        ev_pt = []
        for bb in range(nkc // 4):
            bank = 5 + (bb % 2)
            mmp = []
            for j in range(4):
                kc = bb * 4 + j
                mmp.append(self.mm(bank, ps[bank][:, j * 128:(j + 1) * 128],
                                   probs[:, kc * 128:(kc + 1) * 128],
                                   sb["diagR" + pp][:, :], True, True,
                                   deps=prb_toks + [dR]))
            if bb % 2 == 0:
                epv = self.op("dve", lambda e, bb=bb, bank=bank: e.tensor_copy(
                    out=pTs[:, bb * 512:(bb + 1) * 512], in_=ps[bank][:, 0:512]),
                    mmp + [self.tok_pts_free])
            else:
                epv = self.act(pTs[:, bb * 512:(bb + 1) * 512],
                               ps[bank][:, 0:512],
                               deps=mmp + [self.tok_pts_free])
            self.free_bank(bank, epv)
            ev_pt.append(epv)
        self.tok_diagR_free[int(pp)] = ev_pt[-1]
        self._av_par = getattr(self, "_av_par", 0) ^ 1
        avc = self._av_par * 128
        av = None
        avdep = self.tok_av_free[self._av_par] or self.pslast[7]
        self.pslast[7] = None
        for kc in range(nkc):
            tc = kc
            av = self.mm(7, ps[7][0:64, avc:avc + 128],
                         Vf[:, tc * D + h * DH: tc * D + (h + 1) * DH],
                         pTs[:, kc * 128:(kc + 1) * 128],
                         kc == 0, kc == nkc - 1,
                         deps=[ev_pt[kc // 4], avdep if kc == 0 else None] + kv_toks)
        w2 = self.mm(7, ps[7][0:64, 256 + avc:256 + avc + 128],
                     sb["TVmodp"][:, 0:DH], sb["D8Tp" + pp][:, :], True, True,
                     deps=[ed, self.tok_w2_free[self._av_par]])
        self.tok_d8t_free[int(pp)] = w2
        self.tok_pts_free = av
        self.tok_probs_free = [ev_pt[-1], st["dts"][-1], st["mp"]]
        st.update(av=av, w2=w2, avc=avc, avpar=self._av_par)
        return st

    def l0_attn_back_tail(self, st):
        cfg, sb, ps = self.cfg, self.sb, self.ps
        h, qt = st["h"], st["qt"]
        AO = sb["AO"]
        NT = cfg.NT
        par, ch = h % 2, h // 2
        av, w2, avc, avpar = st["av"], st["w2"], st["avc"], st["avpar"]
        aos = AO[par * 64:par * 64 + 64, ch * NT + qt * 128:ch * NT + (qt + 1) * 128]
        ao1 = self.act(aos, ps[7][0:64, avc:avc + 128], func=AF.Identity,
                       bias=sb["ctvT"][:, 0:1], deps=[av])
        ao = self.op("dve", lambda e, aos=aos, avc=avc: e.tensor_tensor(
            out=aos, in0=aos, in1=ps[7][0:64, 256 + avc:256 + avc + 128],
            op=ALU.add), [ao1, w2])
        self.tok_av_free[avpar] = ao1
        self.tok_w2_free[avpar] = ao
        return ao

    def oproj_group(self, wname, g, ao_toks, bias_idx=None, wo_free=None,
                    delta=None):
        cfg, sb, ps = self.cfg, self.sb, self.ps
        AO = sb["AO"]
        S, D, DC, NT = cfg.S, cfg.D, cfg.DC, cfg.NT
        col = g * NT
        toks = []
        for m in range(DC):
            cst = self.stage_cols(wname, m * 128, 128, DC,
                                  [wo_free[0], self.tok_hg_free, self.tok_hg_free2])
            wo_free[0] = cst
            self.tok_hg_free = cst
            self.tok_hg_free2 = cst
            bk = m % 2
            mmo = None
            for kc in range(DC):
                mmo = self.mm(bk, ps[bk][:, 0:NT],
                              sb["hg"][:, kc * 128:(kc + 1) * 128],
                              AO[:, kc * NT:(kc + 1) * NT],
                              kc == 0, kc == DC - 1, deps=[cst] + ao_toks)
            par = m % 2
            bias = (sb["biasT"][:, bias_idx * DC + m:bias_idx * DC + m + 1]
                    if bias_idx is not None else None)
            if delta is not None:
                eo = self.act(delta[:, m * NT:(m + 1) * NT],
                              ps[bk][:, 0:NT],
                              func=AF.Identity if bias is not None else AF.Copy,
                              bias=bias, deps=[mmo] + self.tok_xr_free)
                self.free_bank(bk, eo)
                sp = self.dma("pool", "dsp", lambda e, m=m, col=col: e.dma_start(
                    out=self.dram["dscr"][:, m * S + col:m * S + col + NT],
                    in_=delta[:, m * NT:(m + 1) * NT]), [eo])
                wo_free[1] = eo
                toks.append(sp)
            else:
                ev = sb["evac"][:, par * NT:(par + 1) * NT]
                eo = self.act(ev, ps[bk][:, 0:NT],
                              func=AF.Identity if bias is not None else AF.Copy,
                              bias=bias, deps=[mmo, self.tok_evac_free[par]])
                self.free_bank(bk, eo)
                ad = self.op("dve", lambda e, m=m, col=col, ev=ev: e.tensor_tensor(
                    out=sb["xr"][:, m * S + col:m * S + col + NT],
                    in0=sb["xr"][:, m * S + col:m * S + col + NT],
                    in1=ev, op=ALU.add), [eo] + self.tok_xr_free)
                self.tok_evac_free[par] = ad
                wo_free[1] = ad
                toks.append(ad)
        return toks

    def l0_phase_b(self, mrow, rrow, xng, stat_toks, kv_toks, delta):
        cfg = self.cfg
        wq_free = [None, None]
        wo_free = [None, None]
        all_toks = []
        prev_stats = list(stat_toks)
        for g in range(cfg.NTT):
            stat_g = self.ln_stats_one(mrow, rrow, 0, g, 0, prev_stats)
            xnt = self.xn_tile(lambda c: xng[:, c * cfg.NT:(c + 1) * cfg.NT],
                               mrow, rrow, g, stat_g, bm=2, br=3, stats_col=0)
            prev_stats = stat_g + xnt
            ao_toks = []
            pend = None
            for h in range(cfg.NH):
                eq, erl, mxe = self.l0_attn_head(h, g, xng, xnt, kv_toks, wq_free)
                self._l0pp = getattr(self, "_l0pp", 0) ^ 1
                nxt = self.l0_attn_front_a(h, g, 0, eq, erl, mxe, kv_toks,
                                           str(self._l0pp))
                for qt in range(cfg.QG):
                    st = self.l0_attn_fb_act(nxt)
                    if qt + 1 < cfg.QG:
                        self._l0pp ^= 1
                        nxt = self.l0_attn_front_a(h, g, qt + 1, eq, erl, mxe,
                                                   kv_toks, str(self._l0pp))
                    if pend is not None:
                        pend = self.l0_attn_back_pe(pend, kv_toks)
                    st = self.l0_attn_fb_dve(st)
                    if pend is not None:
                        ao_toks.append(self.l0_attn_back_tail(pend))
                    pend = st
                self.tok_qg_free = self.fw.last("pe")
            pend = self.l0_attn_back_pe(pend, kv_toks)
            ao_toks.append(self.l0_attn_back_tail(pend))
            pend = None
            t = self.oproj_group("Wo", g, ao_toks, bias_idx=3, wo_free=wo_free,
                                 delta=delta)
            all_toks += t
        return all_toks

    def l0_phase_c(self, mrow, rrow, xng, stat_toks, h_all, wif, wih, wif2, wih2,
                   delta=None, bar_deps=()):
        cfg, sb, ps = self.cfg, self.sb, self.ps
        S, D, DC, NT, HC = cfg.S, cfg.D, cfg.DC, cfg.NT, cfg.HC
        bd = list(bar_deps)
        gfree = [bd, bd]
        dfree = [None, None]
        toks = []
        self.tok_dfree = {}
        prev_stats = list(stat_toks) + bd
        for t in range(cfg.NTT):
            col = t * NT
            stat_t = self.ln_stats_one(mrow, rrow, 0, t, 0, prev_stats)
            xnt = self.xn_tile(lambda c: xng[:, c * NT:(c + 1) * NT],
                               mrow, rrow, t, stat_t, bm=6, br=7, stats_col=0)
            prev_stats = stat_t + xnt
            h_toks = []
            for hc in range(HC):
                g0 = gfree[0] if isinstance(gfree[0], list) else [gfree[0]]
                srcg = self.dram["wgT"][hc * 128:(hc + 1) * 128, :]
                dstg = wif[:, 0:DC * 128]
                dg = [self.dma("sp", "w_wg", lambda e, srcg=srcg, dstg=dstg:
                               e.dma_start(out=dstg, in_=srcg), g0)]
                cg = self.act(wih[:, 0:DC * 128], wif[:, 0:DC * 128], deps=dg)
                g1 = gfree[1] if isinstance(gfree[1], list) else [gfree[1]]
                srcu = self.dram["wuT"][hc * 128:(hc + 1) * 128, :]
                dstu = wif2[:, 0:DC * 128]
                du = [self.dma("pool", "w_wu", lambda e, srcu=srcu, dstu=dstu:
                               e.dma_start(out=dstu, in_=srcu), g1)]
                cu = self.act(wih2[:, 0:DC * 128], wif2[:, 0:DC * 128], deps=du)
                mg = mu = None
                bg, bu = 0 + (hc % 2) * 2, 1 + (hc % 2) * 2
                for kc in range(DC):
                    mg = self.mm(bg, ps[bg][:, 0:NT], wih[:, kc * 128:(kc + 1) * 128],
                                 xng[:, kc * NT:(kc + 1) * NT],
                                 kc == 0, kc == DC - 1, deps=[cg] + xnt)
                    mu = self.mm(bu, ps[bu][:, 0:NT], wih2[:, kc * 128:(kc + 1) * 128],
                                 xng[:, kc * NT:(kc + 1) * NT],
                                 kc == 0, kc == DC - 1, deps=[cu] + xnt)
                gfree[0] = mg
                gfree[1] = mu
                eg = self.act(sb["hg"][:, 0:NT], ps[bg][:, 0:NT],
                              func=AF.Silu, scale=1.0 / 127.0,
                              deps=[mg, self.tok_hgc_free])
                self.free_bank(bg, eg)
                eu = self.act(sb["hg"][:, NT:2 * NT], ps[bu][:, 0:NT],
                              func=AF.Copy, scale=1.0 / 127.0,
                              deps=[mu, self.tok_hgc_free])
                self.free_bank(bu, eu)
                hm = self.op("dve", lambda e, hc=hc: e.tensor_mul(
                    out=h_all[:, hc * NT:(hc + 1) * NT],
                    in0=sb["hg"][:, 0:NT], in1=sb["hg"][:, NT:2 * NT]),
                    [eg, eu])
                self.tok_hgc_free = hm
                h_toks.append(hm)
            self.tok_dld = {}
            for m in range(DC):
                dl_ = self.dma("pool", "dsp", lambda e, m=m, col=col: e.dma_start(
                    out=delta[:, m * NT:(m + 1) * NT],
                    in_=self.dram["dscr"][:, m * S + col:m * S + col + NT]),
                    [self.tok_dfree.get(m)] + bd)
                self.tok_dld[m] = dl_
            mmd = {}
            for hc in range(HC):
                dd = self.dma("sp", "w_wd", lambda e, hc=hc: e.dma_start(
                    out=wif[:, 0:D],
                    in_=self.dram["wdT"][hc * 128:(hc + 1) * 128, :]),
                    [dfree[0]] + (gfree[0] if isinstance(gfree[0], list) else [gfree[0]]))
                cd = self.act(wih[:, 0:D], wif[:, 0:D],
                              deps=[dd, dfree[1]] +
                              (gfree[1] if isinstance(gfree[1], list) else [gfree[1]]))
                dfree[0] = cd
                for m in range(DC):
                    mmd[m] = self.mm(m, ps[m][:, 0:NT],
                                     wih[:, m * 128:(m + 1) * 128],
                                     h_all[:, hc * NT:(hc + 1) * NT],
                                     hc == 0, hc == HC - 1,
                                     deps=[cd, h_toks[hc]])
                dfree[1] = mmd[DC - 1]
            for m in range(DC):
                par = m % 2
                ev = sb["evac"][:, par * NT:(par + 1) * NT]
                eo = self.act(ev, ps[m][:, 0:NT], func=AF.Copy, scale=1.0 / 127.0,
                              deps=[mmd[m], self.tok_evac_free[par]])
                self.free_bank(m, eo)
                ad = self.op("dve", lambda e, m=m, col=col, ev=ev: e.tensor_tensor(
                    out=sb["xr"][:, m * S + col:m * S + col + NT],
                    in0=sb["xr"][:, m * S + col:m * S + col + NT],
                    in1=ev, op=ALU.add), [eo] + self.tok_xr_free)
                ad2 = self.op("dve", lambda e, m=m, col=col: e.tensor_tensor(
                    out=sb["xr"][:, m * S + col:m * S + col + NT],
                    in0=sb["xr"][:, m * S + col:m * S + col + NT],
                    in1=delta[:, m * NT:(m + 1) * NT], op=ALU.add),
                    [ad, self.tok_dld[m]])
                self.tok_evac_free[par] = ad2
                self.tok_dfree[m] = ad2
                toks.append(ad2)
            gfree[0] = dfree[1]
            gfree[1] = dfree[1]
        return toks

    def reset_attn_toks(self):
        self.tok_qg_free = None
        self.tok_mx_free = None
        self.tok_kmax = None
        self.tok_hg_free = None
        self.tok_hg_free2 = None
        self.tok_uc_free = None
        self.tok_b1_free = [None, None]
        self.tok_b2_free = [None, None]
        self.tok_bandp_free = [None, None]
        self.tok_probs_free = []
        self.tok_diagR_free = [None, None]
        self.tok_d8t_free = [None, None]
        self.tok_d8_free = [None, None]
        self.tok_pts_free = None
        self.tok_hgc_free = None
        self.tok_evac_free = [None, None]
        self.tok_xr_free = []
        self.tok_lgr_free = None
        self.tok_gat_free = None
        self.tok_mid_free = None
        self.tok_zt_free = None
        self.tok_qaug_free = None
        self.tok_ropeq_free = None
        self.tok_qr_free = None
        self.tok_av_free = [None, None]
        self.tok_w2_free = [None, None]

    def build(self, layers=(0, 1), nseq=None):
        cfg = self.cfg
        nseq = nseq if nseq is not None else cfg.B_local
        self.declare()
        with ExitStack() as ctx0:
            self.ctx = ctx0
            if True:
                self.alloc_persistent(ctx0)
                self.alloc_psum(ctx0)
                self.reset_attn_toks()
                self.emit_init()
                dep0 = self.init_toks
                fin = []
                for seq in range(nseq):
                    xl = self.load_x(seq, dep0)
                    if 0 in layers:
                        with ExitStack() as c_l0:
                            mrow = self.sbuf(c_l0, "mrow0", [1, cfg.NT], F32)
                            rrow = self.sbuf(c_l0, "rrow0", [1, cfg.NT], F32)
                            xng = self.sbuf(c_l0, "xng0", [128, cfg.DC * cfg.NT], BF16)
                            delta = self.sbuf(c_l0, "delta0", [128, cfg.DC * cfg.NT], BF16)
                            with ExitStack() as c_kv:
                                Kf = self.sbuf(c_kv, "Kf", [128, cfg.DC * cfg.S], BF16)
                                Vf = self.sbuf(c_kv, "Vf", [128, (cfg.S // 128) * cfg.D], BF16)
                                bvh = self.sbuf(c_kv, "bvh", [1, cfg.D], BF16)
                                tb_ = self.dma("sp", "icv", lambda e: e.dma_start(
                                    out=self.sb["sq"][0:1, 0:cfg.D],
                                    in_=self.dram["bv_row"]), dep0)
                                cb_ = self.act(bvh[0:1, :], self.sb["sq"][0:1, 0:cfg.D],
                                               deps=[tb_])
                                dep0 = dep0 + [cb_]
                                stat_toks, kv_toks = self.l0_phase_a(
                                    mrow, rrow, xng, xl + dep0)
                                if seq == 0:
                                    self.dbg("Kf", Kf[:, :], kv_toks,
                                             (128, cfg.DC * cfg.S), BF16)
                                    self.dbg("Vf", Vf[:, :], kv_toks,
                                             (128, (cfg.S // 128) * cfg.D), BF16)
                                with ExitStack() as c_b:
                                    self.sbuf(c_b, "cMstep", [128, 8 * cfg.W], BF16)
                                    self.sbuf(c_b, "cMpre", [128, 131], BF16)
                                    self.sbuf(c_b, "cDdiag", [128, 7 * 131], BF16)
                                    self.sbuf(c_b, "probs0", [128, cfg.S], BF16)
                                    self.sbuf(c_b, "probs1", [128, cfg.S], BF16)
                                    self.sbuf(c_b, "pTs", [128, cfg.S], BF16)
                                    self.sbuf(c_b, "AO", [128, cfg.DC * cfg.NT], BF16)
                                    self.sbuf(c_b, "wqs", [128, cfg.DC * 64], F32)
                                    self.sbuf(c_b, "wqh", [128, cfg.DC * 64], BF16)
                                    self.sbuf(c_b, "bandp", [128, 2 * cfg.W], BF16)
                                    self.sbuf(c_b, "qrel_s", [cfg.NREL, cfg.NT], BF16)
                                    self.sbuf(c_b, "Qg", [128, cfg.NT], BF16)
                                    self.sbuf(c_b, "Qr", [128, cfg.NT], BF16)
                                    mdep = [self.fw.last("act"), self.fw.last("dve")]
                                    mq_ = self.op("dve", lambda e, q_=self.sb["qrel_s"]:
                                                  e.memset(q_[:, :], 0.0), mdep)
                                    mg_ = self.op("dve", lambda e, q_=self.sb["Qg"]:
                                                  e.memset(q_[:, :], 0.0), mdep)
                                    kv_toks = kv_toks + [mq_, mg_]
                                    ml = [self.dma("sp", "mk",
                                        lambda e, dst=self.sb["c" + n], src=self.dram[n]:
                                        e.dma_start(out=dst[:, :], in_=src), mdep)
                                        for n in ("Mstep", "Mpre", "Ddiag")]
                                    kv_toks = kv_toks + ml
                                    self.l0_phase_b(mrow, rrow, xng, stat_toks, kv_toks, delta)
                                    bar1 = self.bar()
                                    if seq == 0:
                                        self.dbg("xrB", delta[:, :], bar1,
                                                 (128, cfg.DC * cfg.S), BF16)
                                        self.dbg("probs", self.sb["probs0"][:, :], bar1,
                                                 (128, cfg.S), BF16)
                            with ExitStack() as c_c:
                                h_all = self.sbuf(c_c, "h_all", [128, cfg.HC * cfg.NT], BF16)
                                wif = self.sbuf(c_c, "wif", [128, cfg.D], I32)
                                wih = self.sbuf(c_c, "wih", [128, cfg.D], BF16)
                                wif2 = self.sbuf(c_c, "wif2", [128, cfg.D], I32)
                                wih2 = self.sbuf(c_c, "wih2", [128, cfg.D], BF16)
                                self.tok_xr_free = bar1
                                self.tok_hgc_free = bar1[0]
                                self.l0_phase_c(mrow, rrow, xng, stat_toks, h_all,
                                                wif, wih, wif2, wih2, delta=delta,
                                                bar_deps=bar1)
                                if seq == 0:
                                    self.dbg("h_all", h_all[:, :], self.bar(),
                                             (128, cfg.HC * cfg.NT), BF16)
                                    self.dbg("hgate", self.sb["hg"][:, :],
                                             self.bar(), (128, 2 * cfg.NT), BF16)
                                    self.dbg("xng", xng[:, :], self.bar(),
                                             (128, cfg.DC * cfg.NT), BF16)
                            bar2 = self.bar()
                    else:
                        bar2 = self.bar() + xl
                    if 1 in layers:
                        bar2 = self.emit_l1(seq, bar2)
                    st = self.store_y(seq, bar2)
                    fin += st
                    dep0 = st
                self.fw.set_finish(fin + self._dbg_toks)
                with self.nc.Block() as block:
                    self.fw.finalize(block, ctx0)
        return self.nc

    def l1_phase_a(self, mrow, rrow, xng, kvn, Kf, Vf, KR, QC, ccos, csin, deps):
        cfg, sb, ps = self.cfg, self.sb, self.ps
        lgr = sb["lgr"]
        mid = sb["mid"]
        Zt = sb["Zt"]
        gat = sb["gat"]
        lBh = sb["lBh"]
        tok_cosA = [None]
        S, D, DC, NT = cfg.S, cfg.D, cfg.DC, cfg.NT
        KV = cfg.KV_RANK
        KVC = KV // 128
        QRC = cfg.Q_RANK // 128
        RCC = cfg.NH * cfg.RC // 128
        kfree = [None, None]
        out_toks = []
        prev_stats = list(deps)
        xnt_prev = []
        for t in range(cfg.NTT):
            col = t * NT
            stat_toks = self.ln_stats_one(mrow, rrow, 0, t, 0, deps + prev_stats)
            prev_stats = stat_toks
            if kfree[0] is None:
                kfree = [stat_toks[-1], stat_toks[-1]]
            cA = self.dma("sp", "cs", lambda e, col=col: e.dma_start(
                out=ccos[:, 0:NT], in_=self.dram["cos_rep"][:, col:col + NT]),
                [tok_cosA[0]])
            cB = self.dma("sp", "cs", lambda e, col=col: e.dma_start(
                out=csin[:, 0:NT], in_=self.dram["sin_rep"][:, col:col + NT]),
                [tok_cosA[0]])
            xnt = self.xn_tile(lambda c: xng[:, c * NT:(c + 1) * NT],
                               mrow, rrow, t, stat_toks, stats_col=0)
            prev_stats = stat_toks + xnt
            xnt_prev = xnt
            mmkv = {}
            for m in range(KVC):
                cst = self.stage_cols("Wkv", m * 128, 128, DC, [kfree[0], kfree[1]], ch="w_kv")
                kfree[0] = cst
                mk = None
                for kc in range(DC):
                    mk = self.mm(m, ps[m][:, 0:NT],
                                 sb["hg"][:, kc * 128:(kc + 1) * 128],
                                 xng[:, kc * NT:(kc + 1) * NT],
                                 kc == 0, kc == DC - 1, deps=[cst] + xnt)
                mmkv[m] = mk
                kfree[1] = mk
            mms = None
            sqp = None
            for m in range(KVC):
                sq_ = self.act(sb["hg"][:, 0:NT], ps[m][:, 0:NT], func=AF.Square,
                               deps=[mmkv[m], kfree[1], sqp or self.tok_hgc_free])
                mms = self.mm(4, ps[4][0:1, 0:NT], sb["cones_colh"][:, 0:1],
                              sb["hg"][:, 0:NT], m == 0, m == KVC - 1, deps=[sq_])
                sqp = mms
            self.tok_hgc_free = mms
            st_ = self.act(sb["srow"][0:1, 0:NT], ps[4][0:1, 0:NT], func=AF.Sqrt,
                           scale=1.0 / KV, bias=sb["epsrow"][0:1, 1:2], deps=[mms])
            self.free_bank(4, st_)
            rt_ = self.op("dve", lambda e: e.reciprocal(
                out=sb["vrow"][0:1, 0:NT], in_=sb["srow"][0:1, 0:NT]), [st_])
            bc = self.mm(5, ps[5][:, 0:NT], sb["cones_row"][:, :],
                         sb["vrow"][0:1, 0:NT], True, True, deps=[rt_])
            kvn_toks = []
            for m in range(KVC):
                par = m % 2
                ev = sb["evac"][:, par * NT:(par + 1) * NT]
                e1 = self.act(ev, ps[m][:, 0:NT], func=AF.Identity,
                              scale=sb["cgkv"][:, m:m + 1],
                              deps=[mmkv[m], self.tok_evac_free[par]])
                self.free_bank(m, e1)
                e2 = self.op("dve", lambda e, m=m, ev=ev: e.tensor_mul(
                    out=kvn[:, m * NT:(m + 1) * NT], in0=ev, in1=ps[5][:, 0:NT]),
                    [e1, bc])
                self.tok_evac_free[par] = e2
                kvn_toks.append(e2)
            self.free_bank(5, kvn_toks[-1])
            for m in range(DC):
                cst = self.stage_cols("Wkdec", m * 128, 128, KVC, [kfree[0], kfree[1]], ch="w_kd")
                kfree[0] = cst
                bk = 6 + (m % 2)
                mk = None
                for kc in range(KVC):
                    mk = self.mm(bk, ps[bk][:, 0:NT],
                                 sb["hg"][:, kc * 128:(kc + 1) * 128],
                                 kvn[:, kc * NT:(kc + 1) * NT],
                                 kc == 0, kc == KVC - 1, deps=[cst] + kvn_toks)
                ek = self.act(Kf[:, m * S + col:m * S + col + NT], ps[bk][:, 0:NT],
                              deps=[mk])
                self.free_bank(bk, ek)
                kfree[1] = ek
                out_toks.append(ek)
            out_toks += self.v_tm_tile_src("Wvdec", kvn, t, KVC, D // 512, None,
                                           sb["sq"], sb["hg"], kfree, kvn_toks, Vf, NT)
            for m in range(RCC):
                cst = self.stage_cols("Wkrope", m * 128, 128, DC, [kfree[0], kfree[1]], ch="w_kr")
                kfree[0] = cst
                mk = None
                for kc in range(DC):
                    mk = self.mm(4, ps[4][:, 0:NT],
                                 sb["hg"][:, kc * 128:(kc + 1) * 128],
                                 xng[:, kc * NT:(kc + 1) * NT],
                                 kc == 0, kc == DC - 1, deps=[cst] + xnt)
                kfree[1] = mk
                ekr = self.act(sb["hg"][:, NT:2 * NT], ps[4][:, 0:NT],
                               deps=[mk, self.tok_hg_free])
                self.free_bank(4, ekr)
                rot = self.mm(5, ps[5][:, 0:NT], sb["cR128T"][:, :],
                              sb["hg"][:, NT:2 * NT], True, True, deps=[ekr])
                t1 = self.op("dve", lambda e: e.tensor_mul(
                    out=sb["hg"][:, 0:NT], in0=sb["hg"][:, NT:2 * NT],
                    in1=ccos[:, 0:NT]), [ekr, self.tok_hgc_free, cA])
                t2 = self.op("dve", lambda e, m=m, col=col: e.scalar_tensor_tensor(
                    out=KR[:, m * S + col:m * S + col + NT], in0=ps[5][:, 0:NT],
                    scalar=1.0, in1=csin[:, 0:NT],
                    op0=ALU.mult, op1=ALU.mult), [rot, cB])
                t3 = self.op("dve", lambda e, m=m, col=col: e.tensor_tensor(
                    out=KR[:, m * S + col:m * S + col + NT],
                    in0=KR[:, m * S + col:m * S + col + NT],
                    in1=sb["hg"][:, 0:NT], op=ALU.add), [t1, t2])
                self.free_bank(5, t2)
                self.tok_hg_free = t3
                self.tok_hgc_free = t3
                tok_cosA[0] = t3
                out_toks.append(t3)
            for m in range(QRC):
                cst = self.stage_cols("Wq1", m * 128, 128, DC, [kfree[0], kfree[1]], ch="w_q1")
                kfree[0] = cst
                bk = 6 + (m % 2)
                mk = None
                for kc in range(DC):
                    mk = self.mm(bk, ps[bk][:, 0:NT],
                                 sb["hg"][:, kc * 128:(kc + 1) * 128],
                                 xng[:, kc * NT:(kc + 1) * NT],
                                 kc == 0, kc == DC - 1, deps=[cst] + xnt)
                eq = self.act(QC[:, m * S + col:m * S + col + NT], ps[bk][:, 0:NT],
                              deps=[mk])
                self.free_bank(bk, eq)
                kfree[1] = eq
                out_toks.append(eq)
            mr = None
            for kc in range(DC):
                mr = self.mm(4, ps[4][0:8, 0:NT], sb["rwh"][:, kc * 8:(kc + 1) * 8],
                             xng[:, kc * NT:(kc + 1) * NT],
                             kc == 0, kc == DC - 1, deps=xnt)
            er = self.act(lgr[0:8, 0:NT], ps[4][0:8, 0:NT], func=AF.Identity,
                          bias=sb["crb"][:, 0:1], deps=[mr, self.tok_lgr_free])
            self.free_bank(4, er)
            b0 = self.mm(5, ps[5][0:8, 0:NT], sb["cones_row"][0:1, 0:8],
                         lgr[0:1, 0:NT], True, True, deps=[er])
            dl = self.op("dve", lambda e: e.tensor_sub(
                out=lgr[0:8, NT:2 * NT], in0=lgr[0:8, 0:NT],
                in1=ps[5][0:8, 0:NT]), [b0, self.tok_lgr_free])
            self.free_bank(5, dl)
            ex = self.act(lgr[0:8, 0:NT], lgr[0:8, NT:2 * NT],
                          func=AF.Exp, deps=[dl])
            sm = self.mm(4, ps[4][0:1, 0:NT], sb["cones_col"][0:8, 0:1],
                         lgr[0:8, 0:NT], True, True, deps=[ex])
            rc = self.op("dve", lambda e: e.reciprocal(
                out=sb["srow"][0:1, 0:NT], in_=ps[4][0:1, 0:NT]), [sm])
            self.free_bank(4, rc)
            bs = self.mm(5, ps[5][0:8, 0:NT], sb["cones_row"][0:1, 0:8],
                         sb["srow"][0:1, 0:NT], True, True, deps=[rc])
            gt = self.op("dve", lambda e: e.tensor_mul(
                out=gat[0:8, 0:NT], in0=lgr[0:8, 0:NT],
                in1=ps[5][0:8, 0:NT]), [bs, self.tok_gat_free])
            self.free_bank(5, gt)
            self.tok_lgr_free = gt
            mm_ = None
            for kc in range(DC):
                mm_ = self.mm(4, ps[4][0:8, 0:NT], sb["lAh"][:, kc * 8:(kc + 1) * 8],
                              xng[:, kc * NT:(kc + 1) * NT],
                              kc == 0, kc == DC - 1, deps=xnt)
            em = self.act(mid[0:8, 0:NT], ps[4][0:8, 0:NT],
                          deps=[mm_, self.tok_mid_free])
            self.free_bank(4, em)
            gr = self.mm(4, ps[4][0:64, 0:NT], sb["cSELe"][:, :], gat[0:8, 0:NT],
                         True, True, deps=[gt])
            mrp = self.mm(5, ps[5][0:64, 0:NT], sb["cSELr"][:, :], mid[0:8, 0:NT],
                          True, True, deps=[em])
            ev = sb["evac"][:, 0:NT]
            eg = self.act(ev[0:64, :], ps[4][0:64, 0:NT],
                          deps=[gr, self.tok_evac_free[0]])
            self.free_bank(4, eg)
            zt = self.op("dve", lambda e: e.tensor_mul(
                out=Zt[0:64, 0:NT], in0=ev[0:64, :], in1=ps[5][0:64, 0:NT]),
                [eg, mrp, self.tok_zt_free])
            self.free_bank(5, zt)
            self.tok_evac_free[0] = zt
            for m in range(DC):
                cst = self.stage_cols("bwT", m * 128, 128, DC, [kfree[0], kfree[1]], ch="w_bw")
                kfree[0] = cst
                bk = 6 + (m % 2)
                mb = None
                for kc in range(DC):
                    mb = self.mm(bk, ps[bk][:, 0:NT],
                                 sb["hg"][:, kc * 128:(kc + 1) * 128],
                                 xng[:, kc * NT:(kc + 1) * NT],
                                 kc == 0, False, deps=[cst] + xnt)
                ml_ = self.mm(bk, ps[bk][:, 0:NT], lBh[:, m * 128:(m + 1) * 128],
                              Zt[:, 0:NT], False, True, deps=[zt, mb])
                kfree[1] = ml_
                par = m % 2
                ev2 = sb["evac"][:, par * NT:(par + 1) * NT]
                eo = self.act(ev2, ps[bk][:, 0:NT], func=AF.Identity,
                              bias=sb["biasT"][:, 4 * DC + m:4 * DC + m + 1],
                              deps=[ml_, self.tok_evac_free[par]])
                self.free_bank(bk, eo)
                ad = self.op("dve", lambda e, m=m, col=col, ev2=ev2: e.tensor_tensor(
                    out=sb["xr"][:, m * S + col:m * S + col + NT],
                    in0=sb["xr"][:, m * S + col:m * S + col + NT],
                    in1=ev2, op=ALU.add), [eo] + self.tok_xr_free)
                self.tok_evac_free[par] = ad
                out_toks.append(ad)
            self.tok_mid_free = mrp
            self.tok_gat_free = gr
            self.tok_zt_free = ml_
        return stat_toks, out_toks

    def v_tm_tile_src(self, wrows_name, src, t, kchunks, nn, bias_row,
                      wstage_f, wstage_h, wfree, src_toks, Vf, src_nt):
        cfg, sb = self.cfg, self.sb
        D, NT = cfg.D, cfg.NT
        ntc = NT // 128
        mms = {}
        cstp = wfree[0]
        last_mm = wfree[1]
        for kc in range(kchunks):
            dt = self.dma("sp", "w_" + wrows_name, lambda e, kc=kc: e.dma_start(
                out=wstage_f[:, 0:D],
                in_=self.dram[wrows_name][kc * 128:(kc + 1) * 128, :]), [cstp])
            cst = self.act(wstage_h[:, 0:D], wstage_f[:, 0:D], deps=[dt, last_mm])
            cstp = cst
            for tc in range(ntc):
                for n2 in range(nn):
                    bank = tc * nn + n2
                    mms[(tc, n2)] = self.mm(
                        bank, self.ps[bank][:, 0:512],
                        src[:, kc * src_nt + tc * 128: kc * src_nt + (tc + 1) * 128],
                        wstage_h[:, n2 * 512:(n2 + 1) * 512],
                        kc == 0, kc == kchunks - 1, deps=[cst] + src_toks)
                    last_mm = mms[(tc, n2)]
        wfree[0] = cstp
        evs = []
        for tc in range(ntc):
            for n2 in range(nn):
                bank = tc * nn + n2
                ev = self.act(Vf[:, (t * ntc + tc) * D + n2 * 512:
                                 (t * ntc + tc) * D + (n2 + 1) * 512],
                              self.ps[bank][:, 0:512], deps=[mms[(tc, n2)]])
                self.free_bank(bank, ev)
                evs.append(ev)
        wfree[1] = evs[-1]
        return evs

    def l1_attn_head(self, h, g, Kf, KR, QC, ccos, csin, a_toks, wq_free):
        cfg, sb, ps = self.cfg, self.sb, self.ps
        wqs = sb["wqs"]
        wqh = sb["wqh"]
        ropeq = sb["ropeq"]
        qaug = sb["qaug"]
        Qg = sb["Qg"]
        Qr = sb["Qr"]
        cSELq = sb["cSELq"]
        S, D, DC, NT, DH, RC = cfg.S, cfg.D, cfg.DC, cfg.NT, cfg.DH, cfg.RC
        par, ch = h % 2, h // 2
        qch = h % 4
        rch = h // 4
        QRC = cfg.Q_RANK // 128
        col = g * NT
        dts = [self.dma("sp", "w_qa", lambda e: e.dma_start(
            out=wqs[:, 0:QRC * 96],
            in_=self.dram["Wqaug"][h * 128:(h + 1) * 128, :]),
            [wq_free[0]])]
        cst = self.act(wqh[:, 0:QRC * 96], wqs[:, 0:QRC * 96],
                       deps=dts + [wq_free[1]])
        wq_free[0] = cst
        mq = None
        for kc in range(QRC):
            mq = self.mm(4, ps[4][0:96, 0:NT], wqh[:, kc * 96:(kc + 1) * 96],
                         QC[:, kc * S + col:kc * S + col + NT],
                         kc == 0, kc == QRC - 1, deps=[cst] + a_toks)
        wq_free[1] = mq
        eq = self.act(qaug[0:96, 0:NT], ps[4][0:96, 0:NT],
                      deps=[mq, self.tok_qaug_free])
        self.free_bank(4, eq)
        rot = self.mm(5, ps[5][0:32, 0:NT], sb["cR32Tp"][64:96, :],
                      qaug[64:96, 0:NT], True, True, deps=[eq])
        t0 = self.act(sb["hg"][0:32, 0:NT], qaug[64:96, 0:NT],
                      deps=[eq, self.tok_hg_free])
        t1 = self.op("dve", lambda e: e.tensor_mul(
            out=sb["hg"][0:32, NT:2 * NT], in0=sb["hg"][0:32, 0:NT],
            in1=ccos[0:32, 0:NT]), [t0, self.tok_hgc_free])
        t2 = self.op("dve", lambda e: e.scalar_tensor_tensor(
            out=ropeq[0:32, 0:NT], in0=ps[5][0:32, 0:NT], scalar=1.0,
            in1=csin[0:32, 0:NT], op0=ALU.mult, op1=ALU.mult),
            [rot, self.tok_ropeq_free])
        t3 = self.op("dve", lambda e: e.tensor_tensor(
            out=ropeq[0:32, 0:NT], in0=ropeq[0:32, 0:NT],
            in1=sb["hg"][0:32, NT:2 * NT], op=ALU.add), [t1, t2])
        self.free_bank(5, t2)
        self.tok_hg_free = t1
        self.tok_hgc_free = t3
        ms = self.op("dve", lambda e: e.memset(
            Qg[(1 - par) * 64:(1 - par) * 64 + 64, :], 0.0),
            [self.tok_qg_free])
        qc_ = self.act(Qg[par * 64:par * 64 + 64, :], qaug[0:64, 0:NT],
                       deps=[eq, ms, self.tok_qg_free])
        qr_ = self.mm(5, ps[5][:, 0:NT], cSELq[:, qch * 128:(qch + 1) * 128],
                      ropeq[0:32, 0:NT], True, True, deps=[t3])
        er_ = self.act(Qr[:, :], ps[5][:, 0:NT], deps=[qr_, self.tok_qr_free])
        self.free_bank(5, er_)
        self.tok_ropeq_free = qr_
        self.tok_qaug_free = qc_
        sq1 = self.act(sb["hg"][:, 0:NT], Qg[:, :], func=AF.Square,
                       deps=[qc_, self.tok_hg_free])
        n1 = self.mm(4, ps[4][0:1, 0:NT], sb["cones_colh"][:, 0:1],
                     sb["hg"][:, 0:NT], True, False, deps=[sq1])
        sq2 = self.act(sb["hg"][:, NT:2 * NT], Qr[:, :], func=AF.Square,
                       deps=[er_, self.tok_hgc_free])
        n2 = self.mm(4, ps[4][0:1, 0:NT], sb["cones_colh"][:, 0:1],
                     sb["hg"][:, NT:2 * NT], False, True, deps=[sq2])
        self.tok_hg_free = n1
        self.tok_hgc_free = n2
        sqt = self.act(sb["srow"][0:1, 0:NT], ps[4][0:1, 0:NT], func=AF.Sqrt,
                       deps=[n2])
        self.free_bank(4, sqt)
        qm = self.op("dve", lambda e: e.tensor_reduce(
            out=sb["hrow"][0:1, 0:1], in_=sb["srow"][0:1, 0:NT],
            axis=AXX.X, op=ALU.max), [sqt, self.tok_mx_free])
        if g == 0:
            kms = []
            for tq in range(cfg.NTT):
                sqk = self.act(sb["hg"][:, 0:NT],
                               Kf[:, ch * S + tq * NT:ch * S + (tq + 1) * NT],
                               func=AF.Square, deps=a_toks + [self.tok_hg_free])
                m1 = self.mm(4, ps[4][0:1, 0:NT],
                             sb["cones_e"][:, 0:1] if par == 0 else sb["cones_o"][:, 0:1],
                             sb["hg"][:, 0:NT], True, False, deps=[sqk])
                sqk2 = self.act(sb["hg"][:, NT:2 * NT],
                                KR[:, rch * S + tq * NT:rch * S + (tq + 1) * NT],
                                func=AF.Square, deps=a_toks + [self.tok_hgc_free])
                m2 = self.mm(4, ps[4][0:1, 0:NT], sb["cones_q"][:, qch:qch + 1],
                             sb["hg"][:, NT:2 * NT], False, True, deps=[sqk2])
                self.tok_hg_free = m1
                self.tok_hgc_free = m2
                km = self.op("dve", lambda e, tq=tq: e.tensor_reduce(
                    out=sb["hrow"][0:1, 8 + tq:9 + tq], in_=ps[4][0:1, 0:NT],
                    axis=AXX.X, op=ALU.max), [m2])
                self.free_bank(4, km)
                kms.append(km)
            c1 = self.op("dve", lambda e: e.tensor_reduce(
                out=sb["hrow"][0:1, 1:2], in_=sb["hrow"][0:1, 8:8 + cfg.NTT],
                axis=AXX.X, op=ALU.max), kms)
            c2 = self.act(sb["hrow"][0:1, 1:2], sb["hrow"][0:1, 1:2], func=AF.Sqrt,
                          deps=[c1])
            self.tok_kmax = c2
        mx2 = self.op("dve", lambda e: e.tensor_mul(
            out=sb["hrow"][0:1, 3:4], in0=sb["hrow"][0:1, 1:2],
            in1=sb["hrow"][0:1, 0:1]), [self.tok_kmax, qm])
        mxb = self.mm(4, ps[4][:, 0:1], sb["cones_row"][:, :],
                      sb["hrow"][0:1, 3:4], True, True, deps=[mx2])
        mxe = self.act(sb["MXcol"][:, 0:1], ps[4][:, 0:1], scale=-1.0 / (96 ** 0.5),
                       deps=[mxb])
        self.free_bank(4, mxe)
        self.tok_mx_free = mxe
        return qc_, er_, mxe

    def l1_attn_qtile(self, h, g, qt, qc_, er_, mxe, a_toks):
        cfg, sb, ps = self.cfg, self.sb, self.ps
        probs = sb["probs"]
        pTs = sb["pTs"]
        AO = sb["AO"]
        Vf = self.sb_Vf
        Kfl = self.sb_Kf
        KRl = self.sb_KR
        Qg = sb["Qg"]
        Qr = sb["Qr"]
        ccausal = sb["ccausal"]
        S, D, DC, NT, DH = cfg.S, cfg.D, cfg.DC, cfg.NT, cfg.DH
        par, ch = h % 2, h // 2
        rch = h // 4
        qti = g * cfg.QG + qt
        kmax_col = (qti + 1) * 128
        qs_g = Qg[:, qt * 128:(qt + 1) * 128]
        qs_r = Qr[:, qt * 128:(qt + 1) * 128]
        nb = (kmax_col + 511) // 512
        smm = []
        for b in range(nb):
            k0, k1 = b * 512, min(kmax_col, (b + 1) * 512)
            t1 = self.mm(b, ps[b][:, 0:k1 - k0], qs_g,
                         Kfl[:, ch * S + k0: ch * S + k1], True, False,
                         deps=[qc_] + a_toks)
            t2 = self.mm(b, ps[b][:, 0:k1 - k0], qs_r,
                         KRl[:, rch * S + k0: rch * S + k1], False, True,
                         deps=[er_] + a_toks)
            smm.append(t2)
        db = qti * 128 // 512
        doff = qti * 128 - db * 512
        cm = self.op("dve", lambda e, db=db, doff=doff: e.tensor_tensor(
            out=ps[db][:, doff:doff + 128], in0=ps[db][:, doff:doff + 128],
            in1=ccausal[:, :], op=ALU.add), [smm[db]])
        prb_toks = []
        etoks = []
        for b in range(nb):
            k0, k1 = b * 512, min(kmax_col, (b + 1) * 512)
            dep_sc = [cm if b == db else smm[b], mxe] + list(self.tok_probs_free)
            at = self.act(probs[:, k0:k1], ps[b][:, 0:k1 - k0],
                          func=AF.Exp, bias=sb["MXcol"][:, 0:1],
                          scale=1.0 / (96 ** 0.5),
                          accum=sb["accD"][:, 8 + b:9 + b], deps=dep_sc)
            prb_toks.append(at)
            etoks.append(at)
        for b in range(nb):
            self.free_bank(b, prb_toks[-1])
        s1 = None
        if nb == 1:
            s1 = self.op("dve", lambda e: e.tensor_copy(
                out=sb["Scol"][:, 0:1], in_=sb["accD"][:, 8:9]), etoks)
        else:
            s1 = self.op("dve", lambda e: e.tensor_add(
                out=sb["Scol"][:, 0:1], in0=sb["accD"][:, 8:9],
                in1=sb["accD"][:, 9:10]), etoks)
            for b in range(2, nb):
                s1 = self.op("dve", lambda e, b=b: e.tensor_add(
                    out=sb["Scol"][:, 0:1], in0=sb["Scol"][:, 0:1],
                    in1=sb["accD"][:, 8 + b:9 + b]), [s1] + etoks)
        rs = self.op("dve", lambda e: e.reciprocal(
            out=sb["recipS"][:, 0:1], in_=sb["Scol"][:, 0:1]), [s1])
        dR = self.op("dve", lambda e: e.tensor_scalar(
            out=sb["diagR0"][:, :], in0=sb["cI128"][:, :],
            scalar1=sb["recipS"][:, 0:1], scalar2=None, op0=ALU.mult),
            [rs, self.tok_diagR_free[0]])
        nkc = qti + 1
        ev_pt = []
        for bb in range((nkc + 3) // 4):
            bank = 5 + (bb % 2)
            mmp = []
            jmax = min(4, nkc - bb * 4)
            for j in range(jmax):
                kc = bb * 4 + j
                mmp.append(self.mm(bank, ps[bank][:, j * 128:(j + 1) * 128],
                                   probs[:, kc * 128:(kc + 1) * 128],
                                   sb["diagR0"][:, :], True, True,
                                   deps=prb_toks + [dR]))
            epv = self.op("dve", lambda e, bb=bb, bank=bank, jmax=jmax:
                          e.tensor_copy(out=pTs[:, bb * 512:bb * 512 + jmax * 128],
                                        in_=ps[bank][:, 0:jmax * 128]),
                          mmp + [self.tok_pts_free])
            self.free_bank(bank, epv)
            ev_pt.append(epv)
        self.tok_diagR_free[0] = ev_pt[-1]
        self._av_par = getattr(self, "_av_par", 0) ^ 1
        avc = self._av_par * 128
        avdep = self.tok_av_free[self._av_par] or self.pslast[7]
        self.pslast[7] = None
        av = None
        for kc in range(nkc):
            av = self.mm(7, ps[7][0:64, avc:avc + 128],
                         Vf[:, kc * D + h * DH:kc * D + (h + 1) * DH],
                         pTs[:, kc * 128:(kc + 1) * 128],
                         kc == 0, kc == nkc - 1,
                         deps=[ev_pt[kc // 4], avdep if kc == 0 else None] + a_toks)
        ao = self.act(AO[par * 64:par * 64 + 64,
                               ch * NT + qt * 128:ch * NT + (qt + 1) * 128],
                      ps[7][0:64, avc:avc + 128], deps=[av])
        self.tok_av_free[self._av_par] = ao
        self.tok_pts_free = av
        self.tok_probs_free = [ev_pt[-1]]
        return ao

    def l1_phase_b(self, Kf, KR, QC, ccos, csin, a_toks):
        cfg = self.cfg
        self.sb_Kf = Kf
        self.sb_KR = KR
        wq_free = [None, None]
        wo_free = [None, None]
        cos_free = [None]
        for g in range(cfg.NTT):
            col = g * cfg.NT
            cg1 = self.dma("sp", "csb", lambda e, col=col: e.dma_start(
                out=ccos[:, 0:cfg.NT],
                in_=self.dram["cos_rep"][0:32, col:col + cfg.NT]), [cos_free[0]])
            cg2 = self.dma("sp", "csb", lambda e, col=col: e.dma_start(
                out=csin[:, 0:cfg.NT],
                in_=self.dram["sin_rep"][0:32, col:col + cfg.NT]), [cos_free[0]])
            a_toks = a_toks + [cg1, cg2]
            ao_toks = []
            for h in range(cfg.NH):
                qc_, er_, mxe = self.l1_attn_head(h, g, Kf, KR, QC, ccos, csin,
                                                  a_toks, wq_free)
                for qt in range(cfg.QG):
                    ao = self.l1_attn_qtile(h, g, qt, qc_, er_, mxe, a_toks)
                    ao_toks.append(ao)
                self.tok_qg_free = self.fw.last("pe")
                self.tok_qr_free = self.fw.last("pe")
            cos_free[0] = self.fw.last("dve")
            self.oproj_group("Wo1", g, ao_toks, bias_idx=None, wo_free=wo_free)

    def emit_l1(self, seq, deps):
        cfg = self.cfg
        with ExitStack() as c_l1:
            Kf = self.sbuf(c_l1, "Kf1", [128, cfg.DC * cfg.S], BF16)
            Vf = self.sbuf(c_l1, "Vf1", [128, (cfg.S // 128) * cfg.D], BF16)
            KR = self.sbuf(c_l1, "KR1",
                           [128, (cfg.NH * cfg.RC // 128) * cfg.S], BF16)
            QC = self.sbuf(c_l1, "QC1", [128, (cfg.Q_RANK // 128) * cfg.S], BF16)
            cl = []
            with ExitStack() as c_a:
                mrow = self.sbuf(c_a, "mrow1", [1, cfg.NT], F32)
                rrow = self.sbuf(c_a, "rrow1", [1, cfg.NT], F32)
                xng = self.sbuf(c_a, "xng1", [128, cfg.DC * cfg.NT], BF16)
                kvn = self.sbuf(c_a, "kvn", [128, 2 * cfg.NT], BF16)
                self.sbuf(c_a, "lBh", [128, cfg.D], BF16)
                self.sbuf(c_a, "lgr", [8, 2 * cfg.NT], F32)
                self.sbuf(c_a, "mid", [128, cfg.NT], BF16)
                self.sbuf(c_a, "Zt", [128, cfg.NT], BF16)
                self.sbuf(c_a, "gat", [8, cfg.NT], BF16)
                ccos = self.sbuf(c_a, "ccosA", [128, cfg.NT], BF16)
                csin = self.sbuf(c_a, "csinA", [128, cfg.NT], BF16)
                tlb = self.dma("sp", "icb", lambda e: e.dma_start(
                    out=self.sb["evac"][:, 0:cfg.D], in_=self.dram["lBpad"]),
                    deps + [self.tok_evac_free[0], self.tok_evac_free[1]])
                clb = self.act(self.sb["lBh"][:, :], self.sb["evac"][:, 0:cfg.D],
                               scale=2.0, deps=[tlb])
                mm1_ = self.op("dve", lambda e: e.memset(self.sb["mid"][:, :], 0.0),
                               deps)
                mm2_ = self.op("dve", lambda e: e.memset(self.sb["Zt"][:, :], 0.0),
                               deps)
                self.tok_evac_free[0] = clb
                deps = deps + [clb, mm1_, mm2_]
                stat_toks, a_toks = self.l1_phase_a(
                    mrow, rrow, xng, kvn, Kf, Vf, KR, QC, ccos, csin, deps + cl)
            self.sb_Vf = Vf
            with ExitStack() as c_b1:
                self.sbuf(c_b1, "probs", [128, cfg.S], BF16)
                self.sbuf(c_b1, "pTs", [128, cfg.S], BF16)
                self.sbuf(c_b1, "AO", [128, cfg.DC * cfg.NT], BF16)
                self.sbuf(c_b1, "wqs", [128, 4 * 96], F32)
                self.sbuf(c_b1, "wqh", [128, 4 * 96], BF16)
                self.sbuf(c_b1, "ropeq", [32, cfg.NT], BF16)
                self.sbuf(c_b1, "qaug", [96, cfg.NT], BF16)
                self.sbuf(c_b1, "Qg", [128, cfg.NT], BF16)
                self.sbuf(c_b1, "Qr", [128, cfg.NT], BF16)
                self.sbuf(c_b1, "cSELq", [32, 4 * 128], BF16)
                self.sbuf(c_b1, "ccausal", [128, 128], F32)
                ccosB = self.sbuf(c_b1, "ccosG", [32, cfg.NT], BF16)
                csinB = self.sbuf(c_b1, "csinG", [32, cfg.NT], BF16)
                bdep = [self.fw.last("act"), self.fw.last("dve")]
                l1 = self.dma("sp", "mk", lambda e, d=self.sb["cSELq"]:
                              e.dma_start(out=d[:, :], in_=self.dram["SELq"]), bdep)
                l2 = self.dma("sp", "mk", lambda e, d=self.sb["ccausal"]:
                              e.dma_start(out=d[:, :], in_=self.dram["causal"]), bdep)
                mg1_ = self.op("dve", lambda e, q_=self.sb["Qg"]:
                               e.memset(q_[:, :], 0.0), bdep)
                a_toks = a_toks + [l1, l2, mg1_]
                self.l1_phase_b(Kf, KR, QC, ccosB, csinB, a_toks + cl)
            return self.bar()


def Kf_ap(b, ch, k0, k1):
    return b.sb_Kf[:, ch * b.cfg.S + k0: ch * b.cfg.S + k1]


def KR_ap(b, rch, k0, k1):
    return b.sb_KR[:, rch * b.cfg.S + k0: rch * b.cfg.S + k1]


def host_inputs(cfg, x_core, w):
    import ml_dtypes
    c = host_constants(cfg)
    d0 = host_weights_l0(cfg, w)
    d1 = host_weights_l1(cfg, w)
    inp = {}
    inp["x"] = np.ascontiguousarray(x_core.transpose(0, 2, 1))
    bf = ml_dtypes.bfloat16
    for n in ("Mstep", "Mpre", "Ddiag", "ones_e", "ones_o", "ones_q",
              "ones_colh", "cos_rep", "sin_rep", "R32T", "R32Tp", "R128T", "SELq",
              "SELe", "SELr", "I128", "ones_rowh"):
        inp[n] = c[n].astype(bf)
    for n in ("PcT", "ones_row", "ones_col", "causal"):
        inp[n] = c[n].astype(np.float32)
    inp["tkTpad"] = d0["tkTpad"]
    inp["Wq"] = d0["Wq"]; inp["Wk"] = d0["Wk"]; inp["Wv"] = d0["Wv"]
    inp["Wo"] = d0["Wo"]
    for n in ("bq", "bk", "bv", "bo"):
        inp[n] = d0["b" + n[1:]]
    inp["bv_row"] = np.ascontiguousarray(w["L0_attn_bv"].reshape(1, cfg.D))
    inp["tv"] = d0["tv"]; inp["tvT"] = d0["tvT"]
    inp["wgT"] = d0["wgT"]; inp["wuT"] = d0["wuT"]; inp["wdT"] = d0["wdT"]
    inp["Wkv"] = d1["Wkv"]; inp["gkv"] = d1["gkv"]
    inp["Wkdec"] = d1["Wkdec"]; inp["Wvdec"] = d1["Wvdec"]
    inp["Wq1"] = d1["Wq"]; inp["Wkrope"] = d1["Wkrope"]
    inp["Wqaug"] = d1["Wqaug"]; inp["Wo1"] = d1["Wo"]
    inp["bwT"] = d1["bwT"]; inp["bb"] = d1["bb"]
    inp["lA"] = d1["lA"]; inp["lBpad"] = d1["lBpad"]
    inp["rwT"] = d1["rwT"]; inp["rb"] = d1["rb"]
    return inp


_NC_CACHE = {}


def _build_program():
    if "nc" not in _NC_CACHE:
        cfg = Cfg()
        b = B(cfg)
        nc = b.build(layers=(0, 1), nseq=cfg.B_local)
        _NC_CACHE["nc"] = nc
        _NC_CACHE["cfg"] = cfg
    return _NC_CACHE["nc"], _NC_CACHE["cfg"]


def kernel(**inputs):
    nc, cfg = _build_program()
    x = np.asarray(inputs["L0_attn_x"])
    ncore = 8
    shared = host_inputs(cfg, x[0:cfg.B_local], inputs)
    in_maps = []
    for c in range(ncore):
        m = dict(shared)
        m["x"] = np.ascontiguousarray(
            x[c * cfg.B_local:(c + 1) * cfg.B_local].transpose(0, 2, 1))
        in_maps.append(m)
    res = run_bass_kernel_spmd(nc, in_maps, core_ids=list(range(ncore)))
    outs = []
    for c in range(ncore):
        y = np.asarray(res.results[c]["y"])
        outs.append(np.ascontiguousarray(y.transpose(0, 2, 1)))
    return np.concatenate(outs, 0).astype(np.float32)


# version 21
# speedup vs baseline: 1.1326x; 1.1326x over previous
import sys
sys.path.insert(0, "/opt/trn_rl_repo")
import numpy as np
from dataclasses import dataclass
from contextlib import ExitStack

import concourse.bass as bass
import concourse.mybir as mybir
from concourse.bass_utils import run_bass_kernel_spmd


"""Tiny explicit-semaphore scheduling framework for raw Bass.

Engines: 'pe', 'act', 'dve', 'pool', 'sp' ('sp' = sync HWDGE queue,
'pool' = gpsimd: dma issue + memset etc).

DMA completions within one queue/semaphore are UNORDERED, so each logical
DMA stream gets its own channel (= own semaphore); consumers wait for the
channel's cumulative count snapshotted at consumer-record time. Emitters
must keep each channel either (a) internally chain-serialized via deps, or
(b) grouped with full-group waits and group boundaries ordered through
consumer dependencies.
"""

ENGINES = ("pe", "act", "dve", "pool", "sp")


@dataclass(frozen=True)
class Tok:
    eng: str
    idx: int


class FW:
    def __init__(self, nc):
        self.nc = nc
        self.ops = {e: [] for e in ENGINES}
        self.nops = {e: 0 for e in ENGINES}
        self.ndma = {}
        self.chq = {}
        self.waited = set()
        self._finish = ()

    def _snap_deps(self, deps):
        out = []
        for d in deps:
            if d.eng.startswith("dmach_"):
                ch = d.eng[6:]
                out.append(Tok(d.eng, self.ndma[ch]))
            else:
                out.append(d)
        return tuple(out)

    def op(self, eng, fn, deps=()):
        deps = self._snap_deps(deps)
        self.nops[eng] += 1
        self.ops[eng].append((fn, deps, None, self.nops[eng]))
        t = Tok(eng, self.nops[eng])
        for d in deps:
            self.waited.add(d)
        return t

    def dma(self, q, ch, fn, deps=()):
        deps = self._snap_deps(deps)
        self.ndma[ch] = self.ndma.get(ch, 0) + 1
        assert self.chq.setdefault(ch, q) == q, f"channel {ch} used on two queues"
        self.ops[q].append((fn, deps, ch, None))
        for d in deps:
            self.waited.add(d)
        return Tok("dmach_" + ch, self.ndma[ch])

    def last(self, eng):
        n = self.nops[eng]
        return Tok(eng, n) if n else None

    def set_finish(self, toks):
        self._finish = tuple(self._snap_deps(toks))
        for t in self._finish:
            self.waited.add(t)

    def finalize(self, block, ctx):
        nc = self.nc
        sems = {}
        for e in ENGINES:
            sems[e] = ctx.enter_context(nc.semaphore("s_" + e))
        for ch in self.ndma:
            sems["dmach_" + ch] = ctx.enter_context(nc.semaphore("sd_" + ch))

        inc_idx = {e: set() for e in ENGINES}
        for t in self.waited:
            if t.eng in ENGINES:
                inc_idx[t.eng].add(t.idx)
        inc_val = {}
        for e in ENGINES:
            c = 0
            vals = {}
            for i in range(1, self.nops[e] + 1):
                if i in inc_idx[e]:
                    c += 1
                vals[i] = c
            inc_val[e] = vals

        def tok_sem_val(t):
            if t.eng.startswith("dmach_"):
                return sems[t.eng], 16 * t.idx
            return sems[t.eng], inc_val[t.eng][t.idx]

        def emit_stream(e, engobj, tail=()):
            watermark = {}
            for (fn, deps, ch, cidx) in self.ops[e]:
                for d in deps:
                    sem, val = tok_sem_val(d)
                    if watermark.get(sem.name, -1) < val:
                        engobj.wait_ge(sem, val)
                        watermark[sem.name] = val
                ins = fn(engobj)
                if ch is not None:
                    ins.then_inc(sems["dmach_" + ch], 16)
                elif cidx in inc_idx[e]:
                    ins.then_inc(sems[e], 1)
            for t in tail:
                sem, val = tok_sem_val(t)
                if watermark.get(sem.name, -1) < val:
                    engobj.wait_ge(sem, val)
                    watermark[sem.name] = val

        @block.tensor
        def _(engobj):
            emit_stream("pe", engobj)

        @block.scalar
        def _(engobj):
            emit_stream("act", engobj)

        @block.vector
        def _(engobj):
            emit_stream("dve", engobj)

        @block.gpsimd
        def _(engobj):
            emit_stream("pool", engobj, tail=self._finish)

        @block.sync
        def _(engobj):
            emit_stream("sp", engobj)


"""Megakernel builder for mk_prb_d2 (2-layer parallel-residual transformer).

Layout conventions (per core):
 - residual stream x: feature-major fp32, SBUF tile xr [128, DC*S]
   (DC = D/128 chunks side by side; chunk c at cols [c*S, (c+1)*S)).
 - activations for matmuls: bf16, feature-major.
 - all matmul contractions padded to K=128 partitions (zero-pad constants).
 - per (head, qtile) attention in [d, q] output form.

Sequences: B_local sequences processed sequentially per core.
"""


F32 = mybir.dt.float32
BF16 = mybir.dt.bfloat16
I32 = mybir.dt.int32
AF = mybir.ActivationFunctionType
ALU = mybir.AluOpType
AX = mybir.AxisListType


class Cfg:
    def __init__(self, S=2048, D=1024, NH=16, DH=64, B_local=2,
                 FFN_H=3072, KV_RANK=256, Q_RANK=512, ROPE_DIM=32,
                 MAXREL=4, NT=512):
        self.S, self.D, self.NH, self.DH = S, D, NH, DH
        self.B_local = B_local
        self.FFN_H = FFN_H
        self.KV_RANK, self.Q_RANK, self.ROPE_DIM = KV_RANK, Q_RANK, ROPE_DIM
        self.MAXREL = MAXREL
        self.NT = NT
        self.DC = D // 128
        self.NTT = S // NT
        self.NQT = S // 128
        self.HC = FFN_H // 128
        self.NREL = 2 * MAXREL + 1
        self.W = 17 * 8
        self.RC = ROPE_DIM
        self.QG = NT // 128



def host_constants(cfg):
    c = {}
    W = cfg.W
    ms = np.zeros((128, 8 * W), np.float32)
    for j in range(1, 9):
        for p in range(128):
            for cpp in range(W):
                if cpp - p >= j:
                    ms[p, (j - 1) * W + cpp] = 1.0
    c["Mstep"] = ms
    WP = 131
    mp = np.zeros((128, WP), np.float32)
    for p in range(128):
        mp[p, : min(p + 4, WP)] = 1.0
    c["Mpre"] = mp
    dd = np.zeros((128, 7 * WP), np.float32)
    for oi, o in enumerate(range(-3, 4)):
        for p in range(128):
            j = p + o
            if 0 <= j < WP:
                dd[p, oi * WP + j] = 1.0
    c["Ddiag"] = dd
    Pc = np.zeros((8, 9), np.float32)
    for r in range(7):
        Pc[r, r + 1] = 1.0
        Pc[r, 0] = -1.0
    Pc[7, 8] = 1.0
    Pc[7, 0] = -1.0
    c["PcT"] = np.ascontiguousarray(Pc.T)
    ie = np.zeros((128, 64), np.float32)
    ie[:64] = np.eye(64)
    io = np.zeros((128, 64), np.float32)
    io[64:] = np.eye(64)
    c["Ipad_e"], c["Ipad_o"] = ie, io
    c["ones_row"] = np.ones((1, 128), np.float32)
    oe = np.zeros((128, 1), np.float32); oe[:64] = 1.0
    oo = np.zeros((128, 1), np.float32); oo[64:] = 1.0
    c["ones_e"], c["ones_o"] = oe, oo
    c["ones_col"] = np.ones((128, 1), np.float32)
    cm = np.zeros((128, 128), np.float32)
    for p in range(128):
        cm[p, p + 1:] = -3e38 / 4
    c["causal"] = cm
    rd = cfg.ROPE_DIM
    inv = 1.0 / 10000.0 ** (np.arange(0, rd, 2, dtype=np.float32) / rd)
    t = np.arange(cfg.S, dtype=np.float32)
    fr = np.outer(t, inv)
    emb = np.concatenate([fr, fr], -1)
    cos, sin = np.cos(emb), np.sin(emb)
    nrep = 128 // rd
    c["cos_rep"] = np.ascontiguousarray(np.tile(cos.T, (nrep, 1)))
    c["sin_rep"] = np.ascontiguousarray(np.tile(sin.T, (nrep, 1)))
    R = np.zeros((rd, rd), np.float32)
    h = rd // 2
    for i in range(h):
        R[i, i + h] = -1.0
        R[i + h, i] = 1.0
    c["R32T"] = np.ascontiguousarray(R.T)
    rp = np.zeros((128, rd), np.float32)
    rp[64:64 + rd] = R.T
    c["R32Tp"] = rp
    Rb = np.zeros((128, 128), np.float32)
    for b in range(nrep):
        Rb[b * rd:(b + 1) * rd, b * rd:(b + 1) * rd] = R
    c["R128T"] = np.ascontiguousarray(Rb.T)
    s3 = np.zeros((128, 32), np.float32)
    s3[96:128] = np.eye(32)
    c["SEL3"] = s3
    se = np.zeros((8, 64), np.float32)
    sr = np.zeros((8, 64), np.float32)
    for er in range(64):
        se[er // 8, er] = 1.0
        sr[er % 8, er] = 1.0
    c["SELe"], c["SELr"] = se, sr
    c["I128"] = np.eye(128, dtype=np.float32)
    c["ones_rowh"] = np.ones((1, 128), np.float32)
    c["ones_colh"] = np.ones((128, 1), np.float32)
    oq = np.zeros((128, 4), np.float32)
    for i in range(4):
        oq[32 * i:32 * (i + 1), i] = 1.0
    c["ones_q"] = oq
    sq_ = np.zeros((32, 4 * 128), np.float32)
    for qq in range(4):
        for i in range(32):
            sq_[i, qq * 128 + 32 * qq + i] = 1.0
    c["SELq"] = sq_
    return c


def tile_w(W, mcols, kchunks=None):
    K, M = W.shape
    kchunks = kchunks or K // 128
    mt = M // mcols
    Wr = W.reshape(kchunks, 128, mt, mcols)
    out = np.ascontiguousarray(Wr.transpose(2, 1, 0, 3).reshape(mt * 128,
                                                                kchunks * mcols))
    return out


def host_weights_l0(cfg, w):
    d = {}
    d["Wq"] = tile_w(w["L0_attn_Wq"], cfg.DH)
    d["Wk"] = tile_w(w["L0_attn_Wk"], 128)
    d["Wv"] = w["L0_attn_Wv"]
    d["Wo"] = tile_w(w["L0_attn_Wo"], 128)
    for n in ("q", "k", "v", "o"):
        b = w["L0_attn_b" + n]
        d["b" + n] = np.ascontiguousarray(b.reshape(cfg.DC, 128).T)
    tkT = np.ascontiguousarray(w["L0_attn_table_k"].T)
    nrel = tkT.shape[1]
    tkp = np.zeros((128, 2 * nrel), np.float32)
    dh = tkT.shape[0]
    tkp[:dh, :nrel] = tkT
    tkp[dh:2 * dh, nrel:] = tkT
    d["tkTpad"] = tkp
    d["tv"] = w["L0_attn_table_v"]
    d["tvT"] = np.ascontiguousarray(w["L0_attn_table_v"].T)
    d["wgT"] = tile_w(np.ascontiguousarray(
        w["L0_ffn_w_gate"].T.astype(np.float32)), 128)
    d["wuT"] = tile_w(np.ascontiguousarray(
        w["L0_ffn_w_up"].T.astype(np.float32)), 128)
    d["wdT"] = np.ascontiguousarray(
        w["L0_ffn_w_down"].T.astype(np.float32))
    return d


def host_weights_l1(cfg, w):
    d = {}
    d["Wkv"] = tile_w(w["L1_attn_W_kv"], 128)
    d["gkv"] = np.ascontiguousarray(
        w["L1_attn_g_kv"].reshape(cfg.KV_RANK // 128, 128).T)
    d["Wkdec"] = tile_w(w["L1_attn_W_kdec"], 128)
    d["Wvdec"] = w["L1_attn_W_vdec"]
    d["Wq"] = tile_w(w["L1_attn_W_q"], 128)
    d["Wkrope"] = tile_w(w["L1_attn_W_krope"], 128)
    qd = w["L1_attn_W_qdec"].reshape(cfg.Q_RANK, cfg.NH, cfg.DH)
    qr = w["L1_attn_W_qrope"].reshape(cfg.Q_RANK, cfg.NH, cfg.ROPE_DIM)
    d["Wqaug"] = tile_w(np.ascontiguousarray(
        np.concatenate([qd, qr], axis=2).reshape(
            cfg.Q_RANK, cfg.NH * (cfg.DH + cfg.ROPE_DIM))),
        cfg.DH + cfg.ROPE_DIM)
    d["Wo"] = tile_w(w["L1_attn_W_o"], 128)
    d["bwT"] = tile_w(np.ascontiguousarray(w["L1_ffn_base_w"].T), 128)
    d["bb"] = np.ascontiguousarray(w["L1_ffn_base_b"].reshape(cfg.DC, 128).T)
    d["lA"] = w["L1_ffn_lora_A"]
    lb = w["L1_ffn_lora_B"].reshape(64, cfg.D)
    d["lBpad"] = np.ascontiguousarray(
        np.concatenate([lb, np.zeros((64, cfg.D), np.float32)], axis=0))
    d["rwT"] = np.ascontiguousarray(w["L1_ffn_router_w"].T)
    d["rb"] = w["L1_ffn_router_b"].reshape(8, 1)
    return d


"""Device program builder for the mk_prb_d2 megakernel.

Conventions:
 - residual xr [128, DC*S] fp32 f-major (chunk c at cols [c*S,(c+1)*S)).
 - Kf f-major bf16 (2 heads/chunk); KR f-major bf16 (4 rope-heads/chunk).
 - Vf TOKEN-major bf16: [128, (S/128)*D]; token-chunk tc at cols [tc*D,(tc+1)*D).
 - xn is never stored full: recomputed from mrow/rrow where needed.
 - all accumulating matmuls K=128 (zero-padded operands).
 - attention: [d,q]-form attnV, bound-bias softmax, diagR-normalized pT.
"""


F32 = mybir.dt.float32
BF16 = mybir.dt.bfloat16
I32 = mybir.dt.int32
AF = mybir.ActivationFunctionType
ALU = mybir.AluOpType
AXX = mybir.AxisListType


class B:
    def __init__(self, cfg: Cfg, debug_dumps=()):
        self.cfg = cfg
        self.debug = set(debug_dumps)
        self.nc = bass.Bass("TRN2", target_bir_lowering=False)
        self.fw = FW(self.nc)
        self.dram = {}
        self.sb = {}
        self.ps = []
        self.pslast = [None] * 8
        self._dbg_toks = []

    def dram_in(self, name, shape, dtype=F32):
        self.dram[name] = self.nc.dram_tensor(name, list(shape), dtype,
                                              kind="ExternalInput").ap()

    def dram_out(self, name, shape, dtype=F32):
        self.dram[name] = self.nc.dram_tensor(name, list(shape), dtype,
                                              kind="ExternalOutput").ap()

    _name_ctr = 0

    def sbuf(self, ctx, name, shape, dtype):
        B._name_ctr += 1
        t = ctx.enter_context(self.nc.sbuf_tensor(f"{name}__{B._name_ctr}",
                                                  list(shape), dtype))
        self.sb[name] = t
        return t

    def op(self, eng, fn, deps=()):
        return self.fw.op(eng, fn, [d for d in deps if d is not None])

    def dma(self, q, ch, fn, deps=()):
        return self.fw.dma(q, ch, fn, [d for d in deps if d is not None])

    def mm(self, bank, out_slice, lhsT, rhs, start, stop, deps=()):
        d = [x for x in deps if x is not None]
        if start and self.pslast[bank] is not None:
            d.append(self.pslast[bank])
        return self.fw.op("pe", lambda e: e.matmul(out=out_slice, lhsT=lhsT,
                                                   rhs=rhs, start=start, stop=stop), d)

    def free_bank(self, bank, tok):
        self.pslast[bank] = tok

    def act(self, out, in_, func=AF.Copy, bias=None, scale=None, accum=None, deps=()):
        kw = {}
        if bias is not None:
            kw["bias"] = bias
        if scale is not None:
            kw["scale"] = scale
        if accum is not None:
            kw["accum_out"] = accum
        return self.op("act", lambda e: e.activation(out=out, in_=in_, func=func, **kw),
                       deps)

    def dbg(self, name, src_ap, deps, shape, dtype=F32):
        if name not in self.debug:
            return
        key = "dbg_" + name
        if key not in self.dram:
            self.dram_out(key, shape, dtype)
        t = self.dma("pool", "dbg", lambda e: e.dma_start(out=self.dram[key], in_=src_ap), deps)
        self._dbg_toks.append(t)

    def bar(self):
        d = []
        for e in ("pe", "act", "dve", "pool", "sp"):
            t = self.fw.last(e)
            if t:
                d.append(t)
        for ch, n in self.fw.ndma.items():
            if n:
                d.append(Tok("dmach_" + ch, n))
        return d

    def declare(self):
        cfg = self.cfg
        S, D, DC = cfg.S, cfg.D, cfg.DC
        self.dram_in("x", [cfg.B_local, D, S])
        self.dram_out("y", [cfg.B_local, D, S])
        for n, sh in [("Mstep", (128, 8 * cfg.W)), ("Mpre", (128, 131)),
                      ("Ddiag", (128, 7 * 131)),
                      ("ones_e", (128, 1)), ("ones_o", (128, 1)),
                      ("ones_q", (128, 4)), ("ones_colh", (128, 1)),
                      ("cos_rep", (128, S)), ("sin_rep", (128, S)),
                      ("R32T", (cfg.RC, cfg.RC)), ("R32Tp", (128, cfg.RC)), ("R128T", (128, 128)),
                      ("SELq", (32, 4 * 128)), ("SELe", (8, 64)), ("SELr", (8, 64)),
                      ("I128", (128, 128)), ("ones_rowh", (1, 128))]:
            self.dram_in(n, sh, BF16)
        for n, sh in [("PcT", (9, 8)), ("ones_row", (1, 128)),
                      ("ones_col", (128, 1)), ("causal", (128, 128))]:
            self.dram_in(n, sh)
        self.dram_in("tkTpad", (128, 2 * cfg.NREL))
        self.dram["dscr"] = self.nc.dram_tensor(
            "dscr", [128, DC * S], BF16, kind="Internal").ap()
        self.dram_in("Wq", [cfg.NH * 128, DC * cfg.DH])
        self.dram_in("Wk", [DC * 128, DC * 128])
        self.dram_in("Wv", [D, D]); self.dram_in("Wo", [DC * 128, DC * 128])
        for n in ("bq", "bk", "bv", "bo"):
            self.dram_in(n, [128, DC])
        self.dram_in("bv_row", [1, D])
        self.dram_in("tv", [cfg.NREL, cfg.DH]); self.dram_in("tvT", [cfg.DH, cfg.NREL])
        self.dram_in("wgT", [cfg.HC * 128, DC * 128])
        self.dram_in("wuT", [cfg.HC * 128, DC * 128])
        self.dram_in("wdT", [cfg.FFN_H, D])
        KV, QR = cfg.KV_RANK, cfg.Q_RANK
        self.dram_in("Wkv", [(KV // 128) * 128, DC * 128])
        self.dram_in("gkv", [128, max(1, KV // 128)])
        self.dram_in("Wkdec", [DC * 128, (KV // 128) * 128])
        self.dram_in("Wvdec", [KV, D])
        self.dram_in("Wq1", [(QR // 128) * 128, DC * 128])
        self.dram_in("Wkrope", [(cfg.NH * cfg.RC // 128) * 128, DC * 128])
        self.dram_in("Wqaug", [cfg.NH * 128, (QR // 128) * (cfg.DH + cfg.RC)])
        self.dram_in("Wo1", [DC * 128, DC * 128])
        self.dram_in("bwT", [DC * 128, DC * 128]); self.dram_in("bb", [128, DC])
        self.dram_in("lA", [D, 8]); self.dram_in("lBpad", [128, D])
        self.dram_in("rwT", [D, 8]); self.dram_in("rb", [8, 1])

    def alloc_persistent(self, ctx):
        cfg = self.cfg
        D, DC = cfg.D, cfg.DC
        sb = lambda n, s, d: self.sbuf(ctx, n, s, d)
        sb("xr", [128, DC * cfg.S], F32)
        sb("cones_e", [128, 1], BF16); sb("cones_o", [128, 1], BF16)
        sb("cones_q", [128, 4], BF16); sb("cones_colh", [128, 1], BF16)
        sb("cR32T", [cfg.RC, cfg.RC], BF16); sb("cR32Tp", [128, cfg.RC], BF16)
        sb("cR128T", [128, 128], BF16)
        sb("cSELe", [8, 64], BF16); sb("cSELr", [8, 64], BF16)
        sb("cI128", [128, 128], BF16); sb("cones_rowh", [1, 128], BF16)
        sb("cones_row", [1, 128], F32); sb("cones_col", [128, 1], F32)
        sb("cPcT", [cfg.NREL, 8], F32)
        sb("ctv", [cfg.NREL, cfg.DH], F32); sb("ctvT", [cfg.DH, cfg.NREL], F32)
        sb("ctkT", [128, 2 * cfg.NREL], BF16)
        sb("biasT", [128, 8 * DC], F32)
        sb("cgkv", [128, max(1, cfg.KV_RANK // 128)], F32)
        sb("crb", [8, 1], F32)
        sb("lAh", [128, DC * 8], BF16); sb("rwh", [128, DC * 8], BF16)
        sb("QRt_0", [128, 16], F32)
        sb("QRt_1", [128, 16], F32)
        sb("Ucols", [128, 8], F32)
        sb("accD", [128, 24], F32)
        sb("D8_0", [128, 8], BF16)
        sb("D8_1", [128, 8], BF16)
        sb("D8Tp0", [128, 128], BF16)
        sb("D8Tp1", [128, 128], BF16)
        sb("Scol", [128, 1], F32); sb("recipS", [128, 1], F32)
        sb("diagR0", [128, 128], BF16)
        sb("diagR1", [128, 128], BF16)
        sb("bias1_0", [128, 1], F32); sb("bias2_0", [128, 1], F32)
        sb("bias1_1", [128, 1], F32); sb("bias2_1", [128, 1], F32)
        sb("MXcol", [128, 1], F32)
        sb("TVmodp", [128, cfg.DH], BF16)
        sb("hrow", [1, 16], F32)
        sb("epsrow", [1, 2], F32)
        sb("vrow", [1, cfg.NT], F32)
        sb("srow", [1, cfg.NT], F32)
        sb("evac", [128, 2 * cfg.NT], F32)
        sb("sq", [128, 2 * cfg.NT], F32)
        sb("hg", [128, 2 * cfg.NT], BF16)

    def alloc_psum(self, ctx):
        for i in range(8):
            self.ps.append(ctx.enter_context(
                self.nc.psum_tensor(f"pb{i}", [128, 512], F32)))

    def emit_init(self):
        cfg, sb, dr = self.cfg, self.sb, self.dram
        ltoks = []

        def ld(dst_ap, src):
            t = self.dma("sp", "ic", lambda e, d=dst_ap, s=dr[src]: e.dma_start(out=d, in_=s))
            ltoks.append(t)
            return t

        for dst, src in [("cones_e", "ones_e"), ("cones_o", "ones_o"),
                         ("cones_q", "ones_q"), ("cones_colh", "ones_colh"),
                         ("cR32T", "R32T"), ("cR32Tp", "R32Tp"), ("cR128T", "R128T"),
                         ("cSELe", "SELe"), ("cSELr", "SELr"), ("cI128", "I128"),
                         ("cones_rowh", "ones_rowh"), ("cones_row", "ones_row"),
                         ("cones_col", "ones_col"),
                         ("cPcT", "PcT"), ("ctv", "tv"), ("ctvT", "tvT"),
                         ("cgkv", "gkv"), ("crb", "rb")]:
            ld(sb[dst][:, :], src)
        DC = cfg.DC
        for i, n in enumerate(("bq", "bk", "bv", "bo", "bb")):
            ld(sb["biasT"][:, i * DC:(i + 1) * DC], n)
        t1 = self.dma("sp", "ict", lambda e: e.dma_start(
            out=sb["evac"][:, 0:2 * cfg.NREL], in_=dr["tkTpad"]))
        ltoks.append(t1)
        c1 = self.act(sb["ctkT"][:, :], sb["evac"][:, 0:2 * cfg.NREL], deps=[t1])
        lr = []
        srcA = dr["lA"].rearrange("(kc r) c -> r kc c", r=128)
        dstA = sb["sq"][:, 0:DC * 8].rearrange("p (kc c) -> p kc c", c=8)
        lr.append(self.dma("sp", "icl", lambda e: e.dma_start(out=dstA, in_=srcA)))
        srcR = dr["rwT"].rearrange("(kc r) c -> r kc c", r=128)
        dstR = sb["sq"][:, 512:512 + DC * 8].rearrange("p (kc c) -> p kc c", c=8)
        lr.append(self.dma("sp", "icl", lambda e: e.dma_start(out=dstR, in_=srcR)))
        c3 = self.act(sb["lAh"][:, :], sb["sq"][:, 0:DC * 8], deps=lr)
        c4 = self.act(sb["rwh"][:, :], sb["sq"][:, 512:512 + DC * 8], deps=lr)
        c2 = c5 = c4
        m1a = self.op("dve", lambda e: e.memset(sb["D8Tp0"][:, :], 0.0))
        m1 = self.op("dve", lambda e: e.memset(sb["D8Tp1"][:, :], 0.0), [m1a])
        m4 = self.op("dve", lambda e: e.memset(sb["TVmodp"][:, :], 0.0))
        m5 = self.op("dve", lambda e: e.memset(sb["epsrow"][0:1, 0:1], 1e-5))
        m5b = self.op("dve", lambda e: e.memset(sb["epsrow"][0:1, 1:2], 1e-6), [m5])
        m6 = m1
        m7 = m1

        tm = self.mm(7, self.ps[7][0:8, 0:cfg.DH], sb["cPcT"][:, :], sb["ctv"][:, :],
                     True, True, deps=ltoks)
        evt = self.act(sb["TVmodp"][0:8, :], self.ps[7][0:8, 0:cfg.DH], deps=[tm, m4])
        self.free_bank(7, evt)
        sq1 = self.act(sb["sq"][:, 0:2 * cfg.NREL], sb["ctkT"][:, :], func=AF.Square,
                       deps=[c1, c3, c4, c5])
        mm1 = self.mm(7, self.ps[7][0:1, 0:cfg.NREL], sb["cones_col"][:, :],
                      sb["sq"][:, 0:cfg.NREL], True, True, deps=[sq1])
        r1 = self.op("dve", lambda e: e.tensor_reduce(
            out=sb["hrow"][0:1, 2:3], in_=self.ps[7][0:1, 0:cfg.NREL],
            axis=AXX.X, op=ALU.max), [mm1])
        self.free_bank(7, r1)
        r2 = self.act(sb["hrow"][0:1, 2:3], sb["hrow"][0:1, 2:3], func=AF.Sqrt,
                      deps=[r1])
        self.init_toks = ltoks + [c1, c3, c4, m1, m4, m5b, m6, m7, evt, r2]

    def load_x(self, seq, deps):
        cfg, sb = self.cfg, self.sb
        src = self.dram["x"][seq].rearrange("(c r) t -> r c t", r=128)
        dst = sb["xr"][:, :].rearrange("p (c t) -> p c t", t=cfg.S)
        return [self.dma("sp", "xl", lambda e: e.dma_start(out=dst, in_=src), deps)]

    def store_y(self, seq, deps):
        cfg, sb = self.cfg, self.sb
        dst = self.dram["y"][seq].rearrange("(c r) t -> r c t", r=128)
        src = sb["xr"][:, :].rearrange("p (c t) -> p c t", t=cfg.S)
        return [self.dma("sp", "xl", lambda e: e.dma_start(out=dst, in_=src), deps)]

    def ln_stats(self, mrow, rrow, eps_idx, deps):
        out = []
        for t in range(self.cfg.NTT):
            out += self.ln_stats_one(mrow, rrow, eps_idx, t, t * self.cfg.NT, deps)
        return out

    def ln_stats_one(self, mrow, rrow, eps_idx, t, dcol, deps):
        cfg, sb, ps = self.cfg, self.sb, self.ps
        S, DC, NT, D = cfg.S, cfg.DC, cfg.NT, cfg.D
        out = []
        if True:
            col = t * NT
            mm_m = mm_v = None
            sq_free = {}
            for c in range(DC):
                xs = sb["xr"][:, c * S + col: c * S + col + NT]
                mm_m = self.mm(0, ps[0][0:1, 0:NT], sb["cones_col"][:, :], xs,
                               c == 0, c == DC - 1, deps=deps)
                par = c % 2
                sqs = sb["sq"][:, par * NT:(par + 1) * NT]
                sqt = self.act(sqs, xs, func=AF.Square,
                               deps=deps + [sq_free.get(par)])
                mm_v = self.mm(1, ps[1][0:1, 0:NT], sb["cones_col"][:, :], sqs,
                               c == 0, c == DC - 1, deps=[sqt])
                sq_free[par] = mm_v
            mt = self.act(mrow[0:1, dcol:dcol + NT], ps[0][0:1, 0:NT],
                          scale=1.0 / D, deps=[mm_m])
            m2t = self.act(sb["srow"][0:1, 0:NT], ps[0][0:1, 0:NT], func=AF.Square,
                           scale=1.0 / D, deps=[mm_m])
            self.free_bank(0, m2t)
            vt = self.op("dve", lambda e: e.scalar_tensor_tensor(
                out=sb["vrow"][0:1, 0:NT], in0=ps[1][0:1, 0:NT], scalar=1.0 / D,
                in1=sb["srow"][0:1, 0:NT], op0=ALU.mult, op1=ALU.subtract),
                [mm_v, m2t])
            self.free_bank(1, vt)
            st = self.act(sb["srow"][0:1, 0:NT], sb["vrow"][0:1, 0:NT], func=AF.Sqrt,
                          bias=sb["epsrow"][0:1, eps_idx:eps_idx + 1], deps=[vt])
            rt0 = self.op("dve", lambda e: e.reciprocal(
                out=sb["vrow"][0:1, 0:NT], in_=sb["srow"][0:1, 0:NT]), [st])
            rt = self.op("dve", lambda e, dcol=dcol: e.tensor_copy(
                out=rrow[0:1, dcol:dcol + NT], in_=sb["vrow"][0:1, 0:NT]), [rt0])
            out += [mt, rt]
        return out

    def xn_tile(self, xn_of_c, mrow, rrow, t, deps, bm=2, br=3, stats_col=None):
        cfg, sb, ps = self.cfg, self.sb, self.ps
        S, DC, NT = cfg.S, cfg.DC, cfg.NT
        col = t * NT
        scol = col if stats_col is None else stats_col
        ones = sb["cones_row"] if str(mrow.dtype).endswith("float32") else sb["cones_rowh"]
        bmt = self.mm(bm, ps[bm][:, 0:NT], ones[:, :],
                      mrow[0:1, scol:scol + NT], True, True, deps=deps)
        brt = self.mm(br, ps[br][:, 0:NT], ones[:, :],
                      rrow[0:1, scol:scol + NT], True, True, deps=deps)
        toks = []
        for c in range(DC):
            par = c % 2
            ev = sb["evac"][:, par * NT:(par + 1) * NT]
            xs = sb["xr"][:, c * S + col: c * S + col + NT]
            o1 = self.op("dve", lambda e, ev=ev, xs=xs: e.tensor_sub(
                out=ev, in0=xs, in1=ps[bm][:, 0:NT]),
                [bmt, self.tok_evac_free[par]] + deps)
            o2 = self.op("dve", lambda e, ev=ev, c=c: e.tensor_mul(
                out=xn_of_c(c), in0=ev, in1=ps[br][:, 0:NT]), [o1, brt])
            self.tok_evac_free[par] = o2
            toks.append(o2)
        self.free_bank(bm, toks[-1])
        self.free_bank(br, toks[-1])
        return toks

    def stage_cols(self, wname, mcol0, mcols, kchunks, free_dep, sq_off=0, ch=None):
        sb = self.sb
        ch = ch or ("w_" + wname)
        fd = list(free_dep) if isinstance(free_dep, list) else [free_dep]
        fd += [self.tok_hg_free, self.tok_hg_free2, self.tok_hgc_free]
        m = mcol0 // mcols
        src = self.dram[wname][m * 128:(m + 1) * 128, 0:kchunks * mcols]
        dst = sb["sq"][:, sq_off:sq_off + kchunks * mcols]
        dt = self.dma("sp", ch, lambda e: e.dma_start(out=dst, in_=src), fd)
        cst = self.act(sb["hg"][:, sq_off:sq_off + kchunks * mcols],
                       sb["sq"][:, sq_off:sq_off + kchunks * mcols],
                       deps=[dt] + fd)
        return cst

    def v_tm_tile(self, wrows_name, xng, t, kchunks, n_banks_cols, bias_row,
                  wstage_f, wstage_h, wfree, src_toks, dst_col0):
        cfg, sb = self.cfg, self.sb
        D, NT = cfg.D, cfg.NT
        ntc = NT // 128
        nn = n_banks_cols
        mms = {}
        cstp = wfree[0]
        last_mm = wfree[1]
        for kc in range(kchunks):
            dt = self.dma("sp", "w_" + wrows_name, lambda e, kc=kc: e.dma_start(
                out=wstage_f[:, 0:D],
                in_=self.dram[wrows_name][kc * 128:(kc + 1) * 128, :]), [cstp])
            cst = self.act(wstage_h[:, 0:D], wstage_f[:, 0:D],
                           deps=[dt, last_mm])
            cstp = cst
            for tc in range(ntc):
                for n2 in range(nn):
                    bank = tc * nn + n2
                    mms[(tc, n2)] = self.mm(
                        bank, self.ps[bank][:, 0:512],
                        xng[:, kc * NT + tc * 128: kc * NT + (tc + 1) * 128],
                        wstage_h[:, n2 * 512:(n2 + 1) * 512],
                        kc == 0, False, deps=[cst] + src_toks)
                    last_mm = mms[(tc, n2)]
        wfree[0] = cstp
        evs = []
        for tc in range(ntc):
            for n2 in range(nn):
                bank = tc * nn + n2
                bmm = self.mm(bank, self.ps[bank][:, 0:512],
                              sb["cones_rowh"][0:1, 0:128],
                              bias_row[0:1, n2 * 512:(n2 + 1) * 512],
                              False, True, deps=[mms[(tc, n2)]]) if bias_row is not None \
                    else self.mm(bank, self.ps[bank][:, 0:512],
                                 sb["cones_rowh"][0:1, 0:1].broadcast_to((1, 1)),
                                 None, False, True)
                ev = self.act(sb["Vf"][:, dst_col0 + (t * ntc + tc) * D + n2 * 512:
                                       dst_col0 + (t * ntc + tc) * D + (n2 + 1) * 512],
                              self.ps[bank][:, 0:512], deps=[bmm])
                self.free_bank(bank, ev)
                evs.append(ev)
        wfree[1] = evs[-1]
        return evs

    def l0_phase_a(self, mrow, rrow, xng, deps):
        cfg, sb = self.cfg, self.sb
        S, D, DC, NT = cfg.S, cfg.D, cfg.DC, cfg.NT
        ktoks, vtoks = [], []
        kfree = [None, None]
        prev_stats = list(deps)
        for t in range(cfg.NTT):
            stat_toks = self.ln_stats_one(mrow, rrow, 0, t, 0, deps + prev_stats)
            if kfree[0] is None:
                kfree = [stat_toks[-1], stat_toks[-1]]
            xnt = self.xn_tile(lambda c: xng[:, c * NT:(c + 1) * NT],
                               mrow, rrow, t, stat_toks, stats_col=0)
            prev_stats = stat_toks + xnt
            vtoks += self.v_tm_tile("Wv", xng, t, DC, D // 512, sb["bvh"],
                                    sb["sq"], sb["hg"], kfree, xnt, 0)
            for m in range(DC):
                cst = self.stage_cols("Wk", m * 128, 128, DC, [kfree[0], kfree[1]])
                kfree[0] = cst
                bk = m % 2
                mmk = None
                for kc in range(DC):
                    mmk = self.mm(bk, self.ps[bk][:, 0:NT],
                                  sb["hg"][:, kc * 128:(kc + 1) * 128],
                                  xng[:, kc * NT:(kc + 1) * NT],
                                  kc == 0, kc == DC - 1, deps=[cst] + xnt)
                ek = self.act(sb["Kf"][:, m * S + t * NT: m * S + (t + 1) * NT],
                              self.ps[bk][:, 0:NT], func=AF.Identity,
                              bias=sb["biasT"][:, DC + m:DC + m + 1], deps=[mmk])
                self.free_bank(bk, ek)
                kfree[0] = cst
                kfree[1] = ek
                ktoks.append(ek)
        return prev_stats, ktoks + vtoks

    def l0_attn_head(self, h, g, xng, xn_toks, kv_toks, wq_free):
        cfg, sb, ps = self.cfg, self.sb, self.ps
        Kf = sb["Kf"]
        wqs = sb["wqs"]
        wqh = sb["wqh"]
        qrel_s = sb["qrel_s"]
        Qg = sb["Qg"]
        S, D, DC, NT, DH = cfg.S, cfg.D, cfg.DC, cfg.NT, cfg.DH
        NREL, W = cfg.NREL, cfg.W
        par, ch = h % 2, h // 2
        fw = self.fw
        dts = [self.dma("sp", "w_Wq", lambda e: e.dma_start(
            out=wqs[:, 0:DC * DH],
            in_=self.dram["Wq"][h * 128:(h + 1) * 128, :]),
            [wq_free[0]])]
        cst = self.act(wqh[:, 0:DC * DH], wqs[:, 0:DC * DH],
                       deps=dts + [wq_free[1]])
        wq_free[0] = cst
        ms = self.op("dve", lambda e: e.memset(
            Qg[(1 - par) * 64:(1 - par) * 64 + 64, :], 0.0),
            [self.tok_qg_free])
        mmq = None
        for kc in range(DC):
            mmq = self.mm(4, ps[4][par * 64:par * 64 + 64, 0:NT],
                          wqh[:, kc * DH:(kc + 1) * DH],
                          xng[:, kc * NT:(kc + 1) * NT],
                          kc == 0, kc == DC - 1, deps=[cst] + xn_toks)
        eq = self.act(Qg[par * 64:par * 64 + 64, :],
                      ps[4][par * 64:par * 64 + 64, 0:NT], func=AF.Identity,
                      bias=sb["biasT"][par * 64:par * 64 + 64, ch:ch + 1],
                      deps=[mmq, ms, self.tok_qg_free])
        self.free_bank(4, eq)
        wq_free[1] = mmq
        mmr = self.mm(4, ps[4][0:NREL, 0:NT],
                      sb["ctkT"][:, par * NREL:(par + 1) * NREL],
                      Qg[:, 0:NT], True, True, deps=[eq])
        erl = self.act(qrel_s[0:NREL, 0:NT], ps[4][0:NREL, 0:NT], deps=[mmr])
        self.free_bank(4, erl)
        sqq = self.act(sb["hg"][:, 0:NT], Qg[:, :], func=AF.Square,
                       deps=[eq, self.tok_hg_free])
        mmn = self.mm(4, ps[4][0:1, 0:NT], sb["cones_colh"][:, 0:1],
                      sb["hg"][:, 0:NT], True, True, deps=[sqq])
        self.tok_hg_free = mmn
        sqt = self.act(sb["srow"][0:1, 0:NT], ps[4][0:1, 0:NT], func=AF.Sqrt,
                       deps=[mmn])
        self.free_bank(4, sqt)
        qm = self.op("dve", lambda e: e.tensor_reduce(
            out=sb["hrow"][0:1, 0:1], in_=sb["srow"][0:1, 0:NT],
            axis=AXX.X, op=ALU.max), [sqt, self.tok_mx_free])
        if g == 0:
            kms = []
            for tq in range(cfg.NTT):
                sqk = self.act(sb["hg"][:, 0:NT],
                               Kf[:, ch * S + tq * NT: ch * S + (tq + 1) * NT],
                               func=AF.Square, deps=kv_toks + [self.tok_hg_free])
                mmk = self.mm(4, ps[4][0:1, 0:NT],
                              sb["cones_e"][:, 0:1] if par == 0 else sb["cones_o"][:, 0:1],
                              sb["hg"][:, 0:NT], True, True, deps=[sqk])
                self.tok_hg_free = mmk
                km = self.op("dve", lambda e, tq=tq: e.tensor_reduce(
                    out=sb["hrow"][0:1, 8 + tq:9 + tq], in_=ps[4][0:1, 0:NT],
                    axis=AXX.X, op=ALU.max), [mmk])
                self.free_bank(4, km)
                kms.append(km)
            c1 = self.op("dve", lambda e: e.tensor_reduce(
                out=sb["hrow"][0:1, 1:2], in_=sb["hrow"][0:1, 8:8 + cfg.NTT],
                axis=AXX.X, op=ALU.max), kms)
            c2 = self.act(sb["hrow"][0:1, 1:2], sb["hrow"][0:1, 1:2], func=AF.Sqrt,
                          deps=[c1])
            self.tok_kmax = c2
        mx1 = self.op("dve", lambda e: e.tensor_add(
            out=sb["hrow"][0:1, 3:4], in0=sb["hrow"][0:1, 1:2],
            in1=sb["hrow"][0:1, 2:3]), [self.tok_kmax, qm])
        mx2 = self.op("dve", lambda e: e.tensor_mul(
            out=sb["hrow"][0:1, 3:4], in0=sb["hrow"][0:1, 3:4],
            in1=sb["hrow"][0:1, 0:1]), [mx1])
        mxb = self.mm(4, ps[4][:, 0:1], sb["cones_row"][:, :],
                      sb["hrow"][0:1, 3:4], True, True, deps=[mx2])
        mxe = self.act(sb["MXcol"][:, 0:1], ps[4][:, 0:1], deps=[mxb])
        self.free_bank(4, mxe)
        self.tok_mx_free = mxe
        return eq, erl, mxe

    def l0_attn_fa_qr(self, h, g, qt, eq, erl, mxe, kv_toks, pp):
        cfg, sb, ps = self.cfg, self.sb, self.ps
        cMstep = sb["cMstep"]
        qrel_s = sb["qrel_s"]
        bandp = sb["bandp_" + pp]
        S, NREL, W = cfg.S, cfg.NREL, cfg.W
        qti = g * cfg.QG + qt
        q0 = qti * 128
        nk = S // 512
        tq = self.mm(4, ps[4][:, 0:NREL], qrel_s[0:NREL, qt * 128:(qt + 1) * 128],
                     sb["cI128"][0:NREL, 0:NREL], True, True, deps=[erl])
        eqr = self.act(sb["QRt_" + pp][:, 0:NREL], ps[4][:, 0:NREL],
                       deps=[tq, self.tok_qrt_free[int(pp)]])
        self.free_bank(4, eqr)
        uc = self.op("dve", lambda e: e.tensor_scalar(
            out=sb["Ucols"][:, 0:8], in0=sb["QRt_" + pp][:, 1:NREL],
            scalar1=sb["QRt_" + pp][:, 0:1], scalar2=None, op0=ALU.subtract),
            [eqr, self.tok_uc_free])
        b1 = self.op("dve", lambda e: e.tensor_scalar(
            out=sb["bias1_" + pp][:, 0:1], in0=sb["QRt_" + pp][:, 0:1],
            scalar1=sb["MXcol"][:, 0:1], scalar2=1.0 / 8.0,
            op0=ALU.subtract, op1=ALU.mult),
            [eqr, mxe, self.tok_b1_free[int(pp)]])
        b2 = self.op("dve", lambda e: e.tensor_scalar(
            out=sb["bias2_" + pp][:, 0:1], in0=sb["QRt_" + pp][:, NREL - 1:NREL],
            scalar1=sb["MXcol"][:, 0:1], scalar2=1.0 / 8.0,
            op0=ALU.subtract, op1=ALU.mult),
            [eqr, mxe, self.tok_b2_free[int(pp)]])
        self.tok_qrt_free[int(pp)] = b2
        btoks = []
        for half in range(2):
            prev = self.tok_bandp_free[int(pp)]
            for jj in range(4):
                j = half * 4 + jj + 1
                dst = bandp[:, half * W:(half + 1) * W]
                if jj == 0:
                    tt = self.op("dve", lambda e, j=j, dst=dst: e.tensor_scalar(
                        out=dst, in0=cMstep[:, (j - 1) * W:j * W],
                        scalar1=sb["Ucols"][:, j - 1:j], scalar2=None,
                        op0=ALU.mult), [uc, prev])
                else:
                    tt = self.op("dve", lambda e, j=j, dst=dst:
                                 e.scalar_tensor_tensor(
                                     out=dst, in0=cMstep[:, (j - 1) * W:j * W],
                                     scalar=sb["Ucols"][:, j - 1:j], in1=dst,
                                     op0=ALU.mult, op1=ALU.add), [uc, tt])
                prev = tt
            btoks.append(tt)
        wk0 = max(q0 - 4, 0)
        wk1 = min(q0 + 132, S)
        self.tok_uc_free = btoks[-1]
        return dict(h=h, g=g, qt=qt, pp=pp, eq=eq, b1=b1, b2=b2,
                    q0=q0, wk0=wk0, wk1=wk1, nk=nk, btoks=btoks)

    def l0_attn_fa_sc(self, st, kv_toks):
        cfg, sb, ps = self.cfg, self.sb, self.ps
        h, g, qt = st["h"], st["g"], st["qt"]
        Kf = sb["Kf"]
        bandp = sb["bandp_" + st["pp"]]
        Qg = sb["Qg"]
        S, W = cfg.S, cfg.W
        ch = h // 2
        q0, wk0, wk1, nk = st["q0"], st["wk0"], st["wk1"], st["nk"]
        eq, btoks = st["eq"], st["btoks"]
        qs = Qg[:, qt * 128:(qt + 1) * 128]
        smm = []
        for b in range(nk):
            t = self.mm(b, ps[b][:, 0:512], qs,
                        Kf[:, ch * S + b * 512: ch * S + (b + 1) * 512],
                        True, True, deps=[eq] + kv_toks)
            smm.append(t)
        badds = []
        kk = wk0
        while kk < wk1:
            b = kk // 512
            ke = min(wk1, (b + 1) * 512)
            off = kk - (q0 - 4)
            t = self.op("dve", lambda e, b=b, kk=kk, ke=ke, off=off:
                        e.tensor_tensor(
                            out=ps[b][:, kk - b * 512:ke - b * 512],
                            in0=ps[b][:, kk - b * 512:ke - b * 512],
                            in1=bandp[:, off:off + ke - kk],
                            op=ALU.add), btoks + [smm[b]])
            t2 = self.op("dve", lambda e, b=b, kk=kk, ke=ke, off=off:
                         e.tensor_tensor(
                             out=ps[b][:, kk - b * 512:ke - b * 512],
                             in0=ps[b][:, kk - b * 512:ke - b * 512],
                             in1=bandp[:, W + off:W + off + ke - kk],
                             op=ALU.add), [t])
            badds.append((b, t2))
            kk = ke
        self.tok_bandp_free[int(st["pp"])] = badds[-1][1]
        st.update(smm=smm, badd=dict(badds))
        return st

    def l0_attn_fb_act(self, st):
        cfg, sb, ps = self.cfg, self.sb, self.ps
        pp = st["pp"]
        probs = sb["probs" + pp]
        S = cfg.S
        q0, wk1, nk = st["q0"], st["wk1"], st["nk"]
        smm, badd_of_bank, b1, b2 = st["smm"], st["badd"], st["b1"], st["b2"]
        ranges = []
        if q0 > 0:
            ranges.append((0, q0, "bias1_" + pp, 8))
        if wk1 < S:
            ranges.append((wk1, S, "bias2_" + pp, 10))
        ranges.append((q0, wk1, "bias1_" + pp, 9))
        prb_toks = []
        bank_exp = {}
        acc_cols = []
        acc_base = 9 + 7 * int(pp)
        acc_n = acc_base
        for (ka, kb, bn, acc_i) in ranges:
            kk = ka
            while kk < kb:
                b = kk // 512
                ke = min(kb, (b + 1) * 512)
                dep_sc = [badd_of_bank.get(b, smm[b]), b1, b2] + \
                    list(self.tok_probs_free)
                at = self.act(probs[:, kk:ke],
                              ps[b][:, kk - b * 512:ke - b * 512],
                              func=AF.Exp, bias=sb[bn][:, 0:1], scale=1.0 / 8.0,
                              accum=sb["accD"][:, acc_n:acc_n + 1],
                              deps=dep_sc)
                bank_exp[b] = at
                acc_cols.append((acc_i, acc_n))
                acc_n += 1
                prb_toks.append(at)
                kk = ke
        for b in range(nk):
            self.free_bank(b, bank_exp[b])
        self.tok_b1_free[int(pp)] = prb_toks[-1]
        self.tok_b2_free[int(pp)] = prb_toks[-1]
        st.update(prb_toks=prb_toks, acc_cols=acc_cols, acc_n=acc_n,
                  acc_base=acc_base)
        return st

    def l0_attn_fb_dve(self, st):
        cfg, sb, ps = self.cfg, self.sb, self.ps
        pp = st["pp"]
        probs = sb["probs" + pp]
        cMpre = sb["cMpre"]
        cDdiag = sb["cDdiag"]
        S, NT, NREL = cfg.S, cfg.NT, cfg.NREL
        q0, wk1, nk = st["q0"], st["wk1"], st["nk"]
        prb_toks = st["prb_toks"]
        acc_cols, acc_n = st["acc_cols"], st["acc_n"]
        acc_base = st["acc_base"]
        etoks = []
        pre_cols = [c for (ai, c) in acc_cols if ai == 8]
        if q0 > 0:
            npre = len(pre_cols)
            assert pre_cols == list(range(acc_base, acc_base + npre))
            if npre == 1:
                t = self.op("dve", lambda e, ab=acc_base: e.tensor_copy(
                    out=sb["accD"][:, 8:9], in_=sb["accD"][:, ab:ab + 1]),
                    prb_toks)
            else:
                t = self.op("dve", lambda e, npre=npre, ab=acc_base:
                            e.tensor_reduce(
                                out=sb["accD"][:, 8:9],
                                in_=sb["accD"][:, ab:ab + npre],
                                axis=AXX.X, op=ALU.add), prb_toks)
            etoks.append(t)
        sS = self.op("dve", lambda e, an=acc_n, ab=acc_base: e.tensor_reduce(
            out=sb["Scol"][:, 0:1], in_=sb["accD"][:, ab:an],
            axis=AXX.X, op=ALU.add), prb_toks)
        etoks.append(sS)
        s1 = sS
        rs = self.op("dve", lambda e: e.reciprocal(
            out=sb["recipS"][:, 0:1], in_=sb["Scol"][:, 0:1]), [s1])
        wp1 = min(q0 + 131, S)
        wlen = wp1 - q0
        dts = []
        for oi in range(7):
            dt = self.op("dve", lambda e, oi=oi: e.scalar_tensor_tensor(
                out=sb["hg"][:, oi * 131:oi * 131 + wlen], in0=probs[:, q0:wp1],
                scalar=1.0, in1=cDdiag[:, oi * 131:oi * 131 + wlen],
                op0=ALU.mult, op1=ALU.mult,
                accum_out=sb["accD"][:, oi:oi + 1]),
                prb_toks + [self.tok_hg_free2])
            dts.append(dt)
        mp = self.op("dve", lambda e: e.scalar_tensor_tensor(
            out=sb["hg"][:, 0:wlen], in0=probs[:, q0:wp1],
            scalar=1.0, in1=cMpre[:, 0:wlen], op0=ALU.mult, op1=ALU.mult,
            accum_out=sb["accD"][:, 23:24]), prb_toks + [dts[0]])
        self.tok_hg_free = mp
        self.tok_hg_free2 = dts[-1]
        if q0 > 0:
            t8a = self.op("dve", lambda e: e.tensor_sub(
                out=sb["accD"][:, 7:8], in0=sb["Scol"][:, 0:1],
                in1=sb["accD"][:, 8:9]), [s1, mp])
            t8 = self.op("dve", lambda e: e.tensor_sub(
                out=sb["accD"][:, 7:8], in0=sb["accD"][:, 7:8],
                in1=sb["accD"][:, 23:24]), [t8a])
        else:
            t8 = self.op("dve", lambda e: e.tensor_sub(
                out=sb["accD"][:, 7:8], in0=sb["Scol"][:, 0:1],
                in1=sb["accD"][:, 23:24]), [s1, mp])
        d8 = self.act(sb["D8_" + pp][:, 0:8], sb["accD"][:, 0:8],
                      func=AF.Identity, scale=sb["recipS"][:, 0:1],
                      deps=[t8, rs, self.tok_d8_free[int(pp)]] + dts)
        dR = self.op("dve", lambda e: e.tensor_scalar(
            out=sb["diagR" + pp][:, :], in0=sb["cI128"][:, :],
            scalar1=sb["recipS"][:, 0:1], scalar2=None, op0=ALU.mult),
            [rs, self.tok_diagR_free[int(pp)]])
        st.update(prb_toks=prb_toks, dR=dR, d8=d8, mp=mp, dts=dts)
        return st

    def l0_attn_back_pe(self, st, kv_toks):
        cfg, sb, ps = self.cfg, self.sb, self.ps
        h, g, qt, pp = st["h"], st["g"], st["qt"], st["pp"]
        probs = sb["probs" + pp]
        pTs = sb["pTs"]
        AO = sb["AO"]
        Vf = sb["Vf"]
        S, D, DC, NT, DH = cfg.S, cfg.D, cfg.DC, cfg.NT, cfg.DH
        par, ch = h % 2, h // 2
        prb_toks, dR = st["prb_toks"], st["dR"]
        td = self.mm(4, ps[4][0:8, 0:128], sb["D8_" + pp][:, 0:8],
                     sb["cI128"][:, :], True, True, deps=[st["d8"]])
        ed = self.act(sb["D8Tp" + pp][0:8, :], ps[4][0:8, 0:128],
                      deps=[td, self.tok_d8t_free[int(pp)]])
        self.free_bank(4, ed)
        self.tok_d8_free[int(pp)] = td
        nkc = S // 128
        ev_pt = []
        for bb in range(nkc // 4):
            bank = 5 + (bb % 2)
            mmp = []
            for j in range(4):
                kc = bb * 4 + j
                mmp.append(self.mm(bank, ps[bank][:, j * 128:(j + 1) * 128],
                                   probs[:, kc * 128:(kc + 1) * 128],
                                   sb["diagR" + pp][:, :], True, True,
                                   deps=prb_toks + [dR]))
            if bb % 2 == 0:
                epv = self.op("dve", lambda e, bb=bb, bank=bank: e.tensor_copy(
                    out=pTs[:, bb * 512:(bb + 1) * 512], in_=ps[bank][:, 0:512]),
                    mmp + [self.tok_pts_free])
            else:
                epv = self.act(pTs[:, bb * 512:(bb + 1) * 512],
                               ps[bank][:, 0:512],
                               deps=mmp + [self.tok_pts_free])
            self.free_bank(bank, epv)
            ev_pt.append(epv)
        self.tok_diagR_free[int(pp)] = ev_pt[-1]
        self._av_par = getattr(self, "_av_par", 0) ^ 1
        avc = self._av_par * 128
        av = None
        avdep = self.tok_av_free[self._av_par] or self.pslast[7]
        self.pslast[7] = None
        for kc in range(nkc):
            tc = kc
            av = self.mm(7, ps[7][0:64, avc:avc + 128],
                         Vf[:, tc * D + h * DH: tc * D + (h + 1) * DH],
                         pTs[:, kc * 128:(kc + 1) * 128],
                         kc == 0, kc == nkc - 1,
                         deps=[ev_pt[kc // 4], avdep if kc == 0 else None] + kv_toks)
        w2 = self.mm(7, ps[7][0:64, 256 + avc:256 + avc + 128],
                     sb["TVmodp"][:, 0:DH], sb["D8Tp" + pp][:, :], True, True,
                     deps=[ed, self.tok_w2_free[self._av_par]])
        self.tok_d8t_free[int(pp)] = w2
        self.tok_pts_free = av
        self.tok_probs_free = [ev_pt[-1], st["dts"][-1], st["mp"]]
        st.update(av=av, w2=w2, avc=avc, avpar=self._av_par)
        return st

    def l0_attn_back_tail(self, st):
        cfg, sb, ps = self.cfg, self.sb, self.ps
        h, qt = st["h"], st["qt"]
        AO = sb["AO"]
        NT = cfg.NT
        par, ch = h % 2, h // 2
        av, w2, avc, avpar = st["av"], st["w2"], st["avc"], st["avpar"]
        aos = AO[par * 64:par * 64 + 64, ch * NT + qt * 128:ch * NT + (qt + 1) * 128]
        ao1 = self.act(aos, ps[7][0:64, avc:avc + 128], func=AF.Identity,
                       bias=sb["ctvT"][:, 0:1], deps=[av])
        ao = self.op("dve", lambda e, aos=aos, avc=avc: e.tensor_tensor(
            out=aos, in0=aos, in1=ps[7][0:64, 256 + avc:256 + avc + 128],
            op=ALU.add), [ao1, w2])
        self.tok_av_free[avpar] = ao1
        self.tok_w2_free[avpar] = ao
        return ao

    def oproj_group(self, wname, g, ao_toks, bias_idx=None, wo_free=None,
                    delta=None):
        cfg, sb, ps = self.cfg, self.sb, self.ps
        AO = sb["AO"]
        S, D, DC, NT = cfg.S, cfg.D, cfg.DC, cfg.NT
        col = g * NT
        toks = []
        for m in range(DC):
            cst = self.stage_cols(wname, m * 128, 128, DC,
                                  [wo_free[0], self.tok_hg_free, self.tok_hg_free2])
            wo_free[0] = cst
            self.tok_hg_free = cst
            self.tok_hg_free2 = cst
            bk = m % 2
            mmo = None
            for kc in range(DC):
                mmo = self.mm(bk, ps[bk][:, 0:NT],
                              sb["hg"][:, kc * 128:(kc + 1) * 128],
                              AO[:, kc * NT:(kc + 1) * NT],
                              kc == 0, kc == DC - 1, deps=[cst] + ao_toks)
            par = m % 2
            bias = (sb["biasT"][:, bias_idx * DC + m:bias_idx * DC + m + 1]
                    if bias_idx is not None else None)
            if delta is not None:
                eo = self.act(delta[:, m * NT:(m + 1) * NT],
                              ps[bk][:, 0:NT],
                              func=AF.Identity if bias is not None else AF.Copy,
                              bias=bias, deps=[mmo] + self.tok_xr_free)
                self.free_bank(bk, eo)
                sp = self.dma("pool", "dsp", lambda e, m=m, col=col: e.dma_start(
                    out=self.dram["dscr"][:, m * S + col:m * S + col + NT],
                    in_=delta[:, m * NT:(m + 1) * NT]), [eo])
                wo_free[1] = eo
                toks.append(sp)
            else:
                ev = sb["evac"][:, par * NT:(par + 1) * NT]
                eo = self.act(ev, ps[bk][:, 0:NT],
                              func=AF.Identity if bias is not None else AF.Copy,
                              bias=bias, deps=[mmo, self.tok_evac_free[par]])
                self.free_bank(bk, eo)
                ad = self.op("dve", lambda e, m=m, col=col, ev=ev: e.tensor_tensor(
                    out=sb["xr"][:, m * S + col:m * S + col + NT],
                    in0=sb["xr"][:, m * S + col:m * S + col + NT],
                    in1=ev, op=ALU.add), [eo] + self.tok_xr_free)
                self.tok_evac_free[par] = ad
                wo_free[1] = ad
                toks.append(ad)
        return toks

    def l0_phase_b(self, mrow, rrow, xng, stat_toks, kv_toks, delta):
        cfg = self.cfg
        wq_free = [None, None]
        wo_free = [None, None]
        all_toks = []
        prev_stats = list(stat_toks)
        for g in range(cfg.NTT):
            stat_g = self.ln_stats_one(mrow, rrow, 0, g, 0, prev_stats)
            xnt = self.xn_tile(lambda c: xng[:, c * cfg.NT:(c + 1) * cfg.NT],
                               mrow, rrow, g, stat_g, bm=2, br=3, stats_col=0)
            prev_stats = stat_g + xnt
            ao_toks = []
            pend = None
            ntiles = cfg.NH * cfg.QG
            sts = {}
            hd = None
            cur_head = -1

            def _qr(i):
                nonlocal hd, cur_head
                h, qt = divmod(i, cfg.QG)
                if h != cur_head:
                    self.tok_qg_free = self.fw.last("pe")
                    hd = self.l0_attn_head(h, g, xng, xnt, kv_toks, wq_free)
                    cur_head = h
                self._l0pp = getattr(self, "_l0pp", 0) ^ 1
                sts[i] = self.l0_attn_fa_qr(h, g, qt, hd[0], hd[1], hd[2],
                                            kv_toks, str(self._l0pp))

            _qr(0)
            self.l0_attn_fa_sc(sts[0], kv_toks)
            if ntiles > 1:
                _qr(1)
            for i in range(ntiles):
                st = self.l0_attn_fb_act(sts[i])
                if i + 1 < ntiles:
                    self.l0_attn_fa_sc(sts[i + 1], kv_toks)
                if pend is not None:
                    pend = self.l0_attn_back_pe(pend, kv_toks)
                st = self.l0_attn_fb_dve(st)
                if pend is not None:
                    ao_toks.append(self.l0_attn_back_tail(pend))
                if i + 2 < ntiles:
                    _qr(i + 2)
                pend = st
                sts.pop(i, None)
            self.tok_qg_free = self.fw.last("pe")
            pend = self.l0_attn_back_pe(pend, kv_toks)
            ao_toks.append(self.l0_attn_back_tail(pend))
            pend = None
            t = self.oproj_group("Wo", g, ao_toks, bias_idx=3, wo_free=wo_free,
                                 delta=delta)
            all_toks += t
        return all_toks

    def l0_phase_c(self, mrow, rrow, xng, stat_toks, h_all, wif, wih, wif2, wih2,
                   delta=None, bar_deps=()):
        cfg, sb, ps = self.cfg, self.sb, self.ps
        S, D, DC, NT, HC = cfg.S, cfg.D, cfg.DC, cfg.NT, cfg.HC
        bd = list(bar_deps)
        gfree = [bd, bd, bd, bd]
        dfree = [None, None]
        toks = []
        self.tok_dfree = {}
        prev_stats = list(stat_toks) + bd
        for t in range(cfg.NTT):
            col = t * NT
            stat_t = self.ln_stats_one(mrow, rrow, 0, t, 0, prev_stats)
            xnt = self.xn_tile(lambda c: xng[:, c * NT:(c + 1) * NT],
                               mrow, rrow, t, stat_t, bm=6, br=7, stats_col=0)
            prev_stats = stat_t + xnt
            h_toks = []
            for hc in range(HC):
                g0 = gfree[0] if isinstance(gfree[0], list) else [gfree[0]]
                srcg = self.dram["wgT"][hc * 128:(hc + 1) * 128, :]
                dstg = wif[:, 0:DC * 128]
                g2 = gfree[2] if isinstance(gfree[2], list) else [gfree[2]]
                dg = [self.dma("sp", "w_wg", lambda e, srcg=srcg, dstg=dstg:
                               e.dma_start(out=dstg, in_=srcg), g2)]
                cg = self.act(wih[:, 0:DC * 128], wif[:, 0:DC * 128],
                              deps=dg + g0)
                gfree[2] = cg
                g1 = gfree[1] if isinstance(gfree[1], list) else [gfree[1]]
                srcu = self.dram["wuT"][hc * 128:(hc + 1) * 128, :]
                dstu = wif2[:, 0:DC * 128]
                g3 = gfree[3] if isinstance(gfree[3], list) else [gfree[3]]
                du = [self.dma("pool", "w_wu", lambda e, srcu=srcu, dstu=dstu:
                               e.dma_start(out=dstu, in_=srcu), g3)]
                cu = self.act(wih2[:, 0:DC * 128], wif2[:, 0:DC * 128],
                              deps=du + g1)
                gfree[3] = cu
                mg = mu = None
                bg, bu = 0 + (hc % 2) * 2, 1 + (hc % 2) * 2
                for kc in range(DC):
                    mg = self.mm(bg, ps[bg][:, 0:NT], wih[:, kc * 128:(kc + 1) * 128],
                                 xng[:, kc * NT:(kc + 1) * NT],
                                 kc == 0, kc == DC - 1, deps=[cg] + xnt)
                    mu = self.mm(bu, ps[bu][:, 0:NT], wih2[:, kc * 128:(kc + 1) * 128],
                                 xng[:, kc * NT:(kc + 1) * NT],
                                 kc == 0, kc == DC - 1, deps=[cu] + xnt)
                gfree[0] = mg
                gfree[1] = mu
                eg = self.act(sb["hg"][:, 0:NT], ps[bg][:, 0:NT],
                              func=AF.Silu, scale=1.0 / 127.0,
                              deps=[mg, self.tok_hgc_free])
                self.free_bank(bg, eg)
                eu = self.op("dve", lambda e, bu=bu: e.tensor_copy(
                    out=sb["hg"][:, NT:2 * NT], in_=ps[bu][:, 0:NT]),
                    [mu, self.tok_hgc_free])
                self.free_bank(bu, eu)
                hm = self.op("dve", lambda e, hc=hc: e.tensor_mul(
                    out=h_all[:, hc * NT:(hc + 1) * NT],
                    in0=sb["hg"][:, 0:NT], in1=sb["hg"][:, NT:2 * NT]),
                    [eg, eu])
                self.tok_hgc_free = hm
                h_toks.append(hm)
            self.tok_dld = {}
            for m in range(DC):
                dl_ = self.dma("pool", "dsp", lambda e, m=m, col=col: e.dma_start(
                    out=delta[:, m * NT:(m + 1) * NT],
                    in_=self.dram["dscr"][:, m * S + col:m * S + col + NT]),
                    [self.tok_dfree.get(m)] + bd)
                self.tok_dld[m] = dl_
            mmd = {}
            for hc in range(HC):
                dd = self.dma("sp", "w_wd", lambda e, hc=hc: e.dma_start(
                    out=wif[:, 0:D],
                    in_=self.dram["wdT"][hc * 128:(hc + 1) * 128, :]),
                    [dfree[0]] + (gfree[0] if isinstance(gfree[0], list) else [gfree[0]]))
                cd = self.act(wih[:, 0:D], wif[:, 0:D],
                              deps=[dd, dfree[1]] +
                              (gfree[1] if isinstance(gfree[1], list) else [gfree[1]]))
                dfree[0] = cd
                for m in range(DC):
                    mmd[m] = self.mm(m, ps[m][:, 0:NT],
                                     wih[:, m * 128:(m + 1) * 128],
                                     h_all[:, hc * NT:(hc + 1) * NT],
                                     hc == 0, hc == HC - 1,
                                     deps=[cd, h_toks[hc]])
                dfree[1] = mmd[DC - 1]
            for m in range(DC):
                par = m % 2
                ev = sb["evac"][:, par * NT:(par + 1) * NT]
                eo = self.act(ev, ps[m][:, 0:NT], func=AF.Copy,
                              scale=1.0 / (127.0 * 127.0),
                              deps=[mmd[m], self.tok_evac_free[par]])
                self.free_bank(m, eo)
                ad = self.op("dve", lambda e, m=m, col=col, ev=ev: e.tensor_tensor(
                    out=sb["xr"][:, m * S + col:m * S + col + NT],
                    in0=sb["xr"][:, m * S + col:m * S + col + NT],
                    in1=ev, op=ALU.add), [eo] + self.tok_xr_free)
                ad2 = self.op("dve", lambda e, m=m, col=col: e.tensor_tensor(
                    out=sb["xr"][:, m * S + col:m * S + col + NT],
                    in0=sb["xr"][:, m * S + col:m * S + col + NT],
                    in1=delta[:, m * NT:(m + 1) * NT], op=ALU.add),
                    [ad, self.tok_dld[m]])
                self.tok_evac_free[par] = ad2
                self.tok_dfree[m] = ad2
                toks.append(ad2)
            gfree[0] = dfree[1]
            gfree[1] = dfree[1]
            gfree[2] = dfree[0]
        return toks

    def reset_attn_toks(self):
        self.tok_qg_free = None
        self.tok_mx_free = None
        self.tok_kmax = None
        self.tok_hg_free = None
        self.tok_hg_free2 = None
        self.tok_uc_free = None
        self.tok_b1_free = [None, None]
        self.tok_b2_free = [None, None]
        self.tok_qrt_free = [None, None]
        self.tok_bandp_free = [None, None]
        self.tok_probs_free = []
        self.tok_diagR_free = [None, None]
        self.tok_d8t_free = [None, None]
        self.tok_d8_free = [None, None]
        self.tok_pts_free = None
        self.tok_hgc_free = None
        self.tok_evac_free = [None, None]
        self.tok_xr_free = []
        self.tok_lgr_free = None
        self.tok_gat_free = None
        self.tok_mid_free = None
        self.tok_zt_free = None
        self.tok_qaug_free = None
        self.tok_ropeq_free = None
        self.tok_qr_free = None
        self.tok_av_free = [None, None]
        self.tok_w2_free = [None, None]

    def build(self, layers=(0, 1), nseq=None):
        cfg = self.cfg
        nseq = nseq if nseq is not None else cfg.B_local
        self.declare()
        with ExitStack() as ctx0:
            self.ctx = ctx0
            if True:
                self.alloc_persistent(ctx0)
                self.alloc_psum(ctx0)
                self.reset_attn_toks()
                self.emit_init()
                dep0 = self.init_toks
                fin = []
                for seq in range(nseq):
                    xl = self.load_x(seq, dep0)
                    if 0 in layers:
                        with ExitStack() as c_l0:
                            mrow = self.sbuf(c_l0, "mrow0", [1, cfg.NT], F32)
                            rrow = self.sbuf(c_l0, "rrow0", [1, cfg.NT], F32)
                            xng = self.sbuf(c_l0, "xng0", [128, cfg.DC * cfg.NT], BF16)
                            delta = self.sbuf(c_l0, "delta0", [128, cfg.DC * cfg.NT], BF16)
                            with ExitStack() as c_kv:
                                Kf = self.sbuf(c_kv, "Kf", [128, cfg.DC * cfg.S], BF16)
                                Vf = self.sbuf(c_kv, "Vf", [128, (cfg.S // 128) * cfg.D], BF16)
                                bvh = self.sbuf(c_kv, "bvh", [1, cfg.D], BF16)
                                tb_ = self.dma("sp", "icv", lambda e: e.dma_start(
                                    out=self.sb["sq"][0:1, 0:cfg.D],
                                    in_=self.dram["bv_row"]), dep0)
                                cb_ = self.act(bvh[0:1, :], self.sb["sq"][0:1, 0:cfg.D],
                                               deps=[tb_])
                                dep0 = dep0 + [cb_]
                                stat_toks, kv_toks = self.l0_phase_a(
                                    mrow, rrow, xng, xl + dep0)
                                if seq == 0:
                                    self.dbg("Kf", Kf[:, :], kv_toks,
                                             (128, cfg.DC * cfg.S), BF16)
                                    self.dbg("Vf", Vf[:, :], kv_toks,
                                             (128, (cfg.S // 128) * cfg.D), BF16)
                                with ExitStack() as c_b:
                                    self.sbuf(c_b, "cMstep", [128, 8 * cfg.W], BF16)
                                    self.sbuf(c_b, "cMpre", [128, 131], BF16)
                                    self.sbuf(c_b, "cDdiag", [128, 7 * 131], BF16)
                                    self.sbuf(c_b, "probs0", [128, cfg.S], BF16)
                                    self.sbuf(c_b, "probs1", [128, cfg.S], BF16)
                                    self.sbuf(c_b, "pTs", [128, cfg.S], BF16)
                                    self.sbuf(c_b, "AO", [128, cfg.DC * cfg.NT], BF16)
                                    self.sbuf(c_b, "wqs", [128, cfg.DC * 64], F32)
                                    self.sbuf(c_b, "wqh", [128, cfg.DC * 64], BF16)
                                    self.sbuf(c_b, "bandp_0", [128, 2 * cfg.W], BF16)
                                    self.sbuf(c_b, "bandp_1", [128, 2 * cfg.W], BF16)
                                    self.sbuf(c_b, "qrel_s", [cfg.NREL, cfg.NT], BF16)
                                    self.sbuf(c_b, "Qg", [128, cfg.NT], BF16)
                                    self.sbuf(c_b, "Qr", [128, cfg.NT], BF16)
                                    mdep = [self.fw.last("act"), self.fw.last("dve")]
                                    mq_ = self.op("dve", lambda e, q_=self.sb["qrel_s"]:
                                                  e.memset(q_[:, :], 0.0), mdep)
                                    mg_ = self.op("dve", lambda e, q_=self.sb["Qg"]:
                                                  e.memset(q_[:, :], 0.0), mdep)
                                    kv_toks = kv_toks + [mq_, mg_]
                                    ml = [self.dma("sp", "mk",
                                        lambda e, dst=self.sb["c" + n], src=self.dram[n]:
                                        e.dma_start(out=dst[:, :], in_=src), mdep)
                                        for n in ("Mstep", "Mpre", "Ddiag")]
                                    kv_toks = kv_toks + ml
                                    self.l0_phase_b(mrow, rrow, xng, stat_toks, kv_toks, delta)
                                    bar1 = self.bar()
                                    if seq == 0:
                                        self.dbg("xrB", delta[:, :], bar1,
                                                 (128, cfg.DC * cfg.S), BF16)
                                        self.dbg("probs", self.sb["probs0"][:, :], bar1,
                                                 (128, cfg.S), BF16)
                            with ExitStack() as c_c:
                                h_all = self.sbuf(c_c, "h_all", [128, cfg.HC * cfg.NT], BF16)
                                wif = self.sbuf(c_c, "wif", [128, cfg.D], F32)
                                wih = self.sbuf(c_c, "wih", [128, cfg.D], BF16)
                                wif2 = self.sbuf(c_c, "wif2", [128, cfg.D], F32)
                                wih2 = self.sbuf(c_c, "wih2", [128, cfg.D], BF16)
                                self.tok_xr_free = bar1
                                self.tok_hgc_free = bar1[0]
                                self.l0_phase_c(mrow, rrow, xng, stat_toks, h_all,
                                                wif, wih, wif2, wih2, delta=delta,
                                                bar_deps=bar1)
                                if seq == 0:
                                    self.dbg("h_all", h_all[:, :], self.bar(),
                                             (128, cfg.HC * cfg.NT), BF16)
                                    self.dbg("hgate", self.sb["hg"][:, :],
                                             self.bar(), (128, 2 * cfg.NT), BF16)
                                    self.dbg("xng", xng[:, :], self.bar(),
                                             (128, cfg.DC * cfg.NT), BF16)
                            bar2 = self.bar()
                    else:
                        bar2 = self.bar() + xl
                    if 1 in layers:
                        bar2 = self.emit_l1(seq, bar2)
                    st = self.store_y(seq, bar2)
                    fin += st
                    dep0 = st
                self.fw.set_finish(fin + self._dbg_toks)
                with self.nc.Block() as block:
                    self.fw.finalize(block, ctx0)
        return self.nc

    def l1_phase_a(self, mrow, rrow, xng, kvn, Kf, Vf, KR, QC, ccos, csin, deps):
        cfg, sb, ps = self.cfg, self.sb, self.ps
        lgr = sb["lgr"]
        mid = sb["mid"]
        Zt = sb["Zt"]
        gat = sb["gat"]
        lBh = sb["lBh"]
        tok_cosA = [None]
        S, D, DC, NT = cfg.S, cfg.D, cfg.DC, cfg.NT
        KV = cfg.KV_RANK
        KVC = KV // 128
        QRC = cfg.Q_RANK // 128
        RCC = cfg.NH * cfg.RC // 128
        kfree = [None, None]
        out_toks = []
        prev_stats = list(deps)
        xnt_prev = []
        for t in range(cfg.NTT):
            col = t * NT
            stat_toks = self.ln_stats_one(mrow, rrow, 0, t, 0, deps + prev_stats)
            prev_stats = stat_toks
            if kfree[0] is None:
                kfree = [stat_toks[-1], stat_toks[-1]]
            cA = self.dma("sp", "cs", lambda e, col=col: e.dma_start(
                out=ccos[:, 0:NT], in_=self.dram["cos_rep"][:, col:col + NT]),
                [tok_cosA[0]])
            cB = self.dma("sp", "cs", lambda e, col=col: e.dma_start(
                out=csin[:, 0:NT], in_=self.dram["sin_rep"][:, col:col + NT]),
                [tok_cosA[0]])
            xnt = self.xn_tile(lambda c: xng[:, c * NT:(c + 1) * NT],
                               mrow, rrow, t, stat_toks, stats_col=0)
            prev_stats = stat_toks + xnt
            xnt_prev = xnt
            mmkv = {}
            for m in range(KVC):
                cst = self.stage_cols("Wkv", m * 128, 128, DC, [kfree[0], kfree[1]], ch="w_kv")
                kfree[0] = cst
                mk = None
                for kc in range(DC):
                    mk = self.mm(m, ps[m][:, 0:NT],
                                 sb["hg"][:, kc * 128:(kc + 1) * 128],
                                 xng[:, kc * NT:(kc + 1) * NT],
                                 kc == 0, kc == DC - 1, deps=[cst] + xnt)
                mmkv[m] = mk
                kfree[1] = mk
            mms = None
            sqp = None
            for m in range(KVC):
                sq_ = self.act(sb["hg"][:, 0:NT], ps[m][:, 0:NT], func=AF.Square,
                               deps=[mmkv[m], kfree[1], sqp or self.tok_hgc_free])
                mms = self.mm(4, ps[4][0:1, 0:NT], sb["cones_colh"][:, 0:1],
                              sb["hg"][:, 0:NT], m == 0, m == KVC - 1, deps=[sq_])
                sqp = mms
            self.tok_hgc_free = mms
            st_ = self.act(sb["srow"][0:1, 0:NT], ps[4][0:1, 0:NT], func=AF.Sqrt,
                           scale=1.0 / KV, bias=sb["epsrow"][0:1, 1:2], deps=[mms])
            self.free_bank(4, st_)
            rt_ = self.op("dve", lambda e: e.reciprocal(
                out=sb["vrow"][0:1, 0:NT], in_=sb["srow"][0:1, 0:NT]), [st_])
            bc = self.mm(5, ps[5][:, 0:NT], sb["cones_row"][:, :],
                         sb["vrow"][0:1, 0:NT], True, True, deps=[rt_])
            kvn_toks = []
            for m in range(KVC):
                par = m % 2
                ev = sb["evac"][:, par * NT:(par + 1) * NT]
                e1 = self.act(ev, ps[m][:, 0:NT], func=AF.Identity,
                              scale=sb["cgkv"][:, m:m + 1],
                              deps=[mmkv[m], self.tok_evac_free[par]])
                self.free_bank(m, e1)
                e2 = self.op("dve", lambda e, m=m, ev=ev: e.tensor_mul(
                    out=kvn[:, m * NT:(m + 1) * NT], in0=ev, in1=ps[5][:, 0:NT]),
                    [e1, bc])
                self.tok_evac_free[par] = e2
                kvn_toks.append(e2)
            self.free_bank(5, kvn_toks[-1])
            for m in range(DC):
                cst = self.stage_cols("Wkdec", m * 128, 128, KVC, [kfree[0], kfree[1]], ch="w_kd")
                kfree[0] = cst
                bk = 6 + (m % 2)
                mk = None
                for kc in range(KVC):
                    mk = self.mm(bk, ps[bk][:, 0:NT],
                                 sb["hg"][:, kc * 128:(kc + 1) * 128],
                                 kvn[:, kc * NT:(kc + 1) * NT],
                                 kc == 0, kc == KVC - 1, deps=[cst] + kvn_toks)
                ek = self.act(Kf[:, m * S + col:m * S + col + NT], ps[bk][:, 0:NT],
                              deps=[mk])
                self.free_bank(bk, ek)
                kfree[1] = ek
                out_toks.append(ek)
            out_toks += self.v_tm_tile_src("Wvdec", kvn, t, KVC, D // 512, None,
                                           sb["sq"], sb["hg"], kfree, kvn_toks, Vf, NT)
            for m in range(RCC):
                cst = self.stage_cols("Wkrope", m * 128, 128, DC, [kfree[0], kfree[1]], ch="w_kr")
                kfree[0] = cst
                mk = None
                for kc in range(DC):
                    mk = self.mm(4, ps[4][:, 0:NT],
                                 sb["hg"][:, kc * 128:(kc + 1) * 128],
                                 xng[:, kc * NT:(kc + 1) * NT],
                                 kc == 0, kc == DC - 1, deps=[cst] + xnt)
                kfree[1] = mk
                ekr = self.act(sb["hg"][:, NT:2 * NT], ps[4][:, 0:NT],
                               deps=[mk, self.tok_hg_free])
                self.free_bank(4, ekr)
                rot = self.mm(5, ps[5][:, 0:NT], sb["cR128T"][:, :],
                              sb["hg"][:, NT:2 * NT], True, True, deps=[ekr])
                t1 = self.op("dve", lambda e: e.tensor_mul(
                    out=sb["hg"][:, 0:NT], in0=sb["hg"][:, NT:2 * NT],
                    in1=ccos[:, 0:NT]), [ekr, self.tok_hgc_free, cA])
                t2 = self.op("dve", lambda e, m=m, col=col: e.scalar_tensor_tensor(
                    out=KR[:, m * S + col:m * S + col + NT], in0=ps[5][:, 0:NT],
                    scalar=1.0, in1=csin[:, 0:NT],
                    op0=ALU.mult, op1=ALU.mult), [rot, cB])
                t3 = self.op("dve", lambda e, m=m, col=col: e.tensor_tensor(
                    out=KR[:, m * S + col:m * S + col + NT],
                    in0=KR[:, m * S + col:m * S + col + NT],
                    in1=sb["hg"][:, 0:NT], op=ALU.add), [t1, t2])
                self.free_bank(5, t2)
                self.tok_hg_free = t3
                self.tok_hgc_free = t3
                tok_cosA[0] = t3
                out_toks.append(t3)
            for m in range(QRC):
                cst = self.stage_cols("Wq1", m * 128, 128, DC, [kfree[0], kfree[1]], ch="w_q1")
                kfree[0] = cst
                bk = 6 + (m % 2)
                mk = None
                for kc in range(DC):
                    mk = self.mm(bk, ps[bk][:, 0:NT],
                                 sb["hg"][:, kc * 128:(kc + 1) * 128],
                                 xng[:, kc * NT:(kc + 1) * NT],
                                 kc == 0, kc == DC - 1, deps=[cst] + xnt)
                eq = self.act(QC[:, m * S + col:m * S + col + NT], ps[bk][:, 0:NT],
                              deps=[mk])
                self.free_bank(bk, eq)
                kfree[1] = eq
                out_toks.append(eq)
            mr = None
            for kc in range(DC):
                mr = self.mm(4, ps[4][0:8, 0:NT], sb["rwh"][:, kc * 8:(kc + 1) * 8],
                             xng[:, kc * NT:(kc + 1) * NT],
                             kc == 0, kc == DC - 1, deps=xnt)
            er = self.act(lgr[0:8, 0:NT], ps[4][0:8, 0:NT], func=AF.Identity,
                          bias=sb["crb"][:, 0:1], deps=[mr, self.tok_lgr_free])
            self.free_bank(4, er)
            b0 = self.mm(5, ps[5][0:8, 0:NT], sb["cones_row"][0:1, 0:8],
                         lgr[0:1, 0:NT], True, True, deps=[er])
            dl = self.op("dve", lambda e: e.tensor_sub(
                out=lgr[0:8, NT:2 * NT], in0=lgr[0:8, 0:NT],
                in1=ps[5][0:8, 0:NT]), [b0, self.tok_lgr_free])
            self.free_bank(5, dl)
            ex = self.act(lgr[0:8, 0:NT], lgr[0:8, NT:2 * NT],
                          func=AF.Exp, deps=[dl])
            sm = self.mm(4, ps[4][0:1, 0:NT], sb["cones_col"][0:8, 0:1],
                         lgr[0:8, 0:NT], True, True, deps=[ex])
            rc = self.op("dve", lambda e: e.reciprocal(
                out=sb["srow"][0:1, 0:NT], in_=ps[4][0:1, 0:NT]), [sm])
            self.free_bank(4, rc)
            bs = self.mm(5, ps[5][0:8, 0:NT], sb["cones_row"][0:1, 0:8],
                         sb["srow"][0:1, 0:NT], True, True, deps=[rc])
            gt = self.op("dve", lambda e: e.tensor_mul(
                out=gat[0:8, 0:NT], in0=lgr[0:8, 0:NT],
                in1=ps[5][0:8, 0:NT]), [bs, self.tok_gat_free])
            self.free_bank(5, gt)
            self.tok_lgr_free = gt
            mm_ = None
            for kc in range(DC):
                mm_ = self.mm(4, ps[4][0:8, 0:NT], sb["lAh"][:, kc * 8:(kc + 1) * 8],
                              xng[:, kc * NT:(kc + 1) * NT],
                              kc == 0, kc == DC - 1, deps=xnt)
            em = self.act(mid[0:8, 0:NT], ps[4][0:8, 0:NT],
                          deps=[mm_, self.tok_mid_free])
            self.free_bank(4, em)
            gr = self.mm(4, ps[4][0:64, 0:NT], sb["cSELe"][:, :], gat[0:8, 0:NT],
                         True, True, deps=[gt])
            mrp = self.mm(5, ps[5][0:64, 0:NT], sb["cSELr"][:, :], mid[0:8, 0:NT],
                          True, True, deps=[em])
            ev = sb["evac"][:, 0:NT]
            eg = self.act(ev[0:64, :], ps[4][0:64, 0:NT],
                          deps=[gr, self.tok_evac_free[0]])
            self.free_bank(4, eg)
            zt = self.op("dve", lambda e: e.tensor_mul(
                out=Zt[0:64, 0:NT], in0=ev[0:64, :], in1=ps[5][0:64, 0:NT]),
                [eg, mrp, self.tok_zt_free])
            self.free_bank(5, zt)
            self.tok_evac_free[0] = zt
            for m in range(DC):
                cst = self.stage_cols("bwT", m * 128, 128, DC, [kfree[0], kfree[1]], ch="w_bw")
                kfree[0] = cst
                bk = 6 + (m % 2)
                mb = None
                for kc in range(DC):
                    mb = self.mm(bk, ps[bk][:, 0:NT],
                                 sb["hg"][:, kc * 128:(kc + 1) * 128],
                                 xng[:, kc * NT:(kc + 1) * NT],
                                 kc == 0, False, deps=[cst] + xnt)
                ml_ = self.mm(bk, ps[bk][:, 0:NT], lBh[:, m * 128:(m + 1) * 128],
                              Zt[:, 0:NT], False, True, deps=[zt, mb])
                kfree[1] = ml_
                par = m % 2
                ev2 = sb["evac"][:, par * NT:(par + 1) * NT]
                eo = self.act(ev2, ps[bk][:, 0:NT], func=AF.Identity,
                              bias=sb["biasT"][:, 4 * DC + m:4 * DC + m + 1],
                              deps=[ml_, self.tok_evac_free[par]])
                self.free_bank(bk, eo)
                ad = self.op("dve", lambda e, m=m, col=col, ev2=ev2: e.tensor_tensor(
                    out=sb["xr"][:, m * S + col:m * S + col + NT],
                    in0=sb["xr"][:, m * S + col:m * S + col + NT],
                    in1=ev2, op=ALU.add), [eo] + self.tok_xr_free)
                self.tok_evac_free[par] = ad
                out_toks.append(ad)
            self.tok_mid_free = mrp
            self.tok_gat_free = gr
            self.tok_zt_free = ml_
        return stat_toks, out_toks

    def v_tm_tile_src(self, wrows_name, src, t, kchunks, nn, bias_row,
                      wstage_f, wstage_h, wfree, src_toks, Vf, src_nt):
        cfg, sb = self.cfg, self.sb
        D, NT = cfg.D, cfg.NT
        ntc = NT // 128
        mms = {}
        cstp = wfree[0]
        last_mm = wfree[1]
        for kc in range(kchunks):
            dt = self.dma("sp", "w_" + wrows_name, lambda e, kc=kc: e.dma_start(
                out=wstage_f[:, 0:D],
                in_=self.dram[wrows_name][kc * 128:(kc + 1) * 128, :]), [cstp])
            cst = self.act(wstage_h[:, 0:D], wstage_f[:, 0:D], deps=[dt, last_mm])
            cstp = cst
            for tc in range(ntc):
                for n2 in range(nn):
                    bank = tc * nn + n2
                    mms[(tc, n2)] = self.mm(
                        bank, self.ps[bank][:, 0:512],
                        src[:, kc * src_nt + tc * 128: kc * src_nt + (tc + 1) * 128],
                        wstage_h[:, n2 * 512:(n2 + 1) * 512],
                        kc == 0, kc == kchunks - 1, deps=[cst] + src_toks)
                    last_mm = mms[(tc, n2)]
        wfree[0] = cstp
        evs = []
        for tc in range(ntc):
            for n2 in range(nn):
                bank = tc * nn + n2
                ev = self.act(Vf[:, (t * ntc + tc) * D + n2 * 512:
                                 (t * ntc + tc) * D + (n2 + 1) * 512],
                              self.ps[bank][:, 0:512], deps=[mms[(tc, n2)]])
                self.free_bank(bank, ev)
                evs.append(ev)
        wfree[1] = evs[-1]
        return evs

    def l1_attn_head(self, h, g, Kf, KR, QC, ccos, csin, a_toks, wq_free):
        cfg, sb, ps = self.cfg, self.sb, self.ps
        wqs = sb["wqs"]
        wqh = sb["wqh"]
        ropeq = sb["ropeq"]
        qaug = sb["qaug"]
        Qg = sb["Qg"]
        Qr = sb["Qr"]
        cSELq = sb["cSELq"]
        S, D, DC, NT, DH, RC = cfg.S, cfg.D, cfg.DC, cfg.NT, cfg.DH, cfg.RC
        par, ch = h % 2, h // 2
        qch = h % 4
        rch = h // 4
        QRC = cfg.Q_RANK // 128
        col = g * NT
        dts = [self.dma("sp", "w_qa", lambda e: e.dma_start(
            out=wqs[:, 0:QRC * 96],
            in_=self.dram["Wqaug"][h * 128:(h + 1) * 128, :]),
            [wq_free[0]])]
        cst = self.act(wqh[:, 0:QRC * 96], wqs[:, 0:QRC * 96],
                       deps=dts + [wq_free[1]])
        wq_free[0] = cst
        mq = None
        for kc in range(QRC):
            mq = self.mm(4, ps[4][0:96, 0:NT], wqh[:, kc * 96:(kc + 1) * 96],
                         QC[:, kc * S + col:kc * S + col + NT],
                         kc == 0, kc == QRC - 1, deps=[cst] + a_toks)
        wq_free[1] = mq
        eq = self.act(qaug[0:96, 0:NT], ps[4][0:96, 0:NT],
                      deps=[mq, self.tok_qaug_free])
        self.free_bank(4, eq)
        rot = self.mm(5, ps[5][0:32, 0:NT], sb["cR32Tp"][64:96, :],
                      qaug[64:96, 0:NT], True, True, deps=[eq])
        t0 = self.act(sb["hg"][0:32, 0:NT], qaug[64:96, 0:NT],
                      deps=[eq, self.tok_hg_free])
        t1 = self.op("dve", lambda e: e.tensor_mul(
            out=sb["hg"][0:32, NT:2 * NT], in0=sb["hg"][0:32, 0:NT],
            in1=ccos[0:32, 0:NT]), [t0, self.tok_hgc_free])
        t2 = self.op("dve", lambda e: e.scalar_tensor_tensor(
            out=ropeq[0:32, 0:NT], in0=ps[5][0:32, 0:NT], scalar=1.0,
            in1=csin[0:32, 0:NT], op0=ALU.mult, op1=ALU.mult),
            [rot, self.tok_ropeq_free])
        t3 = self.op("dve", lambda e: e.tensor_tensor(
            out=ropeq[0:32, 0:NT], in0=ropeq[0:32, 0:NT],
            in1=sb["hg"][0:32, NT:2 * NT], op=ALU.add), [t1, t2])
        self.free_bank(5, t2)
        self.tok_hg_free = t1
        self.tok_hgc_free = t3
        ms = self.op("dve", lambda e: e.memset(
            Qg[(1 - par) * 64:(1 - par) * 64 + 64, :], 0.0),
            [self.tok_qg_free])
        qc_ = self.act(Qg[par * 64:par * 64 + 64, :], qaug[0:64, 0:NT],
                       deps=[eq, ms, self.tok_qg_free])
        qr_ = self.mm(5, ps[5][:, 0:NT], cSELq[:, qch * 128:(qch + 1) * 128],
                      ropeq[0:32, 0:NT], True, True, deps=[t3])
        er_ = self.act(Qr[:, :], ps[5][:, 0:NT], deps=[qr_, self.tok_qr_free])
        self.free_bank(5, er_)
        self.tok_ropeq_free = qr_
        self.tok_qaug_free = qc_
        sq1 = self.act(sb["hg"][:, 0:NT], Qg[:, :], func=AF.Square,
                       deps=[qc_, self.tok_hg_free])
        n1 = self.mm(4, ps[4][0:1, 0:NT], sb["cones_colh"][:, 0:1],
                     sb["hg"][:, 0:NT], True, False, deps=[sq1])
        sq2 = self.act(sb["hg"][:, NT:2 * NT], Qr[:, :], func=AF.Square,
                       deps=[er_, self.tok_hgc_free])
        n2 = self.mm(4, ps[4][0:1, 0:NT], sb["cones_colh"][:, 0:1],
                     sb["hg"][:, NT:2 * NT], False, True, deps=[sq2])
        self.tok_hg_free = n1
        self.tok_hgc_free = n2
        sqt = self.act(sb["srow"][0:1, 0:NT], ps[4][0:1, 0:NT], func=AF.Sqrt,
                       deps=[n2])
        self.free_bank(4, sqt)
        qm = self.op("dve", lambda e: e.tensor_reduce(
            out=sb["hrow"][0:1, 0:1], in_=sb["srow"][0:1, 0:NT],
            axis=AXX.X, op=ALU.max), [sqt, self.tok_mx_free])
        if g == 0:
            kms = []
            for tq in range(cfg.NTT):
                sqk = self.act(sb["hg"][:, 0:NT],
                               Kf[:, ch * S + tq * NT:ch * S + (tq + 1) * NT],
                               func=AF.Square, deps=a_toks + [self.tok_hg_free])
                m1 = self.mm(4, ps[4][0:1, 0:NT],
                             sb["cones_e"][:, 0:1] if par == 0 else sb["cones_o"][:, 0:1],
                             sb["hg"][:, 0:NT], True, False, deps=[sqk])
                sqk2 = self.act(sb["hg"][:, NT:2 * NT],
                                KR[:, rch * S + tq * NT:rch * S + (tq + 1) * NT],
                                func=AF.Square, deps=a_toks + [self.tok_hgc_free])
                m2 = self.mm(4, ps[4][0:1, 0:NT], sb["cones_q"][:, qch:qch + 1],
                             sb["hg"][:, NT:2 * NT], False, True, deps=[sqk2])
                self.tok_hg_free = m1
                self.tok_hgc_free = m2
                km = self.op("dve", lambda e, tq=tq: e.tensor_reduce(
                    out=sb["hrow"][0:1, 8 + tq:9 + tq], in_=ps[4][0:1, 0:NT],
                    axis=AXX.X, op=ALU.max), [m2])
                self.free_bank(4, km)
                kms.append(km)
            c1 = self.op("dve", lambda e: e.tensor_reduce(
                out=sb["hrow"][0:1, 1:2], in_=sb["hrow"][0:1, 8:8 + cfg.NTT],
                axis=AXX.X, op=ALU.max), kms)
            c2 = self.act(sb["hrow"][0:1, 1:2], sb["hrow"][0:1, 1:2], func=AF.Sqrt,
                          deps=[c1])
            self.tok_kmax = c2
        mx2 = self.op("dve", lambda e: e.tensor_mul(
            out=sb["hrow"][0:1, 3:4], in0=sb["hrow"][0:1, 1:2],
            in1=sb["hrow"][0:1, 0:1]), [self.tok_kmax, qm])
        mxb = self.mm(4, ps[4][:, 0:1], sb["cones_row"][:, :],
                      sb["hrow"][0:1, 3:4], True, True, deps=[mx2])
        mxe = self.act(sb["MXcol"][:, 0:1], ps[4][:, 0:1], scale=-1.0 / (96 ** 0.5),
                       deps=[mxb])
        self.free_bank(4, mxe)
        self.tok_mx_free = mxe
        return qc_, er_, mxe

    def l1_attn_qtile(self, h, g, qt, qc_, er_, mxe, a_toks):
        cfg, sb, ps = self.cfg, self.sb, self.ps
        probs = sb["probs"]
        pTs = sb["pTs"]
        AO = sb["AO"]
        Vf = self.sb_Vf
        Kfl = self.sb_Kf
        KRl = self.sb_KR
        Qg = sb["Qg"]
        Qr = sb["Qr"]
        ccausal = sb["ccausal"]
        S, D, DC, NT, DH = cfg.S, cfg.D, cfg.DC, cfg.NT, cfg.DH
        par, ch = h % 2, h // 2
        rch = h // 4
        qti = g * cfg.QG + qt
        kmax_col = (qti + 1) * 128
        qs_g = Qg[:, qt * 128:(qt + 1) * 128]
        qs_r = Qr[:, qt * 128:(qt + 1) * 128]
        nb = (kmax_col + 511) // 512
        smm = []
        for b in range(nb):
            k0, k1 = b * 512, min(kmax_col, (b + 1) * 512)
            t1 = self.mm(b, ps[b][:, 0:k1 - k0], qs_g,
                         Kfl[:, ch * S + k0: ch * S + k1], True, False,
                         deps=[qc_] + a_toks)
            t2 = self.mm(b, ps[b][:, 0:k1 - k0], qs_r,
                         KRl[:, rch * S + k0: rch * S + k1], False, True,
                         deps=[er_] + a_toks)
            smm.append(t2)
        db = qti * 128 // 512
        doff = qti * 128 - db * 512
        cm = self.op("dve", lambda e, db=db, doff=doff: e.tensor_tensor(
            out=ps[db][:, doff:doff + 128], in0=ps[db][:, doff:doff + 128],
            in1=ccausal[:, :], op=ALU.add), [smm[db]])
        prb_toks = []
        etoks = []
        for b in range(nb):
            k0, k1 = b * 512, min(kmax_col, (b + 1) * 512)
            dep_sc = [cm if b == db else smm[b], mxe] + list(self.tok_probs_free)
            at = self.act(probs[:, k0:k1], ps[b][:, 0:k1 - k0],
                          func=AF.Exp, bias=sb["MXcol"][:, 0:1],
                          scale=1.0 / (96 ** 0.5),
                          accum=sb["accD"][:, 8 + b:9 + b], deps=dep_sc)
            prb_toks.append(at)
            etoks.append(at)
        for b in range(nb):
            self.free_bank(b, prb_toks[-1])
        s1 = None
        if nb == 1:
            s1 = self.op("dve", lambda e: e.tensor_copy(
                out=sb["Scol"][:, 0:1], in_=sb["accD"][:, 8:9]), etoks)
        else:
            s1 = self.op("dve", lambda e: e.tensor_add(
                out=sb["Scol"][:, 0:1], in0=sb["accD"][:, 8:9],
                in1=sb["accD"][:, 9:10]), etoks)
            for b in range(2, nb):
                s1 = self.op("dve", lambda e, b=b: e.tensor_add(
                    out=sb["Scol"][:, 0:1], in0=sb["Scol"][:, 0:1],
                    in1=sb["accD"][:, 8 + b:9 + b]), [s1] + etoks)
        rs = self.op("dve", lambda e: e.reciprocal(
            out=sb["recipS"][:, 0:1], in_=sb["Scol"][:, 0:1]), [s1])
        dR = self.op("dve", lambda e: e.tensor_scalar(
            out=sb["diagR0"][:, :], in0=sb["cI128"][:, :],
            scalar1=sb["recipS"][:, 0:1], scalar2=None, op0=ALU.mult),
            [rs, self.tok_diagR_free[0]])
        nkc = qti + 1
        ev_pt = []
        for bb in range((nkc + 3) // 4):
            bank = 5 + (bb % 2)
            mmp = []
            jmax = min(4, nkc - bb * 4)
            for j in range(jmax):
                kc = bb * 4 + j
                mmp.append(self.mm(bank, ps[bank][:, j * 128:(j + 1) * 128],
                                   probs[:, kc * 128:(kc + 1) * 128],
                                   sb["diagR0"][:, :], True, True,
                                   deps=prb_toks + [dR]))
            epv = self.op("dve", lambda e, bb=bb, bank=bank, jmax=jmax:
                          e.tensor_copy(out=pTs[:, bb * 512:bb * 512 + jmax * 128],
                                        in_=ps[bank][:, 0:jmax * 128]),
                          mmp + [self.tok_pts_free])
            self.free_bank(bank, epv)
            ev_pt.append(epv)
        self.tok_diagR_free[0] = ev_pt[-1]
        self._av_par = getattr(self, "_av_par", 0) ^ 1
        avc = self._av_par * 128
        avdep = self.tok_av_free[self._av_par] or self.pslast[7]
        self.pslast[7] = None
        av = None
        for kc in range(nkc):
            av = self.mm(7, ps[7][0:64, avc:avc + 128],
                         Vf[:, kc * D + h * DH:kc * D + (h + 1) * DH],
                         pTs[:, kc * 128:(kc + 1) * 128],
                         kc == 0, kc == nkc - 1,
                         deps=[ev_pt[kc // 4], avdep if kc == 0 else None] + a_toks)
        ao = self.act(AO[par * 64:par * 64 + 64,
                               ch * NT + qt * 128:ch * NT + (qt + 1) * 128],
                      ps[7][0:64, avc:avc + 128], deps=[av])
        self.tok_av_free[self._av_par] = ao
        self.tok_pts_free = av
        self.tok_probs_free = [ev_pt[-1]]
        return ao

    def l1_phase_b(self, Kf, KR, QC, ccos, csin, a_toks):
        cfg = self.cfg
        self.sb_Kf = Kf
        self.sb_KR = KR
        wq_free = [None, None]
        wo_free = [None, None]
        cos_free = [None]
        for g in range(cfg.NTT):
            col = g * cfg.NT
            cg1 = self.dma("sp", "csb", lambda e, col=col: e.dma_start(
                out=ccos[:, 0:cfg.NT],
                in_=self.dram["cos_rep"][0:32, col:col + cfg.NT]), [cos_free[0]])
            cg2 = self.dma("sp", "csb", lambda e, col=col: e.dma_start(
                out=csin[:, 0:cfg.NT],
                in_=self.dram["sin_rep"][0:32, col:col + cfg.NT]), [cos_free[0]])
            a_toks = a_toks + [cg1, cg2]
            ao_toks = []
            for h in range(cfg.NH):
                qc_, er_, mxe = self.l1_attn_head(h, g, Kf, KR, QC, ccos, csin,
                                                  a_toks, wq_free)
                for qt in range(cfg.QG):
                    ao = self.l1_attn_qtile(h, g, qt, qc_, er_, mxe, a_toks)
                    ao_toks.append(ao)
                self.tok_qg_free = self.fw.last("pe")
                self.tok_qr_free = self.fw.last("pe")
            cos_free[0] = self.fw.last("dve")
            self.oproj_group("Wo1", g, ao_toks, bias_idx=None, wo_free=wo_free)

    def emit_l1(self, seq, deps):
        cfg = self.cfg
        with ExitStack() as c_l1:
            Kf = self.sbuf(c_l1, "Kf1", [128, cfg.DC * cfg.S], BF16)
            Vf = self.sbuf(c_l1, "Vf1", [128, (cfg.S // 128) * cfg.D], BF16)
            KR = self.sbuf(c_l1, "KR1",
                           [128, (cfg.NH * cfg.RC // 128) * cfg.S], BF16)
            QC = self.sbuf(c_l1, "QC1", [128, (cfg.Q_RANK // 128) * cfg.S], BF16)
            cl = []
            with ExitStack() as c_a:
                mrow = self.sbuf(c_a, "mrow1", [1, cfg.NT], F32)
                rrow = self.sbuf(c_a, "rrow1", [1, cfg.NT], F32)
                xng = self.sbuf(c_a, "xng1", [128, cfg.DC * cfg.NT], BF16)
                kvn = self.sbuf(c_a, "kvn", [128, 2 * cfg.NT], BF16)
                self.sbuf(c_a, "lBh", [128, cfg.D], BF16)
                self.sbuf(c_a, "lgr", [8, 2 * cfg.NT], F32)
                self.sbuf(c_a, "mid", [128, cfg.NT], BF16)
                self.sbuf(c_a, "Zt", [128, cfg.NT], BF16)
                self.sbuf(c_a, "gat", [8, cfg.NT], BF16)
                ccos = self.sbuf(c_a, "ccosA", [128, cfg.NT], BF16)
                csin = self.sbuf(c_a, "csinA", [128, cfg.NT], BF16)
                tlb = self.dma("sp", "icb", lambda e: e.dma_start(
                    out=self.sb["evac"][:, 0:cfg.D], in_=self.dram["lBpad"]),
                    deps + [self.tok_evac_free[0], self.tok_evac_free[1]])
                clb = self.act(self.sb["lBh"][:, :], self.sb["evac"][:, 0:cfg.D],
                               scale=2.0, deps=[tlb])
                mm1_ = self.op("dve", lambda e: e.memset(self.sb["mid"][:, :], 0.0),
                               deps)
                mm2_ = self.op("dve", lambda e: e.memset(self.sb["Zt"][:, :], 0.0),
                               deps)
                self.tok_evac_free[0] = clb
                deps = deps + [clb, mm1_, mm2_]
                stat_toks, a_toks = self.l1_phase_a(
                    mrow, rrow, xng, kvn, Kf, Vf, KR, QC, ccos, csin, deps + cl)
            self.sb_Vf = Vf
            with ExitStack() as c_b1:
                self.sbuf(c_b1, "probs", [128, cfg.S], BF16)
                self.sbuf(c_b1, "pTs", [128, cfg.S], BF16)
                self.sbuf(c_b1, "AO", [128, cfg.DC * cfg.NT], BF16)
                self.sbuf(c_b1, "wqs", [128, 4 * 96], F32)
                self.sbuf(c_b1, "wqh", [128, 4 * 96], BF16)
                self.sbuf(c_b1, "ropeq", [32, cfg.NT], BF16)
                self.sbuf(c_b1, "qaug", [96, cfg.NT], BF16)
                self.sbuf(c_b1, "Qg", [128, cfg.NT], BF16)
                self.sbuf(c_b1, "Qr", [128, cfg.NT], BF16)
                self.sbuf(c_b1, "cSELq", [32, 4 * 128], BF16)
                self.sbuf(c_b1, "ccausal", [128, 128], F32)
                ccosB = self.sbuf(c_b1, "ccosG", [32, cfg.NT], BF16)
                csinB = self.sbuf(c_b1, "csinG", [32, cfg.NT], BF16)
                bdep = [self.fw.last("act"), self.fw.last("dve")]
                l1 = self.dma("sp", "mk", lambda e, d=self.sb["cSELq"]:
                              e.dma_start(out=d[:, :], in_=self.dram["SELq"]), bdep)
                l2 = self.dma("sp", "mk", lambda e, d=self.sb["ccausal"]:
                              e.dma_start(out=d[:, :], in_=self.dram["causal"]), bdep)
                mg1_ = self.op("dve", lambda e, q_=self.sb["Qg"]:
                               e.memset(q_[:, :], 0.0), bdep)
                a_toks = a_toks + [l1, l2, mg1_]
                self.l1_phase_b(Kf, KR, QC, ccosB, csinB, a_toks + cl)
            return self.bar()


def Kf_ap(b, ch, k0, k1):
    return b.sb_Kf[:, ch * b.cfg.S + k0: ch * b.cfg.S + k1]


def KR_ap(b, rch, k0, k1):
    return b.sb_KR[:, rch * b.cfg.S + k0: rch * b.cfg.S + k1]


def host_inputs(cfg, x_core, w):
    import ml_dtypes
    c = host_constants(cfg)
    d0 = host_weights_l0(cfg, w)
    d1 = host_weights_l1(cfg, w)
    inp = {}
    inp["x"] = np.ascontiguousarray(x_core.transpose(0, 2, 1))
    bf = ml_dtypes.bfloat16
    for n in ("Mstep", "Mpre", "Ddiag", "ones_e", "ones_o", "ones_q",
              "ones_colh", "cos_rep", "sin_rep", "R32T", "R32Tp", "R128T", "SELq",
              "SELe", "SELr", "I128", "ones_rowh"):
        inp[n] = c[n].astype(bf)
    for n in ("PcT", "ones_row", "ones_col", "causal"):
        inp[n] = c[n].astype(np.float32)
    inp["tkTpad"] = d0["tkTpad"]
    inp["Wq"] = d0["Wq"]; inp["Wk"] = d0["Wk"]; inp["Wv"] = d0["Wv"]
    inp["Wo"] = d0["Wo"]
    for n in ("bq", "bk", "bv", "bo"):
        inp[n] = d0["b" + n[1:]]
    inp["bv_row"] = np.ascontiguousarray(w["L0_attn_bv"].reshape(1, cfg.D))
    inp["tv"] = d0["tv"]; inp["tvT"] = d0["tvT"]
    inp["wgT"] = d0["wgT"]; inp["wuT"] = d0["wuT"]; inp["wdT"] = d0["wdT"]
    inp["Wkv"] = d1["Wkv"]; inp["gkv"] = d1["gkv"]
    inp["Wkdec"] = d1["Wkdec"]; inp["Wvdec"] = d1["Wvdec"]
    inp["Wq1"] = d1["Wq"]; inp["Wkrope"] = d1["Wkrope"]
    inp["Wqaug"] = d1["Wqaug"]; inp["Wo1"] = d1["Wo"]
    inp["bwT"] = d1["bwT"]; inp["bb"] = d1["bb"]
    inp["lA"] = d1["lA"]; inp["lBpad"] = d1["lBpad"]
    inp["rwT"] = d1["rwT"]; inp["rb"] = d1["rb"]
    return inp


_NC_CACHE = {}


def _build_program():
    if "nc" not in _NC_CACHE:
        cfg = Cfg()
        b = B(cfg)
        nc = b.build(layers=(0, 1), nseq=cfg.B_local)
        _NC_CACHE["nc"] = nc
        _NC_CACHE["cfg"] = cfg
    return _NC_CACHE["nc"], _NC_CACHE["cfg"]


def kernel(**inputs):
    nc, cfg = _build_program()
    x = np.asarray(inputs["L0_attn_x"])
    ncore = 8
    shared = host_inputs(cfg, x[0:cfg.B_local], inputs)
    in_maps = []
    for c in range(ncore):
        m = dict(shared)
        m["x"] = np.ascontiguousarray(
            x[c * cfg.B_local:(c + 1) * cfg.B_local].transpose(0, 2, 1))
        in_maps.append(m)
    res = run_bass_kernel_spmd(nc, in_maps, core_ids=list(range(ncore)))
    outs = []
    for c in range(ncore):
        y = np.asarray(res.results[c]["y"])
        outs.append(np.ascontiguousarray(y.transpose(0, 2, 1)))
    return np.concatenate(outs, 0).astype(np.float32)


# version 22 (final)
# speedup vs baseline: 1.1402x; 1.0067x over previous
import sys
sys.path.insert(0, "/opt/trn_rl_repo")
import numpy as np
from dataclasses import dataclass
from contextlib import ExitStack

import concourse.bass as bass
import concourse.mybir as mybir
from concourse.bass_utils import run_bass_kernel_spmd


"""Tiny explicit-semaphore scheduling framework for raw Bass.

Engines: 'pe', 'act', 'dve', 'pool', 'sp' ('sp' = sync HWDGE queue,
'pool' = gpsimd: dma issue + memset etc).

DMA completions within one queue/semaphore are UNORDERED, so each logical
DMA stream gets its own channel (= own semaphore); consumers wait for the
channel's cumulative count snapshotted at consumer-record time. Emitters
must keep each channel either (a) internally chain-serialized via deps, or
(b) grouped with full-group waits and group boundaries ordered through
consumer dependencies.
"""

ENGINES = ("pe", "act", "dve", "pool", "sp")


@dataclass(frozen=True)
class Tok:
    eng: str
    idx: int


class FW:
    def __init__(self, nc):
        self.nc = nc
        self.ops = {e: [] for e in ENGINES}
        self.nops = {e: 0 for e in ENGINES}
        self.ndma = {}
        self.chq = {}
        self.waited = set()
        self._finish = ()

    def _snap_deps(self, deps):
        out = []
        for d in deps:
            if d.eng.startswith("dmach_"):
                ch = d.eng[6:]
                out.append(Tok(d.eng, self.ndma[ch]))
            else:
                out.append(d)
        return tuple(out)

    def op(self, eng, fn, deps=()):
        deps = self._snap_deps(deps)
        self.nops[eng] += 1
        self.ops[eng].append((fn, deps, None, self.nops[eng]))
        t = Tok(eng, self.nops[eng])
        for d in deps:
            self.waited.add(d)
        return t

    def dma(self, q, ch, fn, deps=()):
        deps = self._snap_deps(deps)
        self.ndma[ch] = self.ndma.get(ch, 0) + 1
        assert self.chq.setdefault(ch, q) == q, f"channel {ch} used on two queues"
        self.ops[q].append((fn, deps, ch, None))
        for d in deps:
            self.waited.add(d)
        return Tok("dmach_" + ch, self.ndma[ch])

    def last(self, eng):
        n = self.nops[eng]
        return Tok(eng, n) if n else None

    def set_finish(self, toks):
        self._finish = tuple(self._snap_deps(toks))
        for t in self._finish:
            self.waited.add(t)

    def finalize(self, block, ctx):
        nc = self.nc
        sems = {}
        for e in ENGINES:
            sems[e] = ctx.enter_context(nc.semaphore("s_" + e))
        for ch in self.ndma:
            sems["dmach_" + ch] = ctx.enter_context(nc.semaphore("sd_" + ch))

        inc_idx = {e: set() for e in ENGINES}
        for t in self.waited:
            if t.eng in ENGINES:
                inc_idx[t.eng].add(t.idx)
        inc_val = {}
        for e in ENGINES:
            c = 0
            vals = {}
            for i in range(1, self.nops[e] + 1):
                if i in inc_idx[e]:
                    c += 1
                vals[i] = c
            inc_val[e] = vals

        def tok_sem_val(t):
            if t.eng.startswith("dmach_"):
                return sems[t.eng], 16 * t.idx
            return sems[t.eng], inc_val[t.eng][t.idx]

        def emit_stream(e, engobj, tail=()):
            watermark = {}
            for (fn, deps, ch, cidx) in self.ops[e]:
                for d in deps:
                    sem, val = tok_sem_val(d)
                    if watermark.get(sem.name, -1) < val:
                        engobj.wait_ge(sem, val)
                        watermark[sem.name] = val
                ins = fn(engobj)
                if ch is not None:
                    ins.then_inc(sems["dmach_" + ch], 16)
                elif cidx in inc_idx[e]:
                    ins.then_inc(sems[e], 1)
            for t in tail:
                sem, val = tok_sem_val(t)
                if watermark.get(sem.name, -1) < val:
                    engobj.wait_ge(sem, val)
                    watermark[sem.name] = val

        @block.tensor
        def _(engobj):
            emit_stream("pe", engobj)

        @block.scalar
        def _(engobj):
            emit_stream("act", engobj)

        @block.vector
        def _(engobj):
            emit_stream("dve", engobj)

        @block.gpsimd
        def _(engobj):
            emit_stream("pool", engobj, tail=self._finish)

        @block.sync
        def _(engobj):
            emit_stream("sp", engobj)


"""Megakernel builder for mk_prb_d2 (2-layer parallel-residual transformer).

Layout conventions (per core):
 - residual stream x: feature-major fp32, SBUF tile xr [128, DC*S]
   (DC = D/128 chunks side by side; chunk c at cols [c*S, (c+1)*S)).
 - activations for matmuls: bf16, feature-major.
 - all matmul contractions padded to K=128 partitions (zero-pad constants).
 - per (head, qtile) attention in [d, q] output form.

Sequences: B_local sequences processed sequentially per core.
"""


F32 = mybir.dt.float32
BF16 = mybir.dt.bfloat16
I32 = mybir.dt.int32
AF = mybir.ActivationFunctionType
ALU = mybir.AluOpType
AX = mybir.AxisListType


class Cfg:
    def __init__(self, S=2048, D=1024, NH=16, DH=64, B_local=2,
                 FFN_H=3072, KV_RANK=256, Q_RANK=512, ROPE_DIM=32,
                 MAXREL=4, NT=512):
        self.S, self.D, self.NH, self.DH = S, D, NH, DH
        self.B_local = B_local
        self.FFN_H = FFN_H
        self.KV_RANK, self.Q_RANK, self.ROPE_DIM = KV_RANK, Q_RANK, ROPE_DIM
        self.MAXREL = MAXREL
        self.NT = NT
        self.DC = D // 128
        self.NTT = S // NT
        self.NQT = S // 128
        self.HC = FFN_H // 128
        self.NREL = 2 * MAXREL + 1
        self.W = 17 * 8
        self.RC = ROPE_DIM
        self.QG = NT // 128



def host_constants(cfg):
    c = {}
    W = cfg.W
    ms = np.zeros((128, 8 * W), np.float32)
    for j in range(1, 9):
        for p in range(128):
            for cpp in range(W):
                if cpp - p >= j:
                    ms[p, (j - 1) * W + cpp] = 1.0
    c["Mstep"] = ms
    WP = 131
    mp = np.zeros((128, WP), np.float32)
    for p in range(128):
        mp[p, : min(p + 4, WP)] = 1.0
    c["Mpre"] = mp
    dd = np.zeros((128, 7 * WP), np.float32)
    for oi, o in enumerate(range(-3, 4)):
        for p in range(128):
            j = p + o
            if 0 <= j < WP:
                dd[p, oi * WP + j] = 1.0
    c["Ddiag"] = dd
    Pc = np.zeros((8, 9), np.float32)
    for r in range(7):
        Pc[r, r + 1] = 1.0
        Pc[r, 0] = -1.0
    Pc[7, 8] = 1.0
    Pc[7, 0] = -1.0
    c["PcT"] = np.ascontiguousarray(Pc.T)
    ie = np.zeros((128, 64), np.float32)
    ie[:64] = np.eye(64)
    io = np.zeros((128, 64), np.float32)
    io[64:] = np.eye(64)
    c["Ipad_e"], c["Ipad_o"] = ie, io
    c["ones_row"] = np.ones((1, 128), np.float32)
    oe = np.zeros((128, 1), np.float32); oe[:64] = 1.0
    oo = np.zeros((128, 1), np.float32); oo[64:] = 1.0
    c["ones_e"], c["ones_o"] = oe, oo
    c["ones_col"] = np.ones((128, 1), np.float32)
    cm = np.zeros((128, 128), np.float32)
    for p in range(128):
        cm[p, p + 1:] = -3e38 / 4
    c["causal"] = cm
    rd = cfg.ROPE_DIM
    inv = 1.0 / 10000.0 ** (np.arange(0, rd, 2, dtype=np.float32) / rd)
    t = np.arange(cfg.S, dtype=np.float32)
    fr = np.outer(t, inv)
    emb = np.concatenate([fr, fr], -1)
    cos, sin = np.cos(emb), np.sin(emb)
    nrep = 128 // rd
    c["cos_rep"] = np.ascontiguousarray(np.tile(cos.T, (nrep, 1)))
    c["sin_rep"] = np.ascontiguousarray(np.tile(sin.T, (nrep, 1)))
    R = np.zeros((rd, rd), np.float32)
    h = rd // 2
    for i in range(h):
        R[i, i + h] = -1.0
        R[i + h, i] = 1.0
    c["R32T"] = np.ascontiguousarray(R.T)
    rp = np.zeros((128, rd), np.float32)
    rp[64:64 + rd] = R.T
    c["R32Tp"] = rp
    Rb = np.zeros((128, 128), np.float32)
    for b in range(nrep):
        Rb[b * rd:(b + 1) * rd, b * rd:(b + 1) * rd] = R
    c["R128T"] = np.ascontiguousarray(Rb.T)
    s3 = np.zeros((128, 32), np.float32)
    s3[96:128] = np.eye(32)
    c["SEL3"] = s3
    se = np.zeros((8, 64), np.float32)
    sr = np.zeros((8, 64), np.float32)
    for er in range(64):
        se[er // 8, er] = 1.0
        sr[er % 8, er] = 1.0
    c["SELe"], c["SELr"] = se, sr
    c["I128"] = np.eye(128, dtype=np.float32)
    c["ones_rowh"] = np.ones((1, 128), np.float32)
    c["ones_colh"] = np.ones((128, 1), np.float32)
    oq = np.zeros((128, 4), np.float32)
    for i in range(4):
        oq[32 * i:32 * (i + 1), i] = 1.0
    c["ones_q"] = oq
    sq_ = np.zeros((32, 4 * 128), np.float32)
    for qq in range(4):
        for i in range(32):
            sq_[i, qq * 128 + 32 * qq + i] = 1.0
    c["SELq"] = sq_
    return c


def tile_w(W, mcols, kchunks=None):
    K, M = W.shape
    kchunks = kchunks or K // 128
    mt = M // mcols
    Wr = W.reshape(kchunks, 128, mt, mcols)
    out = np.ascontiguousarray(Wr.transpose(2, 1, 0, 3).reshape(mt * 128,
                                                                kchunks * mcols))
    return out


def host_weights_l0(cfg, w):
    d = {}
    d["Wq"] = tile_w(w["L0_attn_Wq"], cfg.DH)
    d["Wk"] = tile_w(w["L0_attn_Wk"], 128)
    d["Wv"] = w["L0_attn_Wv"]
    d["Wo"] = tile_w(w["L0_attn_Wo"], 128)
    for n in ("q", "k", "v", "o"):
        b = w["L0_attn_b" + n]
        d["b" + n] = np.ascontiguousarray(b.reshape(cfg.DC, 128).T)
    tkT = np.ascontiguousarray(w["L0_attn_table_k"].T)
    nrel = tkT.shape[1]
    tkp = np.zeros((128, 2 * nrel), np.float32)
    dh = tkT.shape[0]
    tkp[:dh, :nrel] = tkT
    tkp[dh:2 * dh, nrel:] = tkT
    d["tkTpad"] = tkp
    d["tv"] = w["L0_attn_table_v"]
    d["tvT"] = np.ascontiguousarray(w["L0_attn_table_v"].T)
    d["wgT"] = tile_w(np.ascontiguousarray(
        w["L0_ffn_w_gate"].T.astype(np.float32)), 128)
    d["wuT"] = tile_w(np.ascontiguousarray(
        w["L0_ffn_w_up"].T.astype(np.float32)), 128)
    d["wdT"] = np.ascontiguousarray(
        w["L0_ffn_w_down"].T.astype(np.float32))
    return d


def host_weights_l1(cfg, w):
    d = {}
    d["Wkv"] = tile_w(w["L1_attn_W_kv"], 128)
    d["gkv"] = np.ascontiguousarray(
        w["L1_attn_g_kv"].reshape(cfg.KV_RANK // 128, 128).T)
    d["Wkdec"] = tile_w(w["L1_attn_W_kdec"], 128)
    d["Wvdec"] = w["L1_attn_W_vdec"]
    d["Wq"] = tile_w(w["L1_attn_W_q"], 128)
    d["Wkrope"] = tile_w(w["L1_attn_W_krope"], 128)
    qd = w["L1_attn_W_qdec"].reshape(cfg.Q_RANK, cfg.NH, cfg.DH)
    qr = w["L1_attn_W_qrope"].reshape(cfg.Q_RANK, cfg.NH, cfg.ROPE_DIM)
    d["Wqaug"] = tile_w(np.ascontiguousarray(
        np.concatenate([qd, qr], axis=2).reshape(
            cfg.Q_RANK, cfg.NH * (cfg.DH + cfg.ROPE_DIM))),
        cfg.DH + cfg.ROPE_DIM)
    d["Wo"] = tile_w(w["L1_attn_W_o"], 128)
    d["bwT"] = tile_w(np.ascontiguousarray(w["L1_ffn_base_w"].T), 128)
    d["bb"] = np.ascontiguousarray(w["L1_ffn_base_b"].reshape(cfg.DC, 128).T)
    d["lA"] = w["L1_ffn_lora_A"]
    lb = w["L1_ffn_lora_B"].reshape(64, cfg.D)
    d["lBpad"] = np.ascontiguousarray(
        np.concatenate([lb, np.zeros((64, cfg.D), np.float32)], axis=0))
    d["rwT"] = np.ascontiguousarray(w["L1_ffn_router_w"].T)
    d["rb"] = w["L1_ffn_router_b"].reshape(8, 1)
    return d


"""Device program builder for the mk_prb_d2 megakernel.

Conventions:
 - residual xr [128, DC*S] fp32 f-major (chunk c at cols [c*S,(c+1)*S)).
 - Kf f-major bf16 (2 heads/chunk); KR f-major bf16 (4 rope-heads/chunk).
 - Vf TOKEN-major bf16: [128, (S/128)*D]; token-chunk tc at cols [tc*D,(tc+1)*D).
 - xn is never stored full: recomputed from mrow/rrow where needed.
 - all accumulating matmuls K=128 (zero-padded operands).
 - attention: [d,q]-form attnV, bound-bias softmax, diagR-normalized pT.
"""


F32 = mybir.dt.float32
BF16 = mybir.dt.bfloat16
I32 = mybir.dt.int32
AF = mybir.ActivationFunctionType
ALU = mybir.AluOpType
AXX = mybir.AxisListType


class B:
    def __init__(self, cfg: Cfg, debug_dumps=()):
        self.cfg = cfg
        self.debug = set(debug_dumps)
        self.nc = bass.Bass("TRN2", target_bir_lowering=False)
        self.fw = FW(self.nc)
        self.dram = {}
        self.sb = {}
        self.ps = []
        self.pslast = [None] * 8
        self._dbg_toks = []

    def dram_in(self, name, shape, dtype=F32):
        self.dram[name] = self.nc.dram_tensor(name, list(shape), dtype,
                                              kind="ExternalInput").ap()

    def dram_out(self, name, shape, dtype=F32):
        self.dram[name] = self.nc.dram_tensor(name, list(shape), dtype,
                                              kind="ExternalOutput").ap()

    _name_ctr = 0

    def sbuf(self, ctx, name, shape, dtype):
        B._name_ctr += 1
        t = ctx.enter_context(self.nc.sbuf_tensor(f"{name}__{B._name_ctr}",
                                                  list(shape), dtype))
        self.sb[name] = t
        return t

    def op(self, eng, fn, deps=()):
        return self.fw.op(eng, fn, [d for d in deps if d is not None])

    def dma(self, q, ch, fn, deps=()):
        return self.fw.dma(q, ch, fn, [d for d in deps if d is not None])

    def mm(self, bank, out_slice, lhsT, rhs, start, stop, deps=()):
        d = [x for x in deps if x is not None]
        if start and self.pslast[bank] is not None:
            d.append(self.pslast[bank])
        return self.fw.op("pe", lambda e: e.matmul(out=out_slice, lhsT=lhsT,
                                                   rhs=rhs, start=start, stop=stop), d)

    def free_bank(self, bank, tok):
        self.pslast[bank] = tok

    def act(self, out, in_, func=AF.Copy, bias=None, scale=None, accum=None, deps=()):
        kw = {}
        if bias is not None:
            kw["bias"] = bias
        if scale is not None:
            kw["scale"] = scale
        if accum is not None:
            kw["accum_out"] = accum
        return self.op("act", lambda e: e.activation(out=out, in_=in_, func=func, **kw),
                       deps)

    def dbg(self, name, src_ap, deps, shape, dtype=F32):
        if name not in self.debug:
            return
        key = "dbg_" + name
        if key not in self.dram:
            self.dram_out(key, shape, dtype)
        t = self.dma("pool", "dbg", lambda e: e.dma_start(out=self.dram[key], in_=src_ap), deps)
        self._dbg_toks.append(t)

    def bar(self):
        d = []
        for e in ("pe", "act", "dve", "pool", "sp"):
            t = self.fw.last(e)
            if t:
                d.append(t)
        for ch, n in self.fw.ndma.items():
            if n:
                d.append(Tok("dmach_" + ch, n))
        return d

    def declare(self):
        cfg = self.cfg
        S, D, DC = cfg.S, cfg.D, cfg.DC
        self.dram_in("x", [cfg.B_local, D, S])
        self.dram_out("y", [cfg.B_local, D, S])
        for n, sh in [("Mstep", (128, 8 * cfg.W)), ("Mpre", (128, 131)),
                      ("Ddiag", (128, 7 * 131)),
                      ("ones_e", (128, 1)), ("ones_o", (128, 1)),
                      ("ones_q", (128, 4)), ("ones_colh", (128, 1)),
                      ("cos_rep", (128, S)), ("sin_rep", (128, S)),
                      ("R32T", (cfg.RC, cfg.RC)), ("R32Tp", (128, cfg.RC)), ("R128T", (128, 128)),
                      ("SELq", (32, 4 * 128)), ("SELe", (8, 64)), ("SELr", (8, 64)),
                      ("I128", (128, 128)), ("ones_rowh", (1, 128))]:
            self.dram_in(n, sh, BF16)
        for n, sh in [("PcT", (9, 8)), ("ones_row", (1, 128)),
                      ("ones_col", (128, 1)), ("causal", (128, 128))]:
            self.dram_in(n, sh)
        self.dram_in("tkTpad", (128, 2 * cfg.NREL))
        self.dram["dscr"] = self.nc.dram_tensor(
            "dscr", [128, DC * S], BF16, kind="Internal").ap()
        self.dram_in("Wq", [cfg.NH * 128, DC * cfg.DH])
        self.dram_in("Wk", [DC * 128, DC * 128])
        self.dram_in("Wv", [D, D]); self.dram_in("Wo", [DC * 128, DC * 128])
        for n in ("bq", "bk", "bv", "bo"):
            self.dram_in(n, [128, DC])
        self.dram_in("bv_row", [1, D])
        self.dram_in("tv", [cfg.NREL, cfg.DH]); self.dram_in("tvT", [cfg.DH, cfg.NREL])
        self.dram_in("wgT", [cfg.HC * 128, DC * 128])
        self.dram_in("wuT", [cfg.HC * 128, DC * 128])
        self.dram_in("wdT", [cfg.FFN_H, D])
        KV, QR = cfg.KV_RANK, cfg.Q_RANK
        self.dram_in("Wkv", [(KV // 128) * 128, DC * 128])
        self.dram_in("gkv", [128, max(1, KV // 128)])
        self.dram_in("Wkdec", [DC * 128, (KV // 128) * 128])
        self.dram_in("Wvdec", [KV, D])
        self.dram_in("Wq1", [(QR // 128) * 128, DC * 128])
        self.dram_in("Wkrope", [(cfg.NH * cfg.RC // 128) * 128, DC * 128])
        self.dram_in("Wqaug", [cfg.NH * 128, (QR // 128) * (cfg.DH + cfg.RC)])
        self.dram_in("Wo1", [DC * 128, DC * 128])
        self.dram_in("bwT", [DC * 128, DC * 128]); self.dram_in("bb", [128, DC])
        self.dram_in("lA", [D, 8]); self.dram_in("lBpad", [128, D])
        self.dram_in("rwT", [D, 8]); self.dram_in("rb", [8, 1])

    def alloc_persistent(self, ctx):
        cfg = self.cfg
        D, DC = cfg.D, cfg.DC
        sb = lambda n, s, d: self.sbuf(ctx, n, s, d)
        sb("xr", [128, DC * cfg.S], F32)
        sb("cones_e", [128, 1], BF16); sb("cones_o", [128, 1], BF16)
        sb("cones_q", [128, 4], BF16); sb("cones_colh", [128, 1], BF16)
        sb("cR32T", [cfg.RC, cfg.RC], BF16); sb("cR32Tp", [128, cfg.RC], BF16)
        sb("cR128T", [128, 128], BF16)
        sb("cSELe", [8, 64], BF16); sb("cSELr", [8, 64], BF16)
        sb("cI128", [128, 128], BF16); sb("cones_rowh", [1, 128], BF16)
        sb("cones_row", [1, 128], F32); sb("cones_col", [128, 1], F32)
        sb("cPcT", [cfg.NREL, 8], F32)
        sb("ctv", [cfg.NREL, cfg.DH], F32); sb("ctvT", [cfg.DH, cfg.NREL], F32)
        sb("ctkT", [128, 2 * cfg.NREL], BF16)
        sb("biasT", [128, 8 * DC], F32)
        sb("cgkv", [128, max(1, cfg.KV_RANK // 128)], F32)
        sb("crb", [8, 1], F32)
        sb("lAh", [128, DC * 8], BF16); sb("rwh", [128, DC * 8], BF16)
        sb("QRt_0", [128, 16], F32)
        sb("QRt_1", [128, 16], F32)
        sb("Ucols", [128, 8], F32)
        sb("accD", [128, 24], F32)
        sb("D8_0", [128, 8], BF16)
        sb("D8_1", [128, 8], BF16)
        sb("D8Tp0", [128, 128], BF16)
        sb("D8Tp1", [128, 128], BF16)
        sb("Scol", [128, 1], F32); sb("recipS", [128, 1], F32)
        sb("diagR0", [128, 128], BF16)
        sb("diagR1", [128, 128], BF16)
        sb("bias1_0", [128, 1], F32); sb("bias2_0", [128, 1], F32)
        sb("bias1_1", [128, 1], F32); sb("bias2_1", [128, 1], F32)
        sb("MXcol", [128, 1], F32)
        sb("TVmodp", [128, cfg.DH], BF16)
        sb("hrow", [1, 16], F32)
        sb("epsrow", [1, 2], F32)
        sb("vrow", [1, cfg.NT], F32)
        sb("srow", [1, cfg.NT], F32)
        sb("evac", [128, 2 * cfg.NT], F32)
        sb("sq", [128, 2 * cfg.NT], F32)
        sb("hg", [128, 2 * cfg.NT], BF16)

    def alloc_psum(self, ctx):
        for i in range(8):
            self.ps.append(ctx.enter_context(
                self.nc.psum_tensor(f"pb{i}", [128, 512], F32)))

    def emit_init(self):
        cfg, sb, dr = self.cfg, self.sb, self.dram
        ltoks = []

        def ld(dst_ap, src):
            t = self.dma("sp", "ic", lambda e, d=dst_ap, s=dr[src]: e.dma_start(out=d, in_=s))
            ltoks.append(t)
            return t

        for dst, src in [("cones_e", "ones_e"), ("cones_o", "ones_o"),
                         ("cones_q", "ones_q"), ("cones_colh", "ones_colh"),
                         ("cR32T", "R32T"), ("cR32Tp", "R32Tp"), ("cR128T", "R128T"),
                         ("cSELe", "SELe"), ("cSELr", "SELr"), ("cI128", "I128"),
                         ("cones_rowh", "ones_rowh"), ("cones_row", "ones_row"),
                         ("cones_col", "ones_col"),
                         ("cPcT", "PcT"), ("ctv", "tv"), ("ctvT", "tvT"),
                         ("cgkv", "gkv"), ("crb", "rb")]:
            ld(sb[dst][:, :], src)
        DC = cfg.DC
        for i, n in enumerate(("bq", "bk", "bv", "bo", "bb")):
            ld(sb["biasT"][:, i * DC:(i + 1) * DC], n)
        t1 = self.dma("sp", "ict", lambda e: e.dma_start(
            out=sb["evac"][:, 0:2 * cfg.NREL], in_=dr["tkTpad"]))
        ltoks.append(t1)
        c1 = self.act(sb["ctkT"][:, :], sb["evac"][:, 0:2 * cfg.NREL], deps=[t1])
        lr = []
        srcA = dr["lA"].rearrange("(kc r) c -> r kc c", r=128)
        dstA = sb["sq"][:, 0:DC * 8].rearrange("p (kc c) -> p kc c", c=8)
        lr.append(self.dma("sp", "icl", lambda e: e.dma_start(out=dstA, in_=srcA)))
        srcR = dr["rwT"].rearrange("(kc r) c -> r kc c", r=128)
        dstR = sb["sq"][:, 512:512 + DC * 8].rearrange("p (kc c) -> p kc c", c=8)
        lr.append(self.dma("sp", "icl", lambda e: e.dma_start(out=dstR, in_=srcR)))
        c3 = self.act(sb["lAh"][:, :], sb["sq"][:, 0:DC * 8], deps=lr)
        c4 = self.act(sb["rwh"][:, :], sb["sq"][:, 512:512 + DC * 8], deps=lr)
        c2 = c5 = c4
        m1a = self.op("dve", lambda e: e.memset(sb["D8Tp0"][:, :], 0.0))
        m1 = self.op("dve", lambda e: e.memset(sb["D8Tp1"][:, :], 0.0), [m1a])
        m4 = self.op("dve", lambda e: e.memset(sb["TVmodp"][:, :], 0.0))
        m5 = self.op("dve", lambda e: e.memset(sb["epsrow"][0:1, 0:1], 1e-5))
        m5b = self.op("dve", lambda e: e.memset(sb["epsrow"][0:1, 1:2], 1e-6), [m5])
        m6 = m1
        m7 = m1

        tm = self.mm(7, self.ps[7][0:8, 0:cfg.DH], sb["cPcT"][:, :], sb["ctv"][:, :],
                     True, True, deps=ltoks)
        evt = self.act(sb["TVmodp"][0:8, :], self.ps[7][0:8, 0:cfg.DH], deps=[tm, m4])
        self.free_bank(7, evt)
        sq1 = self.act(sb["sq"][:, 0:2 * cfg.NREL], sb["ctkT"][:, :], func=AF.Square,
                       deps=[c1, c3, c4, c5])
        mm1 = self.mm(7, self.ps[7][0:1, 0:cfg.NREL], sb["cones_col"][:, :],
                      sb["sq"][:, 0:cfg.NREL], True, True, deps=[sq1])
        r1 = self.op("dve", lambda e: e.tensor_reduce(
            out=sb["hrow"][0:1, 2:3], in_=self.ps[7][0:1, 0:cfg.NREL],
            axis=AXX.X, op=ALU.max), [mm1])
        self.free_bank(7, r1)
        r2 = self.act(sb["hrow"][0:1, 2:3], sb["hrow"][0:1, 2:3], func=AF.Sqrt,
                      deps=[r1])
        self.init_toks = ltoks + [c1, c3, c4, m1, m4, m5b, m6, m7, evt, r2]

    def load_x(self, seq, deps):
        cfg, sb = self.cfg, self.sb
        src = self.dram["x"][seq].rearrange("(c r) t -> r c t", r=128)
        dst = sb["xr"][:, :].rearrange("p (c t) -> p c t", t=cfg.S)
        return [self.dma("sp", "xl", lambda e: e.dma_start(out=dst, in_=src), deps)]

    def store_y(self, seq, deps):
        cfg, sb = self.cfg, self.sb
        dst = self.dram["y"][seq].rearrange("(c r) t -> r c t", r=128)
        src = sb["xr"][:, :].rearrange("p (c t) -> p c t", t=cfg.S)
        return [self.dma("sp", "xl", lambda e: e.dma_start(out=dst, in_=src), deps)]

    def ln_stats(self, mrow, rrow, eps_idx, deps):
        out = []
        for t in range(self.cfg.NTT):
            out += self.ln_stats_one(mrow, rrow, eps_idx, t, t * self.cfg.NT, deps)
        return out

    def ln_stats_one(self, mrow, rrow, eps_idx, t, dcol, deps):
        cfg, sb, ps = self.cfg, self.sb, self.ps
        S, DC, NT, D = cfg.S, cfg.DC, cfg.NT, cfg.D
        out = []
        if True:
            col = t * NT
            mm_m = mm_v = None
            sq_free = {}
            for c in range(DC):
                xs = sb["xr"][:, c * S + col: c * S + col + NT]
                mm_m = self.mm(0, ps[0][0:1, 0:NT], sb["cones_col"][:, :], xs,
                               c == 0, c == DC - 1, deps=deps)
                par = c % 2
                sqs = sb["sq"][:, par * NT:(par + 1) * NT]
                sqt = self.act(sqs, xs, func=AF.Square,
                               deps=deps + [sq_free.get(par)])
                mm_v = self.mm(1, ps[1][0:1, 0:NT], sb["cones_col"][:, :], sqs,
                               c == 0, c == DC - 1, deps=[sqt])
                sq_free[par] = mm_v
            mt = self.act(mrow[0:1, dcol:dcol + NT], ps[0][0:1, 0:NT],
                          scale=1.0 / D, deps=[mm_m])
            m2t = self.act(sb["srow"][0:1, 0:NT], ps[0][0:1, 0:NT], func=AF.Square,
                           scale=1.0 / D, deps=[mm_m])
            self.free_bank(0, m2t)
            vt = self.op("dve", lambda e: e.scalar_tensor_tensor(
                out=sb["vrow"][0:1, 0:NT], in0=ps[1][0:1, 0:NT], scalar=1.0 / D,
                in1=sb["srow"][0:1, 0:NT], op0=ALU.mult, op1=ALU.subtract),
                [mm_v, m2t])
            self.free_bank(1, vt)
            st = self.act(sb["srow"][0:1, 0:NT], sb["vrow"][0:1, 0:NT], func=AF.Sqrt,
                          bias=sb["epsrow"][0:1, eps_idx:eps_idx + 1], deps=[vt])
            rt0 = self.op("dve", lambda e: e.reciprocal(
                out=sb["vrow"][0:1, 0:NT], in_=sb["srow"][0:1, 0:NT]), [st])
            rt = self.op("dve", lambda e, dcol=dcol: e.tensor_copy(
                out=rrow[0:1, dcol:dcol + NT], in_=sb["vrow"][0:1, 0:NT]), [rt0])
            out += [mt, rt]
        return out

    def xn_tile(self, xn_of_c, mrow, rrow, t, deps, bm=2, br=3, stats_col=None):
        cfg, sb, ps = self.cfg, self.sb, self.ps
        S, DC, NT = cfg.S, cfg.DC, cfg.NT
        col = t * NT
        scol = col if stats_col is None else stats_col
        ones = sb["cones_row"] if str(mrow.dtype).endswith("float32") else sb["cones_rowh"]
        bmt = self.mm(bm, ps[bm][:, 0:NT], ones[:, :],
                      mrow[0:1, scol:scol + NT], True, True, deps=deps)
        brt = self.mm(br, ps[br][:, 0:NT], ones[:, :],
                      rrow[0:1, scol:scol + NT], True, True, deps=deps)
        toks = []
        for c in range(DC):
            par = c % 2
            ev = sb["evac"][:, par * NT:(par + 1) * NT]
            xs = sb["xr"][:, c * S + col: c * S + col + NT]
            o1 = self.op("dve", lambda e, ev=ev, xs=xs: e.tensor_sub(
                out=ev, in0=xs, in1=ps[bm][:, 0:NT]),
                [bmt, self.tok_evac_free[par]] + deps)
            o2 = self.op("dve", lambda e, ev=ev, c=c: e.tensor_mul(
                out=xn_of_c(c), in0=ev, in1=ps[br][:, 0:NT]), [o1, brt])
            self.tok_evac_free[par] = o2
            toks.append(o2)
        self.free_bank(bm, toks[-1])
        self.free_bank(br, toks[-1])
        return toks

    def stage_cols(self, wname, mcol0, mcols, kchunks, free_dep, sq_off=0, ch=None):
        sb = self.sb
        ch = ch or ("w_" + wname)
        fd = list(free_dep) if isinstance(free_dep, list) else [free_dep]
        fd += [self.tok_hg_free, self.tok_hg_free2, self.tok_hgc_free]
        m = mcol0 // mcols
        src = self.dram[wname][m * 128:(m + 1) * 128, 0:kchunks * mcols]
        dst = sb["sq"][:, sq_off:sq_off + kchunks * mcols]
        if ch in self.fw.ndma:
            fd = fd + [Tok("dmach_" + ch, 0)]
        dt = self.dma("sp", ch, lambda e: e.dma_start(out=dst, in_=src), fd)
        cst = self.act(sb["hg"][:, sq_off:sq_off + kchunks * mcols],
                       sb["sq"][:, sq_off:sq_off + kchunks * mcols],
                       deps=[dt] + fd)
        return cst

    def v_tm_tile(self, wrows_name, xng, t, kchunks, n_banks_cols, bias_row,
                  wstage_f, wstage_h, wfree, src_toks, dst_col0):
        cfg, sb = self.cfg, self.sb
        D, NT = cfg.D, cfg.NT
        ntc = NT // 128
        nn = n_banks_cols
        mms = {}
        cstp = wfree[0]
        last_mm = wfree[1]
        for kc in range(kchunks):
            dt = self.dma("sp", "w_" + wrows_name, lambda e, kc=kc: e.dma_start(
                out=wstage_f[:, 0:D],
                in_=self.dram[wrows_name][kc * 128:(kc + 1) * 128, :]), [cstp])
            cst = self.act(wstage_h[:, 0:D], wstage_f[:, 0:D],
                           deps=[dt, last_mm])
            cstp = cst
            for tc in range(ntc):
                for n2 in range(nn):
                    bank = tc * nn + n2
                    mms[(tc, n2)] = self.mm(
                        bank, self.ps[bank][:, 0:512],
                        xng[:, kc * NT + tc * 128: kc * NT + (tc + 1) * 128],
                        wstage_h[:, n2 * 512:(n2 + 1) * 512],
                        kc == 0, False, deps=[cst] + src_toks)
                    last_mm = mms[(tc, n2)]
        wfree[0] = cstp
        evs = []
        for tc in range(ntc):
            for n2 in range(nn):
                bank = tc * nn + n2
                bmm = self.mm(bank, self.ps[bank][:, 0:512],
                              sb["cones_rowh"][0:1, 0:128],
                              bias_row[0:1, n2 * 512:(n2 + 1) * 512],
                              False, True, deps=[mms[(tc, n2)]]) if bias_row is not None \
                    else self.mm(bank, self.ps[bank][:, 0:512],
                                 sb["cones_rowh"][0:1, 0:1].broadcast_to((1, 1)),
                                 None, False, True)
                ev = self.act(sb["Vf"][:, dst_col0 + (t * ntc + tc) * D + n2 * 512:
                                       dst_col0 + (t * ntc + tc) * D + (n2 + 1) * 512],
                              self.ps[bank][:, 0:512], deps=[bmm])
                self.free_bank(bank, ev)
                evs.append(ev)
        wfree[1] = evs[-1]
        return evs

    def l0_phase_a(self, mrow, rrow, xng, deps):
        cfg, sb = self.cfg, self.sb
        S, D, DC, NT = cfg.S, cfg.D, cfg.DC, cfg.NT
        ktoks, vtoks = [], []
        kfree = [None, None]
        prev_stats = list(deps)
        for t in range(cfg.NTT):
            stat_toks = self.ln_stats_one(mrow, rrow, 0, t, 0, deps + prev_stats)
            if kfree[0] is None:
                kfree = [stat_toks[-1], stat_toks[-1]]
            xnt = self.xn_tile(lambda c: xng[:, c * NT:(c + 1) * NT],
                               mrow, rrow, t, stat_toks, stats_col=0)
            prev_stats = stat_toks + xnt
            vtoks += self.v_tm_tile("Wv", xng, t, DC, D // 512, sb["bvh"],
                                    sb["sq"], sb["hg"], kfree, xnt, 0)
            for m in range(DC):
                cst = self.stage_cols("Wk", m * 128, 128, DC, [kfree[0], kfree[1]])
                kfree[0] = cst
                bk = m % 2
                mmk = None
                for kc in range(DC):
                    mmk = self.mm(bk, self.ps[bk][:, 0:NT],
                                  sb["hg"][:, kc * 128:(kc + 1) * 128],
                                  xng[:, kc * NT:(kc + 1) * NT],
                                  kc == 0, kc == DC - 1, deps=[cst] + xnt)
                ek = self.act(sb["Kf"][:, m * S + t * NT: m * S + (t + 1) * NT],
                              self.ps[bk][:, 0:NT], func=AF.Identity,
                              bias=sb["biasT"][:, DC + m:DC + m + 1], deps=[mmk])
                self.free_bank(bk, ek)
                kfree[0] = cst
                kfree[1] = ek
                ktoks.append(ek)
        return prev_stats, ktoks + vtoks

    def l0_attn_head(self, h, g, xng, xn_toks, kv_toks, wq_free):
        cfg, sb, ps = self.cfg, self.sb, self.ps
        Kf = sb["Kf"]
        wqs = sb["wqs"]
        wqh = sb["wqh"]
        qrel_s = sb["qrel_s"]
        Qg = sb["Qg"]
        S, D, DC, NT, DH = cfg.S, cfg.D, cfg.DC, cfg.NT, cfg.DH
        NREL, W = cfg.NREL, cfg.W
        par, ch = h % 2, h // 2
        fw = self.fw
        dts = [self.dma("sp", "w_Wq", lambda e: e.dma_start(
            out=wqs[:, 0:DC * DH],
            in_=self.dram["Wq"][h * 128:(h + 1) * 128, :]),
            [wq_free[0]])]
        cst = self.act(wqh[:, 0:DC * DH], wqs[:, 0:DC * DH],
                       deps=dts + [wq_free[1]])
        wq_free[0] = cst
        ms = self.op("dve", lambda e: e.memset(
            Qg[(1 - par) * 64:(1 - par) * 64 + 64, :], 0.0),
            [self.tok_qg_free])
        mmq = None
        for kc in range(DC):
            mmq = self.mm(4, ps[4][par * 64:par * 64 + 64, 0:NT],
                          wqh[:, kc * DH:(kc + 1) * DH],
                          xng[:, kc * NT:(kc + 1) * NT],
                          kc == 0, kc == DC - 1, deps=[cst] + xn_toks)
        eq = self.act(Qg[par * 64:par * 64 + 64, :],
                      ps[4][par * 64:par * 64 + 64, 0:NT], func=AF.Identity,
                      bias=sb["biasT"][par * 64:par * 64 + 64, ch:ch + 1],
                      deps=[mmq, ms, self.tok_qg_free])
        self.free_bank(4, eq)
        wq_free[1] = mmq
        mmr = self.mm(4, ps[4][0:NREL, 0:NT],
                      sb["ctkT"][:, par * NREL:(par + 1) * NREL],
                      Qg[:, 0:NT], True, True, deps=[eq])
        erl = self.act(qrel_s[0:NREL, 0:NT], ps[4][0:NREL, 0:NT], deps=[mmr])
        self.free_bank(4, erl)
        sqq = self.act(sb["hg"][:, 0:NT], Qg[:, :], func=AF.Square,
                       deps=[eq, self.tok_hg_free])
        mmn = self.mm(4, ps[4][0:1, 0:NT], sb["cones_colh"][:, 0:1],
                      sb["hg"][:, 0:NT], True, True, deps=[sqq])
        self.tok_hg_free = mmn
        sqt = self.act(sb["srow"][0:1, 0:NT], ps[4][0:1, 0:NT], func=AF.Sqrt,
                       deps=[mmn])
        self.free_bank(4, sqt)
        qm = self.op("dve", lambda e: e.tensor_reduce(
            out=sb["hrow"][0:1, 0:1], in_=sb["srow"][0:1, 0:NT],
            axis=AXX.X, op=ALU.max), [sqt, self.tok_mx_free])
        if g == 0:
            kms = []
            for tq in range(cfg.NTT):
                sqk = self.act(sb["hg"][:, 0:NT],
                               Kf[:, ch * S + tq * NT: ch * S + (tq + 1) * NT],
                               func=AF.Square, deps=kv_toks + [self.tok_hg_free])
                mmk = self.mm(4, ps[4][0:1, 0:NT],
                              sb["cones_e"][:, 0:1] if par == 0 else sb["cones_o"][:, 0:1],
                              sb["hg"][:, 0:NT], True, True, deps=[sqk])
                self.tok_hg_free = mmk
                km = self.op("dve", lambda e, tq=tq: e.tensor_reduce(
                    out=sb["hrow"][0:1, 8 + tq:9 + tq], in_=ps[4][0:1, 0:NT],
                    axis=AXX.X, op=ALU.max), [mmk])
                self.free_bank(4, km)
                kms.append(km)
            c1 = self.op("dve", lambda e: e.tensor_reduce(
                out=sb["hrow"][0:1, 1:2], in_=sb["hrow"][0:1, 8:8 + cfg.NTT],
                axis=AXX.X, op=ALU.max), kms)
            c2 = self.act(sb["hrow"][0:1, 1:2], sb["hrow"][0:1, 1:2], func=AF.Sqrt,
                          deps=[c1])
            self.tok_kmax = c2
        mx1 = self.op("dve", lambda e: e.tensor_add(
            out=sb["hrow"][0:1, 3:4], in0=sb["hrow"][0:1, 1:2],
            in1=sb["hrow"][0:1, 2:3]), [self.tok_kmax, qm])
        mx2 = self.op("dve", lambda e: e.tensor_mul(
            out=sb["hrow"][0:1, 3:4], in0=sb["hrow"][0:1, 3:4],
            in1=sb["hrow"][0:1, 0:1]), [mx1])
        mxb = self.mm(4, ps[4][:, 0:1], sb["cones_row"][:, :],
                      sb["hrow"][0:1, 3:4], True, True, deps=[mx2])
        mxe = self.act(sb["MXcol"][:, 0:1], ps[4][:, 0:1], deps=[mxb])
        self.free_bank(4, mxe)
        self.tok_mx_free = mxe
        return eq, erl, mxe

    def l0_attn_fa_qr(self, h, g, qt, eq, erl, mxe, kv_toks, pp):
        cfg, sb, ps = self.cfg, self.sb, self.ps
        cMstep = sb["cMstep"]
        qrel_s = sb["qrel_s"]
        bandp = sb["bandp_" + pp]
        S, NREL, W = cfg.S, cfg.NREL, cfg.W
        qti = g * cfg.QG + qt
        q0 = qti * 128
        nk = S // 512
        tq = self.mm(4, ps[4][:, 0:NREL], qrel_s[0:NREL, qt * 128:(qt + 1) * 128],
                     sb["cI128"][0:NREL, 0:NREL], True, True, deps=[erl])
        eqr = self.act(sb["QRt_" + pp][:, 0:NREL], ps[4][:, 0:NREL],
                       deps=[tq, self.tok_qrt_free[int(pp)]])
        self.free_bank(4, eqr)
        uc = self.op("dve", lambda e: e.tensor_scalar(
            out=sb["Ucols"][:, 0:8], in0=sb["QRt_" + pp][:, 1:NREL],
            scalar1=sb["QRt_" + pp][:, 0:1], scalar2=None, op0=ALU.subtract),
            [eqr, self.tok_uc_free])
        b1 = self.op("dve", lambda e: e.tensor_scalar(
            out=sb["bias1_" + pp][:, 0:1], in0=sb["QRt_" + pp][:, 0:1],
            scalar1=sb["MXcol"][:, 0:1], scalar2=1.0 / 8.0,
            op0=ALU.subtract, op1=ALU.mult),
            [eqr, mxe, self.tok_b1_free[int(pp)]])
        b2 = self.op("dve", lambda e: e.tensor_scalar(
            out=sb["bias2_" + pp][:, 0:1], in0=sb["QRt_" + pp][:, NREL - 1:NREL],
            scalar1=sb["MXcol"][:, 0:1], scalar2=1.0 / 8.0,
            op0=ALU.subtract, op1=ALU.mult),
            [eqr, mxe, self.tok_b2_free[int(pp)]])
        self.tok_qrt_free[int(pp)] = b2
        btoks = []
        for half in range(2):
            prev = self.tok_bandp_free[int(pp)]
            for jj in range(4):
                j = half * 4 + jj + 1
                dst = bandp[:, half * W:(half + 1) * W]
                if jj == 0:
                    tt = self.op("dve", lambda e, j=j, dst=dst: e.tensor_scalar(
                        out=dst, in0=cMstep[:, (j - 1) * W:j * W],
                        scalar1=sb["Ucols"][:, j - 1:j], scalar2=None,
                        op0=ALU.mult), [uc, prev])
                else:
                    tt = self.op("dve", lambda e, j=j, dst=dst:
                                 e.scalar_tensor_tensor(
                                     out=dst, in0=cMstep[:, (j - 1) * W:j * W],
                                     scalar=sb["Ucols"][:, j - 1:j], in1=dst,
                                     op0=ALU.mult, op1=ALU.add), [uc, tt])
                prev = tt
            btoks.append(tt)
        wk0 = max(q0 - 4, 0)
        wk1 = min(q0 + 132, S)
        self.tok_uc_free = btoks[-1]
        return dict(h=h, g=g, qt=qt, pp=pp, eq=eq, b1=b1, b2=b2,
                    q0=q0, wk0=wk0, wk1=wk1, nk=nk, btoks=btoks)

    def l0_attn_fa_sc(self, st, kv_toks):
        cfg, sb, ps = self.cfg, self.sb, self.ps
        h, g, qt = st["h"], st["g"], st["qt"]
        Kf = sb["Kf"]
        bandp = sb["bandp_" + st["pp"]]
        Qg = sb["Qg"]
        S, W = cfg.S, cfg.W
        ch = h // 2
        q0, wk0, wk1, nk = st["q0"], st["wk0"], st["wk1"], st["nk"]
        eq, btoks = st["eq"], st["btoks"]
        qs = Qg[:, qt * 128:(qt + 1) * 128]
        smm = []
        for b in range(nk):
            t = self.mm(b, ps[b][:, 0:512], qs,
                        Kf[:, ch * S + b * 512: ch * S + (b + 1) * 512],
                        True, True, deps=[eq] + kv_toks)
            smm.append(t)
        badds = []
        kk = wk0
        while kk < wk1:
            b = kk // 512
            ke = min(wk1, (b + 1) * 512)
            off = kk - (q0 - 4)
            t = self.op("dve", lambda e, b=b, kk=kk, ke=ke, off=off:
                        e.tensor_tensor(
                            out=ps[b][:, kk - b * 512:ke - b * 512],
                            in0=ps[b][:, kk - b * 512:ke - b * 512],
                            in1=bandp[:, off:off + ke - kk],
                            op=ALU.add), btoks + [smm[b]])
            t2 = self.op("dve", lambda e, b=b, kk=kk, ke=ke, off=off:
                         e.tensor_tensor(
                             out=ps[b][:, kk - b * 512:ke - b * 512],
                             in0=ps[b][:, kk - b * 512:ke - b * 512],
                             in1=bandp[:, W + off:W + off + ke - kk],
                             op=ALU.add), [t])
            badds.append((b, t2))
            kk = ke
        self.tok_bandp_free[int(st["pp"])] = badds[-1][1]
        st.update(smm=smm, badd=dict(badds))
        return st

    def l0_attn_fb_act(self, st):
        cfg, sb, ps = self.cfg, self.sb, self.ps
        pp = st["pp"]
        probs = sb["probs" + pp]
        S = cfg.S
        q0, wk1, nk = st["q0"], st["wk1"], st["nk"]
        smm, badd_of_bank, b1, b2 = st["smm"], st["badd"], st["b1"], st["b2"]
        ranges = []
        if q0 > 0:
            ranges.append((0, q0, "bias1_" + pp, 8))
        if wk1 < S:
            ranges.append((wk1, S, "bias2_" + pp, 10))
        ranges.append((q0, wk1, "bias1_" + pp, 9))
        prb_toks = []
        bank_exp = {}
        acc_cols = []
        acc_base = 9 + 7 * int(pp)
        acc_n = acc_base
        for (ka, kb, bn, acc_i) in ranges:
            kk = ka
            while kk < kb:
                b = kk // 512
                ke = min(kb, (b + 1) * 512)
                dep_sc = [badd_of_bank.get(b, smm[b]), b1, b2] + \
                    list(self.tok_probs_free)
                at = self.act(probs[:, kk:ke],
                              ps[b][:, kk - b * 512:ke - b * 512],
                              func=AF.Exp, bias=sb[bn][:, 0:1], scale=1.0 / 8.0,
                              accum=sb["accD"][:, acc_n:acc_n + 1],
                              deps=dep_sc)
                bank_exp[b] = at
                acc_cols.append((acc_i, acc_n))
                acc_n += 1
                prb_toks.append(at)
                kk = ke
        for b in range(nk):
            self.free_bank(b, bank_exp[b])
        self.tok_b1_free[int(pp)] = prb_toks[-1]
        self.tok_b2_free[int(pp)] = prb_toks[-1]
        st.update(prb_toks=prb_toks, acc_cols=acc_cols, acc_n=acc_n,
                  acc_base=acc_base)
        return st

    def l0_attn_fb_dve(self, st):
        cfg, sb, ps = self.cfg, self.sb, self.ps
        pp = st["pp"]
        probs = sb["probs" + pp]
        cMpre = sb["cMpre"]
        cDdiag = sb["cDdiag"]
        S, NT, NREL = cfg.S, cfg.NT, cfg.NREL
        q0, wk1, nk = st["q0"], st["wk1"], st["nk"]
        prb_toks = st["prb_toks"]
        acc_cols, acc_n = st["acc_cols"], st["acc_n"]
        acc_base = st["acc_base"]
        etoks = []
        pre_cols = [c for (ai, c) in acc_cols if ai == 8]
        if q0 > 0:
            npre = len(pre_cols)
            assert pre_cols == list(range(acc_base, acc_base + npre))
            if npre == 1:
                t = self.op("dve", lambda e, ab=acc_base: e.tensor_copy(
                    out=sb["accD"][:, 8:9], in_=sb["accD"][:, ab:ab + 1]),
                    prb_toks)
            else:
                t = self.op("dve", lambda e, npre=npre, ab=acc_base:
                            e.tensor_reduce(
                                out=sb["accD"][:, 8:9],
                                in_=sb["accD"][:, ab:ab + npre],
                                axis=AXX.X, op=ALU.add), prb_toks)
            etoks.append(t)
        sS = self.op("dve", lambda e, an=acc_n, ab=acc_base: e.tensor_reduce(
            out=sb["Scol"][:, 0:1], in_=sb["accD"][:, ab:an],
            axis=AXX.X, op=ALU.add), prb_toks)
        etoks.append(sS)
        s1 = sS
        rs = self.op("dve", lambda e: e.reciprocal(
            out=sb["recipS"][:, 0:1], in_=sb["Scol"][:, 0:1]), [s1])
        wp1 = min(q0 + 131, S)
        wlen = wp1 - q0
        dts = []
        for oi in range(7):
            dt = self.op("dve", lambda e, oi=oi: e.scalar_tensor_tensor(
                out=sb["hg"][:, oi * 131:oi * 131 + wlen], in0=probs[:, q0:wp1],
                scalar=1.0, in1=cDdiag[:, oi * 131:oi * 131 + wlen],
                op0=ALU.mult, op1=ALU.mult,
                accum_out=sb["accD"][:, oi:oi + 1]),
                prb_toks + [self.tok_hg_free2])
            dts.append(dt)
        mp = self.op("dve", lambda e: e.scalar_tensor_tensor(
            out=sb["hg"][:, 0:wlen], in0=probs[:, q0:wp1],
            scalar=1.0, in1=cMpre[:, 0:wlen], op0=ALU.mult, op1=ALU.mult,
            accum_out=sb["accD"][:, 23:24]), prb_toks + [dts[0]])
        self.tok_hg_free = mp
        self.tok_hg_free2 = dts[-1]
        if q0 > 0:
            t8a = self.op("dve", lambda e: e.tensor_sub(
                out=sb["accD"][:, 7:8], in0=sb["Scol"][:, 0:1],
                in1=sb["accD"][:, 8:9]), [s1, mp])
            t8 = self.op("dve", lambda e: e.tensor_sub(
                out=sb["accD"][:, 7:8], in0=sb["accD"][:, 7:8],
                in1=sb["accD"][:, 23:24]), [t8a])
        else:
            t8 = self.op("dve", lambda e: e.tensor_sub(
                out=sb["accD"][:, 7:8], in0=sb["Scol"][:, 0:1],
                in1=sb["accD"][:, 23:24]), [s1, mp])
        d8 = self.act(sb["D8_" + pp][:, 0:8], sb["accD"][:, 0:8],
                      func=AF.Identity, scale=sb["recipS"][:, 0:1],
                      deps=[t8, rs, self.tok_d8_free[int(pp)]] + dts)
        dR = self.op("dve", lambda e: e.tensor_scalar(
            out=sb["diagR" + pp][:, :], in0=sb["cI128"][:, :],
            scalar1=sb["recipS"][:, 0:1], scalar2=None, op0=ALU.mult),
            [rs, self.tok_diagR_free[int(pp)]])
        st.update(prb_toks=prb_toks, dR=dR, d8=d8, mp=mp, dts=dts)
        return st

    def l0_attn_back_pe(self, st, kv_toks):
        cfg, sb, ps = self.cfg, self.sb, self.ps
        h, g, qt, pp = st["h"], st["g"], st["qt"], st["pp"]
        probs = sb["probs" + pp]
        pTs = sb["pTs"]
        AO = sb["AO"]
        Vf = sb["Vf"]
        S, D, DC, NT, DH = cfg.S, cfg.D, cfg.DC, cfg.NT, cfg.DH
        par, ch = h % 2, h // 2
        prb_toks, dR = st["prb_toks"], st["dR"]
        td = self.mm(4, ps[4][0:8, 0:128], sb["D8_" + pp][:, 0:8],
                     sb["cI128"][:, :], True, True, deps=[st["d8"]])
        ed = self.act(sb["D8Tp" + pp][0:8, :], ps[4][0:8, 0:128],
                      deps=[td, self.tok_d8t_free[int(pp)]])
        self.free_bank(4, ed)
        self.tok_d8_free[int(pp)] = td
        nkc = S // 128
        ev_pt = []
        for bb in range(nkc // 4):
            bank = 5 + (bb % 2)
            mmp = []
            for j in range(4):
                kc = bb * 4 + j
                mmp.append(self.mm(bank, ps[bank][:, j * 128:(j + 1) * 128],
                                   probs[:, kc * 128:(kc + 1) * 128],
                                   sb["diagR" + pp][:, :], True, True,
                                   deps=prb_toks + [dR]))
            if bb % 2 == 0:
                epv = self.op("dve", lambda e, bb=bb, bank=bank: e.tensor_copy(
                    out=pTs[:, bb * 512:(bb + 1) * 512], in_=ps[bank][:, 0:512]),
                    mmp + [self.tok_pts_free])
            else:
                epv = self.act(pTs[:, bb * 512:(bb + 1) * 512],
                               ps[bank][:, 0:512],
                               deps=mmp + [self.tok_pts_free])
            self.free_bank(bank, epv)
            ev_pt.append(epv)
        self.tok_diagR_free[int(pp)] = ev_pt[-1]
        self._av_par = getattr(self, "_av_par", 0) ^ 1
        avc = self._av_par * 128
        av = None
        avdep = self.tok_av_free[self._av_par] or self.pslast[7]
        self.pslast[7] = None
        for kc in range(nkc):
            tc = kc
            av = self.mm(7, ps[7][0:64, avc:avc + 128],
                         Vf[:, tc * D + h * DH: tc * D + (h + 1) * DH],
                         pTs[:, kc * 128:(kc + 1) * 128],
                         kc == 0, kc == nkc - 1,
                         deps=[ev_pt[kc // 4], avdep if kc == 0 else None] + kv_toks)
        w2 = self.mm(7, ps[7][0:64, 256 + avc:256 + avc + 128],
                     sb["TVmodp"][:, 0:DH], sb["D8Tp" + pp][:, :], True, True,
                     deps=[ed, self.tok_w2_free[self._av_par]])
        self.tok_d8t_free[int(pp)] = w2
        self.tok_pts_free = av
        self.tok_probs_free = [ev_pt[-1], st["dts"][-1], st["mp"]]
        st.update(av=av, w2=w2, avc=avc, avpar=self._av_par)
        return st

    def l0_attn_back_tail(self, st):
        cfg, sb, ps = self.cfg, self.sb, self.ps
        h, qt = st["h"], st["qt"]
        AO = sb["AO"]
        NT = cfg.NT
        par, ch = h % 2, h // 2
        av, w2, avc, avpar = st["av"], st["w2"], st["avc"], st["avpar"]
        aos = AO[par * 64:par * 64 + 64, ch * NT + qt * 128:ch * NT + (qt + 1) * 128]
        ao1 = self.act(aos, ps[7][0:64, avc:avc + 128], func=AF.Identity,
                       bias=sb["ctvT"][:, 0:1], deps=[av])
        ao = self.op("dve", lambda e, aos=aos, avc=avc: e.tensor_tensor(
            out=aos, in0=aos, in1=ps[7][0:64, 256 + avc:256 + avc + 128],
            op=ALU.add), [ao1, w2])
        self.tok_av_free[avpar] = ao1
        self.tok_w2_free[avpar] = ao
        return ao

    def oproj_group(self, wname, g, ao_toks, bias_idx=None, wo_free=None,
                    delta=None):
        cfg, sb, ps = self.cfg, self.sb, self.ps
        AO = sb["AO"]
        S, D, DC, NT = cfg.S, cfg.D, cfg.DC, cfg.NT
        col = g * NT
        toks = []
        for m in range(DC):
            cst = self.stage_cols(wname, m * 128, 128, DC,
                                  [wo_free[0], self.tok_hg_free, self.tok_hg_free2])
            wo_free[0] = cst
            self.tok_hg_free = cst
            self.tok_hg_free2 = cst
            bk = m % 2
            mmo = None
            for kc in range(DC):
                mmo = self.mm(bk, ps[bk][:, 0:NT],
                              sb["hg"][:, kc * 128:(kc + 1) * 128],
                              AO[:, kc * NT:(kc + 1) * NT],
                              kc == 0, kc == DC - 1, deps=[cst] + ao_toks)
            par = m % 2
            bias = (sb["biasT"][:, bias_idx * DC + m:bias_idx * DC + m + 1]
                    if bias_idx is not None else None)
            if delta is not None:
                eo = self.act(delta[:, m * NT:(m + 1) * NT],
                              ps[bk][:, 0:NT],
                              func=AF.Identity if bias is not None else AF.Copy,
                              bias=bias, deps=[mmo] + self.tok_xr_free)
                self.free_bank(bk, eo)
                sp = self.dma("pool", "dsp", lambda e, m=m, col=col: e.dma_start(
                    out=self.dram["dscr"][:, m * S + col:m * S + col + NT],
                    in_=delta[:, m * NT:(m + 1) * NT]), [eo])
                wo_free[1] = eo
                toks.append(sp)
            else:
                ev = sb["evac"][:, par * NT:(par + 1) * NT]
                eo = self.act(ev, ps[bk][:, 0:NT],
                              func=AF.Identity if bias is not None else AF.Copy,
                              bias=bias, deps=[mmo, self.tok_evac_free[par]])
                self.free_bank(bk, eo)
                ad = self.op("dve", lambda e, m=m, col=col, ev=ev: e.tensor_tensor(
                    out=sb["xr"][:, m * S + col:m * S + col + NT],
                    in0=sb["xr"][:, m * S + col:m * S + col + NT],
                    in1=ev, op=ALU.add), [eo] + self.tok_xr_free)
                self.tok_evac_free[par] = ad
                wo_free[1] = ad
                toks.append(ad)
        return toks

    def l0_phase_b(self, mrow, rrow, xng, stat_toks, kv_toks, delta):
        cfg = self.cfg
        wq_free = [None, None]
        wo_free = [None, None]
        all_toks = []
        prev_stats = list(stat_toks)
        for g in range(cfg.NTT):
            stat_g = self.ln_stats_one(mrow, rrow, 0, g, 0, prev_stats)
            xnt = self.xn_tile(lambda c: xng[:, c * cfg.NT:(c + 1) * cfg.NT],
                               mrow, rrow, g, stat_g, bm=2, br=3, stats_col=0)
            prev_stats = stat_g + xnt
            ao_toks = []
            pend = None
            ntiles = cfg.NH * cfg.QG
            sts = {}
            hd = None
            cur_head = -1

            def _qr(i):
                nonlocal hd, cur_head
                h, qt = divmod(i, cfg.QG)
                if h != cur_head:
                    self.tok_qg_free = self.fw.last("pe")
                    hd = self.l0_attn_head(h, g, xng, xnt, kv_toks, wq_free)
                    cur_head = h
                self._l0pp = getattr(self, "_l0pp", 0) ^ 1
                sts[i] = self.l0_attn_fa_qr(h, g, qt, hd[0], hd[1], hd[2],
                                            kv_toks, str(self._l0pp))

            _qr(0)
            self.l0_attn_fa_sc(sts[0], kv_toks)
            if ntiles > 1:
                _qr(1)
            for i in range(ntiles):
                st = self.l0_attn_fb_act(sts[i])
                if i + 1 < ntiles:
                    self.l0_attn_fa_sc(sts[i + 1], kv_toks)
                if pend is not None:
                    pend = self.l0_attn_back_pe(pend, kv_toks)
                st = self.l0_attn_fb_dve(st)
                if pend is not None:
                    ao_toks.append(self.l0_attn_back_tail(pend))
                if i + 2 < ntiles:
                    _qr(i + 2)
                pend = st
                sts.pop(i, None)
            self.tok_qg_free = self.fw.last("pe")
            pend = self.l0_attn_back_pe(pend, kv_toks)
            ao_toks.append(self.l0_attn_back_tail(pend))
            pend = None
            t = self.oproj_group("Wo", g, ao_toks, bias_idx=3, wo_free=wo_free,
                                 delta=delta)
            all_toks += t
        return all_toks

    def l0_phase_c(self, mrow, rrow, xng, stat_toks, h_all, wif, wih, wif2, wih2,
                   delta=None, bar_deps=()):
        cfg, sb, ps = self.cfg, self.sb, self.ps
        S, D, DC, NT, HC = cfg.S, cfg.D, cfg.DC, cfg.NT, cfg.HC
        bd = list(bar_deps)
        gfree = [bd, bd, bd, bd]
        dfree = [None, None]
        toks = []
        self.tok_dfree = {}
        prev_stats = list(stat_toks) + bd
        for t in range(cfg.NTT):
            col = t * NT
            stat_t = self.ln_stats_one(mrow, rrow, 0, t, 0, prev_stats)
            xnt = self.xn_tile(lambda c: xng[:, c * NT:(c + 1) * NT],
                               mrow, rrow, t, stat_t, bm=6, br=7, stats_col=0)
            prev_stats = stat_t + xnt
            h_toks = []
            for hc in range(HC):
                g0 = gfree[0] if isinstance(gfree[0], list) else [gfree[0]]
                srcg = self.dram["wgT"][hc * 128:(hc + 1) * 128, :]
                dstg = wif[:, 0:DC * 128]
                g2 = gfree[2] if isinstance(gfree[2], list) else [gfree[2]]
                dg = [self.dma("sp", "w_wg", lambda e, srcg=srcg, dstg=dstg:
                               e.dma_start(out=dstg, in_=srcg), g2)]
                cg = self.act(wih[:, 0:DC * 128], wif[:, 0:DC * 128],
                              deps=dg + g0)
                gfree[2] = cg
                g1 = gfree[1] if isinstance(gfree[1], list) else [gfree[1]]
                srcu = self.dram["wuT"][hc * 128:(hc + 1) * 128, :]
                dstu = wif2[:, 0:DC * 128]
                g3 = gfree[3] if isinstance(gfree[3], list) else [gfree[3]]
                du = [self.dma("pool", "w_wu", lambda e, srcu=srcu, dstu=dstu:
                               e.dma_start(out=dstu, in_=srcu), g3)]
                cu = self.act(wih2[:, 0:DC * 128], wif2[:, 0:DC * 128],
                              deps=du + g1)
                gfree[3] = cu
                mg = mu = None
                bg, bu = 0 + (hc % 2) * 2, 1 + (hc % 2) * 2
                for kc in range(DC):
                    mg = self.mm(bg, ps[bg][:, 0:NT], wih[:, kc * 128:(kc + 1) * 128],
                                 xng[:, kc * NT:(kc + 1) * NT],
                                 kc == 0, kc == DC - 1, deps=[cg] + xnt)
                    mu = self.mm(bu, ps[bu][:, 0:NT], wih2[:, kc * 128:(kc + 1) * 128],
                                 xng[:, kc * NT:(kc + 1) * NT],
                                 kc == 0, kc == DC - 1, deps=[cu] + xnt)
                gfree[0] = mg
                gfree[1] = mu
                eg = self.act(sb["hg"][:, 0:NT], ps[bg][:, 0:NT],
                              func=AF.Silu, scale=1.0 / 127.0,
                              deps=[mg, self.tok_hgc_free])
                self.free_bank(bg, eg)
                eu = self.op("dve", lambda e, bu=bu: e.tensor_copy(
                    out=sb["hg"][:, NT:2 * NT], in_=ps[bu][:, 0:NT]),
                    [mu, self.tok_hgc_free])
                self.free_bank(bu, eu)
                hm = self.op("dve", lambda e, hc=hc: e.tensor_mul(
                    out=h_all[:, hc * NT:(hc + 1) * NT],
                    in0=sb["hg"][:, 0:NT], in1=sb["hg"][:, NT:2 * NT]),
                    [eg, eu])
                self.tok_hgc_free = hm
                h_toks.append(hm)
            self.tok_dld = {}
            for m in range(DC):
                dl_ = self.dma("pool", "dsp", lambda e, m=m, col=col: e.dma_start(
                    out=delta[:, m * NT:(m + 1) * NT],
                    in_=self.dram["dscr"][:, m * S + col:m * S + col + NT]),
                    [self.tok_dfree.get(m)] + bd)
                self.tok_dld[m] = dl_
            mmd = {}
            for hc in range(HC):
                dd = self.dma("sp", "w_wd", lambda e, hc=hc: e.dma_start(
                    out=wif[:, 0:D],
                    in_=self.dram["wdT"][hc * 128:(hc + 1) * 128, :]),
                    [dfree[0]] + (gfree[0] if isinstance(gfree[0], list) else [gfree[0]]))
                cd = self.act(wih[:, 0:D], wif[:, 0:D],
                              deps=[dd, dfree[1]] +
                              (gfree[1] if isinstance(gfree[1], list) else [gfree[1]]))
                dfree[0] = cd
                for m in range(DC):
                    mmd[m] = self.mm(m, ps[m][:, 0:NT],
                                     wih[:, m * 128:(m + 1) * 128],
                                     h_all[:, hc * NT:(hc + 1) * NT],
                                     hc == 0, hc == HC - 1,
                                     deps=[cd, h_toks[hc]])
                dfree[1] = mmd[DC - 1]
            for m in range(DC):
                par = m % 2
                ev = sb["evac"][:, par * NT:(par + 1) * NT]
                eo = self.act(ev, ps[m][:, 0:NT], func=AF.Copy,
                              scale=1.0 / (127.0 * 127.0),
                              deps=[mmd[m], self.tok_evac_free[par]])
                self.free_bank(m, eo)
                ad = self.op("dve", lambda e, m=m, col=col, ev=ev: e.tensor_tensor(
                    out=sb["xr"][:, m * S + col:m * S + col + NT],
                    in0=sb["xr"][:, m * S + col:m * S + col + NT],
                    in1=ev, op=ALU.add), [eo] + self.tok_xr_free)
                ad2 = self.op("dve", lambda e, m=m, col=col: e.tensor_tensor(
                    out=sb["xr"][:, m * S + col:m * S + col + NT],
                    in0=sb["xr"][:, m * S + col:m * S + col + NT],
                    in1=delta[:, m * NT:(m + 1) * NT], op=ALU.add),
                    [ad, self.tok_dld[m]])
                self.tok_evac_free[par] = ad2
                self.tok_dfree[m] = ad2
                toks.append(ad2)
            gfree[0] = dfree[1]
            gfree[1] = dfree[1]
            gfree[2] = dfree[0]
        return toks

    def reset_attn_toks(self):
        self.tok_qg_free = None
        self.tok_mx_free = None
        self.tok_kmax = None
        self.tok_hg_free = None
        self.tok_hg_free2 = None
        self.tok_uc_free = None
        self.tok_b1_free = [None, None]
        self.tok_b2_free = [None, None]
        self.tok_qrt_free = [None, None]
        self.tok_bandp_free = [None, None]
        self.tok_probs_free = []
        self.tok_diagR_free = [None, None]
        self.tok_d8t_free = [None, None]
        self.tok_d8_free = [None, None]
        self.tok_pts_free = None
        self.tok_hgc_free = None
        self.tok_evac_free = [None, None]
        self.tok_xr_free = []
        self.tok_lgr_free = None
        self.tok_gat_free = None
        self.tok_mid_free = None
        self.tok_zt_free = None
        self.tok_qaug_free = None
        self.tok_ropeq_free = None
        self.tok_qr_free = None
        self.tok_av_free = [None, None]
        self.tok_w2_free = [None, None]

    def build(self, layers=(0, 1), nseq=None):
        cfg = self.cfg
        nseq = nseq if nseq is not None else cfg.B_local
        self.declare()
        with ExitStack() as ctx0:
            self.ctx = ctx0
            if True:
                self.alloc_persistent(ctx0)
                self.alloc_psum(ctx0)
                self.reset_attn_toks()
                self.emit_init()
                dep0 = self.init_toks
                fin = []
                for seq in range(nseq):
                    xl = self.load_x(seq, dep0)
                    if 0 in layers:
                        with ExitStack() as c_l0:
                            mrow = self.sbuf(c_l0, "mrow0", [1, cfg.NT], F32)
                            rrow = self.sbuf(c_l0, "rrow0", [1, cfg.NT], F32)
                            xng = self.sbuf(c_l0, "xng0", [128, cfg.DC * cfg.NT], BF16)
                            delta = self.sbuf(c_l0, "delta0", [128, cfg.DC * cfg.NT], BF16)
                            with ExitStack() as c_kv:
                                Kf = self.sbuf(c_kv, "Kf", [128, cfg.DC * cfg.S], BF16)
                                Vf = self.sbuf(c_kv, "Vf", [128, (cfg.S // 128) * cfg.D], BF16)
                                bvh = self.sbuf(c_kv, "bvh", [1, cfg.D], BF16)
                                tb_ = self.dma("sp", "icv", lambda e: e.dma_start(
                                    out=self.sb["sq"][0:1, 0:cfg.D],
                                    in_=self.dram["bv_row"]), dep0)
                                cb_ = self.act(bvh[0:1, :], self.sb["sq"][0:1, 0:cfg.D],
                                               deps=[tb_])
                                dep0 = dep0 + [cb_]
                                stat_toks, kv_toks = self.l0_phase_a(
                                    mrow, rrow, xng, xl + dep0)
                                if seq == 0:
                                    self.dbg("Kf", Kf[:, :], kv_toks,
                                             (128, cfg.DC * cfg.S), BF16)
                                    self.dbg("Vf", Vf[:, :], kv_toks,
                                             (128, (cfg.S // 128) * cfg.D), BF16)
                                with ExitStack() as c_b:
                                    self.sbuf(c_b, "cMstep", [128, 8 * cfg.W], BF16)
                                    self.sbuf(c_b, "cMpre", [128, 131], BF16)
                                    self.sbuf(c_b, "cDdiag", [128, 7 * 131], BF16)
                                    self.sbuf(c_b, "probs0", [128, cfg.S], BF16)
                                    self.sbuf(c_b, "probs1", [128, cfg.S], BF16)
                                    self.sbuf(c_b, "pTs", [128, cfg.S], BF16)
                                    self.sbuf(c_b, "AO", [128, cfg.DC * cfg.NT], BF16)
                                    self.sbuf(c_b, "wqs", [128, cfg.DC * 64], F32)
                                    self.sbuf(c_b, "wqh", [128, cfg.DC * 64], BF16)
                                    self.sbuf(c_b, "bandp_0", [128, 2 * cfg.W], BF16)
                                    self.sbuf(c_b, "bandp_1", [128, 2 * cfg.W], BF16)
                                    self.sbuf(c_b, "qrel_s", [cfg.NREL, cfg.NT], BF16)
                                    self.sbuf(c_b, "Qg", [128, cfg.NT], BF16)
                                    self.sbuf(c_b, "Qr", [128, cfg.NT], BF16)
                                    mdep = [self.fw.last("act"), self.fw.last("dve")]
                                    mq_ = self.op("dve", lambda e, q_=self.sb["qrel_s"]:
                                                  e.memset(q_[:, :], 0.0), mdep)
                                    mg_ = self.op("dve", lambda e, q_=self.sb["Qg"]:
                                                  e.memset(q_[:, :], 0.0), mdep)
                                    kv_toks = kv_toks + [mq_, mg_]
                                    ml = [self.dma("sp", "mk",
                                        lambda e, dst=self.sb["c" + n], src=self.dram[n]:
                                        e.dma_start(out=dst[:, :], in_=src), mdep)
                                        for n in ("Mstep", "Mpre", "Ddiag")]
                                    kv_toks = kv_toks + ml
                                    self.l0_phase_b(mrow, rrow, xng, stat_toks, kv_toks, delta)
                                    bar1 = self.bar()
                                    if seq == 0:
                                        self.dbg("xrB", delta[:, :], bar1,
                                                 (128, cfg.DC * cfg.S), BF16)
                                        self.dbg("probs", self.sb["probs0"][:, :], bar1,
                                                 (128, cfg.S), BF16)
                            with ExitStack() as c_c:
                                h_all = self.sbuf(c_c, "h_all", [128, cfg.HC * cfg.NT], BF16)
                                wif = self.sbuf(c_c, "wif", [128, cfg.D], F32)
                                wih = self.sbuf(c_c, "wih", [128, cfg.D], BF16)
                                wif2 = self.sbuf(c_c, "wif2", [128, cfg.D], F32)
                                wih2 = self.sbuf(c_c, "wih2", [128, cfg.D], BF16)
                                self.tok_xr_free = bar1
                                self.tok_hgc_free = bar1[0]
                                self.l0_phase_c(mrow, rrow, xng, stat_toks, h_all,
                                                wif, wih, wif2, wih2, delta=delta,
                                                bar_deps=bar1)
                                if seq == 0:
                                    self.dbg("h_all", h_all[:, :], self.bar(),
                                             (128, cfg.HC * cfg.NT), BF16)
                                    self.dbg("hgate", self.sb["hg"][:, :],
                                             self.bar(), (128, 2 * cfg.NT), BF16)
                                    self.dbg("xng", xng[:, :], self.bar(),
                                             (128, cfg.DC * cfg.NT), BF16)
                            bar2 = self.bar()
                    else:
                        bar2 = self.bar() + xl
                    if 1 in layers:
                        bar2 = self.emit_l1(seq, bar2)
                    st = self.store_y(seq, bar2)
                    fin += st
                    dep0 = st
                self.fw.set_finish(fin + self._dbg_toks)
                with self.nc.Block() as block:
                    self.fw.finalize(block, ctx0)
        return self.nc

    def l1_phase_a(self, mrow, rrow, xng, kvn, Kf, Vf, KR, QC, ccos, csin, deps):
        cfg, sb, ps = self.cfg, self.sb, self.ps
        lgr = sb["lgr"]
        mid = sb["mid"]
        Zt = sb["Zt"]
        gat = sb["gat"]
        lBh = sb["lBh"]
        tok_cosA = [None]
        S, D, DC, NT = cfg.S, cfg.D, cfg.DC, cfg.NT
        KV = cfg.KV_RANK
        KVC = KV // 128
        QRC = cfg.Q_RANK // 128
        RCC = cfg.NH * cfg.RC // 128
        kfree = [None, None]
        out_toks = []
        prev_stats = list(deps)
        xnt_prev = []
        for t in range(cfg.NTT):
            col = t * NT
            stat_toks = self.ln_stats_one(mrow, rrow, 0, t, 0, deps + prev_stats)
            prev_stats = stat_toks
            if kfree[0] is None:
                kfree = [stat_toks[-1], stat_toks[-1]]
            cA = self.dma("sp", "cs", lambda e, col=col: e.dma_start(
                out=ccos[:, 0:NT], in_=self.dram["cos_rep"][:, col:col + NT]),
                [tok_cosA[0]])
            cB = self.dma("sp", "cs", lambda e, col=col: e.dma_start(
                out=csin[:, 0:NT], in_=self.dram["sin_rep"][:, col:col + NT]),
                [tok_cosA[0]])
            xnt = self.xn_tile(lambda c: xng[:, c * NT:(c + 1) * NT],
                               mrow, rrow, t, stat_toks, stats_col=0)
            prev_stats = stat_toks + xnt
            xnt_prev = xnt
            mmkv = {}
            for m in range(KVC):
                cst = self.stage_cols("Wkv", m * 128, 128, DC, [kfree[0], kfree[1]], ch="w_kv")
                kfree[0] = cst
                mk = None
                for kc in range(DC):
                    mk = self.mm(m, ps[m][:, 0:NT],
                                 sb["hg"][:, kc * 128:(kc + 1) * 128],
                                 xng[:, kc * NT:(kc + 1) * NT],
                                 kc == 0, kc == DC - 1, deps=[cst] + xnt)
                mmkv[m] = mk
                kfree[1] = mk
            mms = None
            sqp = None
            for m in range(KVC):
                sq_ = self.act(sb["hg"][:, 0:NT], ps[m][:, 0:NT], func=AF.Square,
                               deps=[mmkv[m], kfree[1], sqp or self.tok_hgc_free])
                mms = self.mm(4, ps[4][0:1, 0:NT], sb["cones_colh"][:, 0:1],
                              sb["hg"][:, 0:NT], m == 0, m == KVC - 1, deps=[sq_])
                sqp = mms
            self.tok_hgc_free = mms
            st_ = self.act(sb["srow"][0:1, 0:NT], ps[4][0:1, 0:NT], func=AF.Sqrt,
                           scale=1.0 / KV, bias=sb["epsrow"][0:1, 1:2], deps=[mms])
            self.free_bank(4, st_)
            rt_ = self.op("dve", lambda e: e.reciprocal(
                out=sb["vrow"][0:1, 0:NT], in_=sb["srow"][0:1, 0:NT]), [st_])
            bc = self.mm(5, ps[5][:, 0:NT], sb["cones_row"][:, :],
                         sb["vrow"][0:1, 0:NT], True, True, deps=[rt_])
            kvn_toks = []
            for m in range(KVC):
                par = m % 2
                ev = sb["evac"][:, par * NT:(par + 1) * NT]
                e1 = self.act(ev, ps[m][:, 0:NT], func=AF.Identity,
                              scale=sb["cgkv"][:, m:m + 1],
                              deps=[mmkv[m], self.tok_evac_free[par]])
                self.free_bank(m, e1)
                e2 = self.op("dve", lambda e, m=m, ev=ev: e.tensor_mul(
                    out=kvn[:, m * NT:(m + 1) * NT], in0=ev, in1=ps[5][:, 0:NT]),
                    [e1, bc])
                self.tok_evac_free[par] = e2
                kvn_toks.append(e2)
            self.free_bank(5, kvn_toks[-1])
            for m in range(DC):
                cst = self.stage_cols("Wkdec", m * 128, 128, KVC, [kfree[0], kfree[1]], ch="w_kd")
                kfree[0] = cst
                bk = 6 + (m % 2)
                mk = None
                for kc in range(KVC):
                    mk = self.mm(bk, ps[bk][:, 0:NT],
                                 sb["hg"][:, kc * 128:(kc + 1) * 128],
                                 kvn[:, kc * NT:(kc + 1) * NT],
                                 kc == 0, kc == KVC - 1, deps=[cst] + kvn_toks)
                ek = self.act(Kf[:, m * S + col:m * S + col + NT], ps[bk][:, 0:NT],
                              deps=[mk])
                self.free_bank(bk, ek)
                kfree[1] = ek
                out_toks.append(ek)
            out_toks += self.v_tm_tile_src("Wvdec", kvn, t, KVC, D // 512, None,
                                           sb["sq"], sb["hg"], kfree, kvn_toks, Vf, NT)
            for m in range(RCC):
                cst = self.stage_cols("Wkrope", m * 128, 128, DC, [kfree[0], kfree[1]], ch="w_kr")
                kfree[0] = cst
                mk = None
                for kc in range(DC):
                    mk = self.mm(4, ps[4][:, 0:NT],
                                 sb["hg"][:, kc * 128:(kc + 1) * 128],
                                 xng[:, kc * NT:(kc + 1) * NT],
                                 kc == 0, kc == DC - 1, deps=[cst] + xnt)
                kfree[1] = mk
                ekr = self.act(sb["hg"][:, NT:2 * NT], ps[4][:, 0:NT],
                               deps=[mk, self.tok_hg_free])
                self.free_bank(4, ekr)
                rot = self.mm(5, ps[5][:, 0:NT], sb["cR128T"][:, :],
                              sb["hg"][:, NT:2 * NT], True, True, deps=[ekr])
                t1 = self.op("dve", lambda e: e.tensor_mul(
                    out=sb["hg"][:, 0:NT], in0=sb["hg"][:, NT:2 * NT],
                    in1=ccos[:, 0:NT]), [ekr, self.tok_hgc_free, cA])
                t2 = self.op("dve", lambda e, m=m, col=col: e.scalar_tensor_tensor(
                    out=KR[:, m * S + col:m * S + col + NT], in0=ps[5][:, 0:NT],
                    scalar=1.0, in1=csin[:, 0:NT],
                    op0=ALU.mult, op1=ALU.mult), [rot, cB])
                t3 = self.op("dve", lambda e, m=m, col=col: e.tensor_tensor(
                    out=KR[:, m * S + col:m * S + col + NT],
                    in0=KR[:, m * S + col:m * S + col + NT],
                    in1=sb["hg"][:, 0:NT], op=ALU.add), [t1, t2])
                self.free_bank(5, t2)
                self.tok_hg_free = t3
                self.tok_hgc_free = t3
                tok_cosA[0] = t3
                out_toks.append(t3)
            for m in range(QRC):
                cst = self.stage_cols("Wq1", m * 128, 128, DC, [kfree[0], kfree[1]], ch="w_q1")
                kfree[0] = cst
                bk = 6 + (m % 2)
                mk = None
                for kc in range(DC):
                    mk = self.mm(bk, ps[bk][:, 0:NT],
                                 sb["hg"][:, kc * 128:(kc + 1) * 128],
                                 xng[:, kc * NT:(kc + 1) * NT],
                                 kc == 0, kc == DC - 1, deps=[cst] + xnt)
                eq = self.act(QC[:, m * S + col:m * S + col + NT], ps[bk][:, 0:NT],
                              deps=[mk])
                self.free_bank(bk, eq)
                kfree[1] = eq
                out_toks.append(eq)
            mr = None
            for kc in range(DC):
                mr = self.mm(4, ps[4][0:8, 0:NT], sb["rwh"][:, kc * 8:(kc + 1) * 8],
                             xng[:, kc * NT:(kc + 1) * NT],
                             kc == 0, kc == DC - 1, deps=xnt)
            er = self.act(lgr[0:8, 0:NT], ps[4][0:8, 0:NT], func=AF.Identity,
                          bias=sb["crb"][:, 0:1], deps=[mr, self.tok_lgr_free])
            self.free_bank(4, er)
            b0 = self.mm(5, ps[5][0:8, 0:NT], sb["cones_row"][0:1, 0:8],
                         lgr[0:1, 0:NT], True, True, deps=[er])
            dl = self.op("dve", lambda e: e.tensor_sub(
                out=lgr[0:8, NT:2 * NT], in0=lgr[0:8, 0:NT],
                in1=ps[5][0:8, 0:NT]), [b0, self.tok_lgr_free])
            self.free_bank(5, dl)
            ex = self.act(lgr[0:8, 0:NT], lgr[0:8, NT:2 * NT],
                          func=AF.Exp, deps=[dl])
            sm = self.mm(4, ps[4][0:1, 0:NT], sb["cones_col"][0:8, 0:1],
                         lgr[0:8, 0:NT], True, True, deps=[ex])
            rc = self.op("dve", lambda e: e.reciprocal(
                out=sb["srow"][0:1, 0:NT], in_=ps[4][0:1, 0:NT]), [sm])
            self.free_bank(4, rc)
            bs = self.mm(5, ps[5][0:8, 0:NT], sb["cones_row"][0:1, 0:8],
                         sb["srow"][0:1, 0:NT], True, True, deps=[rc])
            gt = self.op("dve", lambda e: e.tensor_mul(
                out=gat[0:8, 0:NT], in0=lgr[0:8, 0:NT],
                in1=ps[5][0:8, 0:NT]), [bs, self.tok_gat_free])
            self.free_bank(5, gt)
            self.tok_lgr_free = gt
            mm_ = None
            for kc in range(DC):
                mm_ = self.mm(4, ps[4][0:8, 0:NT], sb["lAh"][:, kc * 8:(kc + 1) * 8],
                              xng[:, kc * NT:(kc + 1) * NT],
                              kc == 0, kc == DC - 1, deps=xnt)
            em = self.act(mid[0:8, 0:NT], ps[4][0:8, 0:NT],
                          deps=[mm_, self.tok_mid_free])
            self.free_bank(4, em)
            gr = self.mm(4, ps[4][0:64, 0:NT], sb["cSELe"][:, :], gat[0:8, 0:NT],
                         True, True, deps=[gt])
            mrp = self.mm(5, ps[5][0:64, 0:NT], sb["cSELr"][:, :], mid[0:8, 0:NT],
                          True, True, deps=[em])
            ev = sb["evac"][:, 0:NT]
            eg = self.act(ev[0:64, :], ps[4][0:64, 0:NT],
                          deps=[gr, self.tok_evac_free[0]])
            self.free_bank(4, eg)
            zt = self.op("dve", lambda e: e.tensor_mul(
                out=Zt[0:64, 0:NT], in0=ev[0:64, :], in1=ps[5][0:64, 0:NT]),
                [eg, mrp, self.tok_zt_free])
            self.free_bank(5, zt)
            self.tok_evac_free[0] = zt
            for m in range(DC):
                cst = self.stage_cols("bwT", m * 128, 128, DC, [kfree[0], kfree[1]], ch="w_bw")
                kfree[0] = cst
                bk = 6 + (m % 2)
                mb = None
                for kc in range(DC):
                    mb = self.mm(bk, ps[bk][:, 0:NT],
                                 sb["hg"][:, kc * 128:(kc + 1) * 128],
                                 xng[:, kc * NT:(kc + 1) * NT],
                                 kc == 0, False, deps=[cst] + xnt)
                ml_ = self.mm(bk, ps[bk][:, 0:NT], lBh[:, m * 128:(m + 1) * 128],
                              Zt[:, 0:NT], False, True, deps=[zt, mb])
                kfree[1] = ml_
                par = m % 2
                ev2 = sb["evac"][:, par * NT:(par + 1) * NT]
                eo = self.act(ev2, ps[bk][:, 0:NT], func=AF.Identity,
                              bias=sb["biasT"][:, 4 * DC + m:4 * DC + m + 1],
                              deps=[ml_, self.tok_evac_free[par]])
                self.free_bank(bk, eo)
                ad = self.op("dve", lambda e, m=m, col=col, ev2=ev2: e.tensor_tensor(
                    out=sb["xr"][:, m * S + col:m * S + col + NT],
                    in0=sb["xr"][:, m * S + col:m * S + col + NT],
                    in1=ev2, op=ALU.add), [eo] + self.tok_xr_free)
                self.tok_evac_free[par] = ad
                out_toks.append(ad)
            self.tok_mid_free = mrp
            self.tok_gat_free = gr
            self.tok_zt_free = ml_
        return stat_toks, out_toks

    def v_tm_tile_src(self, wrows_name, src, t, kchunks, nn, bias_row,
                      wstage_f, wstage_h, wfree, src_toks, Vf, src_nt):
        cfg, sb = self.cfg, self.sb
        D, NT = cfg.D, cfg.NT
        ntc = NT // 128
        mms = {}
        cstp = wfree[0]
        last_mm = wfree[1]
        for kc in range(kchunks):
            dt = self.dma("sp", "w_" + wrows_name, lambda e, kc=kc: e.dma_start(
                out=wstage_f[:, 0:D],
                in_=self.dram[wrows_name][kc * 128:(kc + 1) * 128, :]), [cstp])
            cst = self.act(wstage_h[:, 0:D], wstage_f[:, 0:D], deps=[dt, last_mm])
            cstp = cst
            for tc in range(ntc):
                for n2 in range(nn):
                    bank = tc * nn + n2
                    mms[(tc, n2)] = self.mm(
                        bank, self.ps[bank][:, 0:512],
                        src[:, kc * src_nt + tc * 128: kc * src_nt + (tc + 1) * 128],
                        wstage_h[:, n2 * 512:(n2 + 1) * 512],
                        kc == 0, kc == kchunks - 1, deps=[cst] + src_toks)
                    last_mm = mms[(tc, n2)]
        wfree[0] = cstp
        evs = []
        for tc in range(ntc):
            for n2 in range(nn):
                bank = tc * nn + n2
                ev = self.act(Vf[:, (t * ntc + tc) * D + n2 * 512:
                                 (t * ntc + tc) * D + (n2 + 1) * 512],
                              self.ps[bank][:, 0:512], deps=[mms[(tc, n2)]])
                self.free_bank(bank, ev)
                evs.append(ev)
        wfree[1] = evs[-1]
        return evs

    def l1_attn_head(self, h, g, Kf, KR, QC, ccos, csin, a_toks, wq_free):
        cfg, sb, ps = self.cfg, self.sb, self.ps
        wqs = sb["wqs"]
        wqh = sb["wqh"]
        ropeq = sb["ropeq"]
        qaug = sb["qaug"]
        Qg = sb["Qg"]
        Qr = sb["Qr"]
        cSELq = sb["cSELq"]
        S, D, DC, NT, DH, RC = cfg.S, cfg.D, cfg.DC, cfg.NT, cfg.DH, cfg.RC
        par, ch = h % 2, h // 2
        qch = h % 4
        rch = h // 4
        QRC = cfg.Q_RANK // 128
        col = g * NT
        dts = [self.dma("sp", "w_qa", lambda e: e.dma_start(
            out=wqs[:, 0:QRC * 96],
            in_=self.dram["Wqaug"][h * 128:(h + 1) * 128, :]),
            [wq_free[0]])]
        cst = self.act(wqh[:, 0:QRC * 96], wqs[:, 0:QRC * 96],
                       deps=dts + [wq_free[1]])
        wq_free[0] = cst
        mq = None
        for kc in range(QRC):
            mq = self.mm(4, ps[4][0:96, 0:NT], wqh[:, kc * 96:(kc + 1) * 96],
                         QC[:, kc * S + col:kc * S + col + NT],
                         kc == 0, kc == QRC - 1, deps=[cst] + a_toks)
        wq_free[1] = mq
        eq = self.act(qaug[0:96, 0:NT], ps[4][0:96, 0:NT],
                      deps=[mq, self.tok_qaug_free])
        self.free_bank(4, eq)
        rot = self.mm(5, ps[5][0:32, 0:NT], sb["cR32Tp"][64:96, :],
                      qaug[64:96, 0:NT], True, True, deps=[eq])
        t0 = self.act(sb["hg"][0:32, 0:NT], qaug[64:96, 0:NT],
                      deps=[eq, self.tok_hg_free])
        t1 = self.op("dve", lambda e: e.tensor_mul(
            out=sb["hg"][0:32, NT:2 * NT], in0=sb["hg"][0:32, 0:NT],
            in1=ccos[0:32, 0:NT]), [t0, self.tok_hgc_free])
        t2 = self.op("dve", lambda e: e.scalar_tensor_tensor(
            out=ropeq[0:32, 0:NT], in0=ps[5][0:32, 0:NT], scalar=1.0,
            in1=csin[0:32, 0:NT], op0=ALU.mult, op1=ALU.mult),
            [rot, self.tok_ropeq_free])
        t3 = self.op("dve", lambda e: e.tensor_tensor(
            out=ropeq[0:32, 0:NT], in0=ropeq[0:32, 0:NT],
            in1=sb["hg"][0:32, NT:2 * NT], op=ALU.add), [t1, t2])
        self.free_bank(5, t2)
        self.tok_hg_free = t1
        self.tok_hgc_free = t3
        ms = self.op("dve", lambda e: e.memset(
            Qg[(1 - par) * 64:(1 - par) * 64 + 64, :], 0.0),
            [self.tok_qg_free])
        qc_ = self.act(Qg[par * 64:par * 64 + 64, :], qaug[0:64, 0:NT],
                       deps=[eq, ms, self.tok_qg_free])
        qr_ = self.mm(5, ps[5][:, 0:NT], cSELq[:, qch * 128:(qch + 1) * 128],
                      ropeq[0:32, 0:NT], True, True, deps=[t3])
        er_ = self.act(Qr[:, :], ps[5][:, 0:NT], deps=[qr_, self.tok_qr_free])
        self.free_bank(5, er_)
        self.tok_ropeq_free = qr_
        self.tok_qaug_free = qc_
        sq1 = self.act(sb["hg"][:, 0:NT], Qg[:, :], func=AF.Square,
                       deps=[qc_, self.tok_hg_free])
        n1 = self.mm(4, ps[4][0:1, 0:NT], sb["cones_colh"][:, 0:1],
                     sb["hg"][:, 0:NT], True, False, deps=[sq1])
        sq2 = self.act(sb["hg"][:, NT:2 * NT], Qr[:, :], func=AF.Square,
                       deps=[er_, self.tok_hgc_free])
        n2 = self.mm(4, ps[4][0:1, 0:NT], sb["cones_colh"][:, 0:1],
                     sb["hg"][:, NT:2 * NT], False, True, deps=[sq2])
        self.tok_hg_free = n1
        self.tok_hgc_free = n2
        sqt = self.act(sb["srow"][0:1, 0:NT], ps[4][0:1, 0:NT], func=AF.Sqrt,
                       deps=[n2])
        self.free_bank(4, sqt)
        qm = self.op("dve", lambda e: e.tensor_reduce(
            out=sb["hrow"][0:1, 0:1], in_=sb["srow"][0:1, 0:NT],
            axis=AXX.X, op=ALU.max), [sqt, self.tok_mx_free])
        if g == 0:
            kms = []
            for tq in range(cfg.NTT):
                sqk = self.act(sb["hg"][:, 0:NT],
                               Kf[:, ch * S + tq * NT:ch * S + (tq + 1) * NT],
                               func=AF.Square, deps=a_toks + [self.tok_hg_free])
                m1 = self.mm(4, ps[4][0:1, 0:NT],
                             sb["cones_e"][:, 0:1] if par == 0 else sb["cones_o"][:, 0:1],
                             sb["hg"][:, 0:NT], True, False, deps=[sqk])
                sqk2 = self.act(sb["hg"][:, NT:2 * NT],
                                KR[:, rch * S + tq * NT:rch * S + (tq + 1) * NT],
                                func=AF.Square, deps=a_toks + [self.tok_hgc_free])
                m2 = self.mm(4, ps[4][0:1, 0:NT], sb["cones_q"][:, qch:qch + 1],
                             sb["hg"][:, NT:2 * NT], False, True, deps=[sqk2])
                self.tok_hg_free = m1
                self.tok_hgc_free = m2
                km = self.op("dve", lambda e, tq=tq: e.tensor_reduce(
                    out=sb["hrow"][0:1, 8 + tq:9 + tq], in_=ps[4][0:1, 0:NT],
                    axis=AXX.X, op=ALU.max), [m2])
                self.free_bank(4, km)
                kms.append(km)
            c1 = self.op("dve", lambda e: e.tensor_reduce(
                out=sb["hrow"][0:1, 1:2], in_=sb["hrow"][0:1, 8:8 + cfg.NTT],
                axis=AXX.X, op=ALU.max), kms)
            c2 = self.act(sb["hrow"][0:1, 1:2], sb["hrow"][0:1, 1:2], func=AF.Sqrt,
                          deps=[c1])
            self.tok_kmax = c2
        mx2 = self.op("dve", lambda e: e.tensor_mul(
            out=sb["hrow"][0:1, 3:4], in0=sb["hrow"][0:1, 1:2],
            in1=sb["hrow"][0:1, 0:1]), [self.tok_kmax, qm])
        mxb = self.mm(4, ps[4][:, 0:1], sb["cones_row"][:, :],
                      sb["hrow"][0:1, 3:4], True, True, deps=[mx2])
        mxe = self.act(sb["MXcol"][:, 0:1], ps[4][:, 0:1], scale=-1.0 / (96 ** 0.5),
                       deps=[mxb])
        self.free_bank(4, mxe)
        self.tok_mx_free = mxe
        return qc_, er_, mxe

    def l1_attn_qtile(self, h, g, qt, qc_, er_, mxe, a_toks):
        cfg, sb, ps = self.cfg, self.sb, self.ps
        probs = sb["probs"]
        pTs = sb["pTs"]
        AO = sb["AO"]
        Vf = self.sb_Vf
        Kfl = self.sb_Kf
        KRl = self.sb_KR
        Qg = sb["Qg"]
        Qr = sb["Qr"]
        ccausal = sb["ccausal"]
        S, D, DC, NT, DH = cfg.S, cfg.D, cfg.DC, cfg.NT, cfg.DH
        par, ch = h % 2, h // 2
        rch = h // 4
        qti = g * cfg.QG + qt
        kmax_col = (qti + 1) * 128
        qs_g = Qg[:, qt * 128:(qt + 1) * 128]
        qs_r = Qr[:, qt * 128:(qt + 1) * 128]
        nb = (kmax_col + 511) // 512
        smm = []
        for b in range(nb):
            k0, k1 = b * 512, min(kmax_col, (b + 1) * 512)
            t1 = self.mm(b, ps[b][:, 0:k1 - k0], qs_g,
                         Kfl[:, ch * S + k0: ch * S + k1], True, False,
                         deps=[qc_] + a_toks)
            t2 = self.mm(b, ps[b][:, 0:k1 - k0], qs_r,
                         KRl[:, rch * S + k0: rch * S + k1], False, True,
                         deps=[er_] + a_toks)
            smm.append(t2)
        db = qti * 128 // 512
        doff = qti * 128 - db * 512
        cm = self.op("dve", lambda e, db=db, doff=doff: e.tensor_tensor(
            out=ps[db][:, doff:doff + 128], in0=ps[db][:, doff:doff + 128],
            in1=ccausal[:, :], op=ALU.add), [smm[db]])
        prb_toks = []
        etoks = []
        for b in range(nb):
            k0, k1 = b * 512, min(kmax_col, (b + 1) * 512)
            dep_sc = [cm if b == db else smm[b], mxe] + list(self.tok_probs_free)
            at = self.act(probs[:, k0:k1], ps[b][:, 0:k1 - k0],
                          func=AF.Exp, bias=sb["MXcol"][:, 0:1],
                          scale=1.0 / (96 ** 0.5),
                          accum=sb["accD"][:, 8 + b:9 + b], deps=dep_sc)
            prb_toks.append(at)
            etoks.append(at)
        for b in range(nb):
            self.free_bank(b, prb_toks[-1])
        s1 = None
        if nb == 1:
            s1 = self.op("dve", lambda e: e.tensor_copy(
                out=sb["Scol"][:, 0:1], in_=sb["accD"][:, 8:9]), etoks)
        else:
            s1 = self.op("dve", lambda e: e.tensor_add(
                out=sb["Scol"][:, 0:1], in0=sb["accD"][:, 8:9],
                in1=sb["accD"][:, 9:10]), etoks)
            for b in range(2, nb):
                s1 = self.op("dve", lambda e, b=b: e.tensor_add(
                    out=sb["Scol"][:, 0:1], in0=sb["Scol"][:, 0:1],
                    in1=sb["accD"][:, 8 + b:9 + b]), [s1] + etoks)
        rs = self.op("dve", lambda e: e.reciprocal(
            out=sb["recipS"][:, 0:1], in_=sb["Scol"][:, 0:1]), [s1])
        dR = self.op("dve", lambda e: e.tensor_scalar(
            out=sb["diagR0"][:, :], in0=sb["cI128"][:, :],
            scalar1=sb["recipS"][:, 0:1], scalar2=None, op0=ALU.mult),
            [rs, self.tok_diagR_free[0]])
        nkc = qti + 1
        ev_pt = []
        for bb in range((nkc + 3) // 4):
            bank = 5 + (bb % 2)
            mmp = []
            jmax = min(4, nkc - bb * 4)
            for j in range(jmax):
                kc = bb * 4 + j
                mmp.append(self.mm(bank, ps[bank][:, j * 128:(j + 1) * 128],
                                   probs[:, kc * 128:(kc + 1) * 128],
                                   sb["diagR0"][:, :], True, True,
                                   deps=prb_toks + [dR]))
            epv = self.op("dve", lambda e, bb=bb, bank=bank, jmax=jmax:
                          e.tensor_copy(out=pTs[:, bb * 512:bb * 512 + jmax * 128],
                                        in_=ps[bank][:, 0:jmax * 128]),
                          mmp + [self.tok_pts_free])
            self.free_bank(bank, epv)
            ev_pt.append(epv)
        self.tok_diagR_free[0] = ev_pt[-1]
        self._av_par = getattr(self, "_av_par", 0) ^ 1
        avc = self._av_par * 128
        avdep = self.tok_av_free[self._av_par] or self.pslast[7]
        self.pslast[7] = None
        av = None
        for kc in range(nkc):
            av = self.mm(7, ps[7][0:64, avc:avc + 128],
                         Vf[:, kc * D + h * DH:kc * D + (h + 1) * DH],
                         pTs[:, kc * 128:(kc + 1) * 128],
                         kc == 0, kc == nkc - 1,
                         deps=[ev_pt[kc // 4], avdep if kc == 0 else None] + a_toks)
        ao = self.act(AO[par * 64:par * 64 + 64,
                               ch * NT + qt * 128:ch * NT + (qt + 1) * 128],
                      ps[7][0:64, avc:avc + 128], deps=[av])
        self.tok_av_free[self._av_par] = ao
        self.tok_pts_free = av
        self.tok_probs_free = [ev_pt[-1]]
        return ao

    def l1_phase_b(self, Kf, KR, QC, ccos, csin, a_toks):
        cfg = self.cfg
        self.sb_Kf = Kf
        self.sb_KR = KR
        wq_free = [None, None]
        wo_free = [None, None]
        cos_free = [None]
        for g in range(cfg.NTT):
            col = g * cfg.NT
            cg1 = self.dma("sp", "csb", lambda e, col=col: e.dma_start(
                out=ccos[:, 0:cfg.NT],
                in_=self.dram["cos_rep"][0:32, col:col + cfg.NT]), [cos_free[0]])
            cg2 = self.dma("sp", "csb", lambda e, col=col: e.dma_start(
                out=csin[:, 0:cfg.NT],
                in_=self.dram["sin_rep"][0:32, col:col + cfg.NT]), [cos_free[0]])
            a_toks = a_toks + [cg1, cg2]
            ao_toks = []
            for h in range(cfg.NH):
                qc_, er_, mxe = self.l1_attn_head(h, g, Kf, KR, QC, ccos, csin,
                                                  a_toks, wq_free)
                for qt in range(cfg.QG):
                    ao = self.l1_attn_qtile(h, g, qt, qc_, er_, mxe, a_toks)
                    ao_toks.append(ao)
                self.tok_qg_free = self.fw.last("pe")
                self.tok_qr_free = self.fw.last("pe")
            cos_free[0] = self.fw.last("dve")
            self.oproj_group("Wo1", g, ao_toks, bias_idx=None, wo_free=wo_free)

    def emit_l1(self, seq, deps):
        cfg = self.cfg
        with ExitStack() as c_l1:
            Kf = self.sbuf(c_l1, "Kf1", [128, cfg.DC * cfg.S], BF16)
            Vf = self.sbuf(c_l1, "Vf1", [128, (cfg.S // 128) * cfg.D], BF16)
            KR = self.sbuf(c_l1, "KR1",
                           [128, (cfg.NH * cfg.RC // 128) * cfg.S], BF16)
            QC = self.sbuf(c_l1, "QC1", [128, (cfg.Q_RANK // 128) * cfg.S], BF16)
            cl = []
            with ExitStack() as c_a:
                mrow = self.sbuf(c_a, "mrow1", [1, cfg.NT], F32)
                rrow = self.sbuf(c_a, "rrow1", [1, cfg.NT], F32)
                xng = self.sbuf(c_a, "xng1", [128, cfg.DC * cfg.NT], BF16)
                kvn = self.sbuf(c_a, "kvn", [128, 2 * cfg.NT], BF16)
                self.sbuf(c_a, "lBh", [128, cfg.D], BF16)
                self.sbuf(c_a, "lgr", [8, 2 * cfg.NT], F32)
                self.sbuf(c_a, "mid", [128, cfg.NT], BF16)
                self.sbuf(c_a, "Zt", [128, cfg.NT], BF16)
                self.sbuf(c_a, "gat", [8, cfg.NT], BF16)
                ccos = self.sbuf(c_a, "ccosA", [128, cfg.NT], BF16)
                csin = self.sbuf(c_a, "csinA", [128, cfg.NT], BF16)
                tlb = self.dma("sp", "icb", lambda e: e.dma_start(
                    out=self.sb["evac"][:, 0:cfg.D], in_=self.dram["lBpad"]),
                    deps + [self.tok_evac_free[0], self.tok_evac_free[1]])
                clb = self.act(self.sb["lBh"][:, :], self.sb["evac"][:, 0:cfg.D],
                               scale=2.0, deps=[tlb])
                mm1_ = self.op("dve", lambda e: e.memset(self.sb["mid"][:, :], 0.0),
                               deps)
                mm2_ = self.op("dve", lambda e: e.memset(self.sb["Zt"][:, :], 0.0),
                               deps)
                self.tok_evac_free[0] = clb
                deps = deps + [clb, mm1_, mm2_]
                stat_toks, a_toks = self.l1_phase_a(
                    mrow, rrow, xng, kvn, Kf, Vf, KR, QC, ccos, csin, deps + cl)
            self.sb_Vf = Vf
            with ExitStack() as c_b1:
                self.sbuf(c_b1, "probs", [128, cfg.S], BF16)
                self.sbuf(c_b1, "pTs", [128, cfg.S], BF16)
                self.sbuf(c_b1, "AO", [128, cfg.DC * cfg.NT], BF16)
                self.sbuf(c_b1, "wqs", [128, 4 * 96], F32)
                self.sbuf(c_b1, "wqh", [128, 4 * 96], BF16)
                self.sbuf(c_b1, "ropeq", [32, cfg.NT], BF16)
                self.sbuf(c_b1, "qaug", [96, cfg.NT], BF16)
                self.sbuf(c_b1, "Qg", [128, cfg.NT], BF16)
                self.sbuf(c_b1, "Qr", [128, cfg.NT], BF16)
                self.sbuf(c_b1, "cSELq", [32, 4 * 128], BF16)
                self.sbuf(c_b1, "ccausal", [128, 128], F32)
                ccosB = self.sbuf(c_b1, "ccosG", [32, cfg.NT], BF16)
                csinB = self.sbuf(c_b1, "csinG", [32, cfg.NT], BF16)
                bdep = [self.fw.last("act"), self.fw.last("dve")]
                l1 = self.dma("sp", "mk", lambda e, d=self.sb["cSELq"]:
                              e.dma_start(out=d[:, :], in_=self.dram["SELq"]), bdep)
                l2 = self.dma("sp", "mk", lambda e, d=self.sb["ccausal"]:
                              e.dma_start(out=d[:, :], in_=self.dram["causal"]), bdep)
                mg1_ = self.op("dve", lambda e, q_=self.sb["Qg"]:
                               e.memset(q_[:, :], 0.0), bdep)
                a_toks = a_toks + [l1, l2, mg1_]
                self.l1_phase_b(Kf, KR, QC, ccosB, csinB, a_toks + cl)
            return self.bar()


def Kf_ap(b, ch, k0, k1):
    return b.sb_Kf[:, ch * b.cfg.S + k0: ch * b.cfg.S + k1]


def KR_ap(b, rch, k0, k1):
    return b.sb_KR[:, rch * b.cfg.S + k0: rch * b.cfg.S + k1]


def host_inputs(cfg, x_core, w):
    import ml_dtypes
    c = host_constants(cfg)
    d0 = host_weights_l0(cfg, w)
    d1 = host_weights_l1(cfg, w)
    inp = {}
    inp["x"] = np.ascontiguousarray(x_core.transpose(0, 2, 1))
    bf = ml_dtypes.bfloat16
    for n in ("Mstep", "Mpre", "Ddiag", "ones_e", "ones_o", "ones_q",
              "ones_colh", "cos_rep", "sin_rep", "R32T", "R32Tp", "R128T", "SELq",
              "SELe", "SELr", "I128", "ones_rowh"):
        inp[n] = c[n].astype(bf)
    for n in ("PcT", "ones_row", "ones_col", "causal"):
        inp[n] = c[n].astype(np.float32)
    inp["tkTpad"] = d0["tkTpad"]
    inp["Wq"] = d0["Wq"]; inp["Wk"] = d0["Wk"]; inp["Wv"] = d0["Wv"]
    inp["Wo"] = d0["Wo"]
    for n in ("bq", "bk", "bv", "bo"):
        inp[n] = d0["b" + n[1:]]
    inp["bv_row"] = np.ascontiguousarray(w["L0_attn_bv"].reshape(1, cfg.D))
    inp["tv"] = d0["tv"]; inp["tvT"] = d0["tvT"]
    inp["wgT"] = d0["wgT"]; inp["wuT"] = d0["wuT"]; inp["wdT"] = d0["wdT"]
    inp["Wkv"] = d1["Wkv"]; inp["gkv"] = d1["gkv"]
    inp["Wkdec"] = d1["Wkdec"]; inp["Wvdec"] = d1["Wvdec"]
    inp["Wq1"] = d1["Wq"]; inp["Wkrope"] = d1["Wkrope"]
    inp["Wqaug"] = d1["Wqaug"]; inp["Wo1"] = d1["Wo"]
    inp["bwT"] = d1["bwT"]; inp["bb"] = d1["bb"]
    inp["lA"] = d1["lA"]; inp["lBpad"] = d1["lBpad"]
    inp["rwT"] = d1["rwT"]; inp["rb"] = d1["rb"]
    return inp


_NC_CACHE = {}


def _build_program():
    if "nc" not in _NC_CACHE:
        cfg = Cfg()
        b = B(cfg)
        nc = b.build(layers=(0, 1), nseq=cfg.B_local)
        _NC_CACHE["nc"] = nc
        _NC_CACHE["cfg"] = cfg
    return _NC_CACHE["nc"], _NC_CACHE["cfg"]


def kernel(**inputs):
    nc, cfg = _build_program()
    x = np.asarray(inputs["L0_attn_x"])
    ncore = 8
    shared = host_inputs(cfg, x[0:cfg.B_local], inputs)
    in_maps = []
    for c in range(ncore):
        m = dict(shared)
        m["x"] = np.ascontiguousarray(
            x[c * cfg.B_local:(c + 1) * cfg.B_local].transpose(0, 2, 1))
        in_maps.append(m)
    res = run_bass_kernel_spmd(nc, in_maps, core_ids=list(range(ncore)))
    outs = []
    for c in range(ncore):
        y = np.asarray(res.results[c]["y"])
        outs.append(np.ascontiguousarray(y.transpose(0, 2, 1)))
    return np.concatenate(outs, 0).astype(np.float32)
